# version 06
# speedup vs baseline: 1.6781x; 1.1508x over previous
import contextlib
import numpy as np
import concourse.bass as bass
import concourse.mybir as mybir
from concourse.bass_utils import run_bass_kernel_spmd

F32 = mybir.dt.float32
BF16 = mybir.dt.bfloat16
AF = mybir.ActivationFunctionType
ALU = mybir.AluOpType
AX = mybir.AxisListType

B, T, D, H, DH = 4, 4096, 1024, 16, 64
HALF = T // 2
NC = 8
CC = D // 128
NT = T // 128
NG = T // 512
DFF = 4096
NE = 8
MOE_SCALE = float(1.0 / np.sqrt(1 + 2.0 / 8.0))
NH = 2
HYT = HALF // NH
PAIRS = [(0, 1), (0, 2), (0, 3), (1, 2), (1, 3), (2, 3)]
JSIGN = [1.0, -1.0, 1.0, 1.0, -1.0, 1.0]
NEG = -1e9

ENGINES = ("sync", "gpsimd", "tensor", "vector", "scalar")


class Plan:
    def __init__(self, nc, ctx):
        self.nc, self.ctx = nc, ctx
        self.ops = {e: [] for e in ENGINES}
        self.sems = {}

    def sem(self, *names):
        for name in names:
            h = self.ctx.enter_context(self.nc.semaphore(name))
            self.sems[name] = [h, 0]

    def n(self, name):
        return self.sems[name][1]

    def wait(self, eng, *pairs):
        for (nm, v) in pairs:
            self.ops[eng].append(("w", self.sems[nm][0], v))

    def waitn(self, eng, *names):
        for nm in names:
            self.ops[eng].append(("w", self.sems[nm][0], self.sems[nm][1]))

    def op(self, eng, method, *args, inc=None, **kw):
        if eng == "vector" or eng == "scalar":
            assert inc is None or inc[0] in ("vS", "sS")
            nm = "vS" if eng == "vector" else "sS"
            rec = self.sems[nm]
            if rec[1] > 0:
                self.ops[eng].append(("w", rec[0], rec[1]))
            rec[1] += 1
            self.ops[eng].append(("o", method, args, kw, rec[0], 1))
            return
        if inc is not None:
            nm, k = inc
            rec = self.sems[nm]
            rec[1] += k
            self.ops[eng].append(("o", method, args, kw, rec[0], k))
        else:
            self.ops[eng].append(("o", method, args, kw, None, 0))

    def gop(self, method, *args, **kw):
        rec = self.sems["gS"]
        if rec[1] > 0:
            self.ops["gpsimd"].append(("w", rec[0], rec[1]))
        rec[1] += 1
        self.ops["gpsimd"].append(("o", method, args, kw, rec[0], 1))

    def dma(self, eng, out, in_, inc, **kw):
        self.op(eng, "dma_start", inc=inc, out=out, in_=in_, **kw)

    def build(self):
        blk = self.ctx.enter_context(self.nc.Block())
        for eng in ENGINES:
            lst = self.ops[eng]
            if not lst:
                continue

            def runner(e, lst=lst):
                for it in lst:
                    if it[0] == "w":
                        e.wait_ge(it[1], it[2])
                    else:
                        _, method, args, kw, h, k = it
                        ins = getattr(e, method)(*args, **kw)
                        if h is not None:
                            ins.then_inc(h, k)
            getattr(blk, eng)(runner)


def din(nc, name, shape, dtype=F32):
    return nc.dram_tensor(name, list(shape), dtype, kind="ExternalInput").ap()


def dout(nc, name, shape, dtype=F32):
    return nc.dram_tensor(name, list(shape), dtype, kind="ExternalOutput").ap()


def dscr(nc, name, shape, dtype=BF16):
    return nc.dram_tensor(name, list(shape), dtype, kind="Internal").ap()


_NAMEC = [0]


def sb(ctx, nc, name, shape, dtype):
    _NAMEC[0] += 1
    return ctx.enter_context(nc.sbuf_tensor(f"{name}_{_NAMEC[0]}", list(shape), dtype)).ap()


def psum(ctx, nc, name, shape, dtype=F32):
    _NAMEC[0] += 1
    return ctx.enter_context(nc.psum_tensor(f"{name}_{_NAMEC[0]}", list(shape), dtype)).ap()


def bounce_cast(p, scr, src, buf, ld, st, ncc=CC, tcols=T, pad=False):
    for cc in range(ncc):
        self_ld0 = p.n(ld)
        p.wait("gpsimd", (st, max(0, p.n(st) - 16)))
        p.dma("gpsimd", buf, src[cc * 128:(cc + 1) * 128, :], (ld, 16))
        p.wait("gpsimd", (ld, p.n(ld)))
        if pad:
            p.dma("gpsimd", scr[cc, :, 1:tcols + 1], buf, (st, 16))
        else:
            p.dma("gpsimd", scr[cc, :, :], buf, (st, 16))
        del self_ld0


def emit_ln(p, s, out, bufs, nblk=CC, inv_n=1.0 / D):
    psS0, psS1 = bufs["psS"]
    psB0, psB1 = bufs["psB"]
    sq0, sq1 = bufs["sq"]
    r_m, r_v, r_sd, r_i = bufs["rows"]
    mm0 = p.n("mm")
    for dt in range(nblk):
        p.op("tensor", "matmul", psS0, bufs["ones_col_f"], s[:, dt, :],
             start=(dt == 0), stop=(dt == nblk - 1))
        if dt >= 2:
            p.wait("scalar", ("mm", mm0 + dt - 1))
        p.op("scalar", "square", sq0 if dt % 2 == 0 else sq1, s[:, dt, :])
        p.wait("tensor", ("sS", p.n("sS")))
        p.op("tensor", "matmul", psS1, bufs["ones_col_f"], sq0 if dt % 2 == 0 else sq1,
             start=(dt == 0), stop=(dt == nblk - 1), inc=("mm", 1))
    p.wait("vector", ("mm", p.n("mm")))
    p.op("vector", "tensor_scalar", r_m, psS0, inv_n, None, ALU.mult)
    p.op("vector", "tensor_scalar", r_v, psS1, inv_n, None, ALU.mult)
    p.op("vector", "tensor_tensor", r_sd, r_m, r_m, ALU.mult)
    p.op("vector", "tensor_tensor", r_v, r_v, r_sd, ALU.subtract)
    p.op("vector", "tensor_scalar", r_v, r_v, 0.0, 1e-5, ALU.max, ALU.add)
    p.wait("scalar", ("vS", p.n("vS")))
    p.op("scalar", "activation", r_sd, r_v, AF.Sqrt)
    p.wait("vector", ("sS", p.n("sS")))
    p.op("vector", "reciprocal", r_i, r_sd)
    p.wait("tensor", ("vS", p.n("vS")))
    p.op("tensor", "matmul", psB0, bufs["ones_row_f"], r_m, start=True, stop=True)
    p.op("tensor", "matmul", psB1, bufs["ones_row_f"], r_i, start=True, stop=True, inc=("mm", 1))
    p.wait("vector", ("mm", p.n("mm")))
    for dt in range(nblk):
        p.op("vector", "tensor_tensor", sq0 if dt % 2 == 0 else sq1, s[:, dt, :], psB0, ALU.subtract)
        p.op("vector", "tensor_tensor", out[:, dt, :], sq0 if dt % 2 == 0 else sq1, psB1, ALU.mult)


BARRIER_SEMS = ["ld0", "ga", "gw", "gw2", "sa", "sb", "sr", "sxa", "sxb", "stB", "stC", "stK", "stV", "stV2", "ka", "kb", "kc2", "kd", "va", "vb", "vc", "vd", "so", "mm", "vS", "sS", "gS", "g1a", "g1b", "g2a", "g2b", "sxw"]


def barrier(p):
    vals = [(nm, p.n(nm)) for nm in BARRIER_SEMS]
    for eng in ENGINES:
        for nm, v in vals:
            if v > 0:
                p.ops[eng].append(("w", p.sems[nm][0], v))


def emit_phase2(p, nc, x1, wr, wk, wv, mur, muk, bval, ocf, orf, x2, x1b, xr_s, xk_s, r_s, ka_s, fused=False):
    ctx = contextlib.ExitStack()
    with ctx:
        cbuf = sb(ctx, nc, "cbuf", [128, T], BF16)
        zcol = sb(ctx, nc, "zcol", [128, 1], BF16)
        w_a = sb(ctx, nc, "w_a", [128, CC, 1024], BF16)
        w_b = sb(ctx, nc, "w_b", [128, CC, 1024], BF16)
        murb = sb(ctx, nc, "murb", [128, CC], F32)
        mukb = sb(ctx, nc, "mukb", [128, CC], F32)
        bvb = sb(ctx, nc, "bvb", [128, 1], F32)
        ocfb = sb(ctx, nc, "ocfb", [128, 1], F32)
        orfb = sb(ctx, nc, "orfb", [1, 128], F32)
        xw = sb(ctx, nc, "xw", [128, CC, 513], BF16)
        xw2 = sb(ctx, nc, "xw2", [128, CC, 513], BF16)
        dbuf = sb(ctx, nc, "dbuf", [128, 512], F32)
        ebuf = sb(ctx, nc, "ebuf", [128, 512], BF16)
        e2 = sb(ctx, nc, "e2", [128, 512], BF16)
        tcol = sb(ctx, nc, "tcol", [128, CC], F32)
        tcol2 = sb(ctx, nc, "tcol2", [128, CC], F32)
        relu_t = sb(ctx, nc, "relu_t", [128, 512], F32)
        rwin = sb(ctx, nc, "rwin", [128, CC, 512], BF16)
        s_f = sb(ctx, nc, "s_f", [128, CC, 512], F32)
        xres = sb(ctx, nc, "xres", [128, 512], F32)
        xres2 = sb(ctx, nc, "xres2", [128, 512], F32)
        sq = sb(ctx, nc, "sq", [128, 512], F32)
        sq2 = sb(ctx, nc, "sq2", [128, 512], F32)
        rows = [sb(ctx, nc, f"row{i}", [1, 512], F32) for i in range(4)]
        ps_a = psum(ctx, nc, "ps_a", [128, 512])
        ps_b = psum(ctx, nc, "ps_b", [128, 512])
        psS0 = psum(ctx, nc, "psS0", [1, 512])
        psS1 = psum(ctx, nc, "psS1", [1, 512])
        psB0 = psum(ctx, nc, "psB0", [128, 512])
        psB1 = psum(ctx, nc, "psB1", [128, 512])
        lnb = dict(psS=(psS0, psS1), psB=(psB0, psB1), sq=(sq, sq2),
                   rows=rows, ones_col_f=ocfb, ones_row_f=orfb)

        for dst, src in ((murb, mur), (mukb, muk), (bvb, bval), (ocfb, ocf), (orfb, orf)):
            p.dma("sync", dst, src, ("ld0", 16))
        p.gop("memset", zcol, 0.0)
        for cc in range(CC):
            p.dma("gpsimd", cbuf, x1[cc * 128:(cc + 1) * 128, :], ("ga", 16))
            p.wait("gpsimd", ("ga", p.n("ga")))
            p.dma("gpsimd", x1b[cc, :, 1:T + 1], cbuf, ("ga", 16))
            p.dma("gpsimd", x1b[cc, :, 0:1], zcol, ("ga", 16), allow_slow_non_contiguous=True)
            p.wait("gpsimd", ("ga", p.n("ga")))
        gaA = p.n("ga")
        for cc in range(CC):
            p.dma("gpsimd", w_a[:, cc, :], wr[cc * 128:(cc + 1) * 128, :], ("gw", 16))
            p.dma("gpsimd", w_b[:, cc, :], wk[cc * 128:(cc + 1) * 128, :], ("gw", 16))
        gwT = p.n("gw")

        p.wait("sync", ("ga", gaA), ("ld0", p.n("ld0")))
        vs_end = {}
        for tc in range(NG):
            bufa = xw if tc % 2 == 0 else xw2
            snm = "sa" if tc % 2 == 0 else "sb"
            if tc >= 2:
                p.wait("sync", ("vS", vs_end[tc - 2]))
            for cc in range(CC):
                p.dma("sync", bufa[:, cc, :], x1b[cc, :, tc * 512:tc * 512 + 513], (snm, 16))
            p.wait("vector", (snm, p.n(snm)))
            for cc in range(CC):
                cur = bufa[:, cc, 1:513]
                prv = bufa[:, cc, 0:512]
                p.wait("vector", ("stB", p.n("stB")))
                p.op("vector", "tensor_tensor", dbuf, cur, prv, ALU.subtract)
                if tc == 4:
                    p.op("vector", "tensor_scalar", tcol[:, cc:cc + 1], bufa[:, cc, 0:1], bvb, None, ALU.mult)
                    p.op("vector", "tensor_tensor", dbuf[:, 0:1], bufa[:, cc, 1:2], tcol[:, cc:cc + 1], ALU.subtract)
                p.op("vector", "scalar_tensor_tensor", ebuf, dbuf, murb[:, cc:cc + 1], prv, ALU.mult, ALU.add)
                if tc == 4:
                    p.op("vector", "tensor_tensor", tcol2[:, cc:cc + 1], dbuf[:, 0:1], murb[:, cc:cc + 1], ALU.mult)
                    p.op("vector", "tensor_tensor", ebuf[:, 0:1], tcol2[:, cc:cc + 1], tcol[:, cc:cc + 1], ALU.add)
                p.op("vector", "scalar_tensor_tensor", e2, dbuf, mukb[:, cc:cc + 1], prv, ALU.mult, ALU.add)
                if tc == 4:
                    p.op("vector", "tensor_tensor", tcol2[:, cc:cc + 1], dbuf[:, 0:1], mukb[:, cc:cc + 1], ALU.mult)
                    p.op("vector", "tensor_tensor", e2[:, 0:1], tcol2[:, cc:cc + 1], tcol[:, cc:cc + 1], ALU.add)
                p.wait("gpsimd", ("vS", p.n("vS")))
                p.dma("gpsimd", xr_s[cc, :, tc * 512:(tc + 1) * 512], ebuf, ("stB", 16))
                p.dma("gpsimd", xk_s[cc, :, tc * 512:(tc + 1) * 512], e2, ("stB", 16))
            vs_end[tc] = p.n("vS")
        stB = p.n("stB")

        p.wait("tensor", ("gw", gwT))
        p.wait("sync", ("stB", stB))
        ss_end = {}
        for tc in range(NG):
            bufa = xw if tc % 2 == 0 else xw2
            snm = "sa" if tc % 2 == 0 else "sb"
            if tc >= 2:
                p.wait("sync", ("mm", tc_mmend[tc - 2] if False else p.n("mm")))
            for cc in range(CC):
                p.dma("sync", bufa[:, cc, 0:512], xr_s[cc, :, tc * 512:(tc + 1) * 512], (snm, 16))
            ld_r = p.n(snm)
            if tc >= 1:
                p.wait("sync", ("mm", p.n("mm")))
            for cc in range(CC):
                p.dma("sync", rwin[:, cc, :], xk_s[cc, :, tc * 512:(tc + 1) * 512], ("sr", 16))
            ld_k = p.n("sr")
            ss0 = p.n("sS")
            p.wait("tensor", (snm, ld_r), ("vS", p.n("vS")), ("sS", ss0))
            for dt in range(CC):
                pa = ps_a if dt % 2 == 0 else ps_b
                if dt >= 2:
                    p.wait("tensor", ("sS", ss0 + dt - 1))
                for cc in range(CC):
                    p.op("tensor", "matmul", pa, w_a[:, cc, dt * 128:(dt + 1) * 128],
                         bufa[:, cc, 0:512], start=(cc == 0), stop=(cc == CC - 1),
                         inc=("mm", 1) if cc == CC - 1 else None)
                p.wait("scalar", ("mm", p.n("mm")), ("stC", p.n("stC")))
                p.op("scalar", "activation", ebuf, pa, AF.Sigmoid)
                p.wait("gpsimd", ("sS", p.n("sS")))
                p.dma("gpsimd", r_s[dt, :, tc * 512:(tc + 1) * 512], ebuf, ("stC", 16))
            vv0 = p.n("vS")
            p.wait("tensor", ("sr", ld_k), ("sS", p.n("sS")))
            for dt in range(CC):
                pa = ps_a if dt % 2 == 0 else ps_b
                if dt >= 2:
                    p.wait("tensor", ("vS", vv0 + dt - 1))
                for cc in range(CC):
                    p.op("tensor", "matmul", pa, w_b[:, cc, dt * 128:(dt + 1) * 128],
                         rwin[:, cc, :], start=(cc == 0), stop=(cc == CC - 1),
                         inc=("mm", 1) if cc == CC - 1 else None)
                if dt >= 1:
                    p.wait("vector", ("sS", p.n("sS")))
                p.wait("vector", ("mm", p.n("mm")))
                p.op("vector", "tensor_scalar", relu_t, pa, 0.0, None, ALU.max)
                p.wait("scalar", ("vS", p.n("vS")), ("stC", p.n("stC")))
                p.op("scalar", "activation", e2, relu_t, AF.Square)
                p.wait("gpsimd", ("sS", p.n("sS")))
                p.dma("gpsimd", ka_s[dt, :, tc * 512:(tc + 1) * 512], e2, ("stC", 16))
            ss_end[tc] = p.n("sS")
        stC = p.n("stC")
        p.waitn("gpsimd", "mm")
        for cc in range(CC):
            p.dma("gpsimd", w_a[:, cc, :], wv[cc * 128:(cc + 1) * 128, :], ("gw2", 16))
        wv_ld = p.n("gw2")

        p.wait("sync", ("stC", stC))
        p.wait("tensor", ("gw2", wv_ld))
        vs_d = {}
        for tc in range(NG):
            bufa = xw if tc % 2 == 0 else xw2
            snm = "sa" if tc % 2 == 0 else "sb"
            if tc >= 2:
                p.wait("sync", ("vS", vs_d[tc - 2]))
            for cc in range(CC):
                p.dma("sync", bufa[:, cc, 0:512], ka_s[cc, :, tc * 512:(tc + 1) * 512], (snm, 16))
            ld_ka = p.n(snm)
            if tc >= 1:
                p.wait("sync", ("vS", vs_d[tc - 1]))
            for cc in range(CC):
                p.dma("sync", rwin[:, cc, :], r_s[cc, :, tc * 512:(tc + 1) * 512], ("sr", 16))
            ld_rr = p.n("sr")
            vvD = p.n("vS")
            p.wait("tensor", (snm, ld_ka), ("vS", vvD))
            for dt in range(CC):
                pa = ps_a if dt % 2 == 0 else ps_b
                if dt >= 2:
                    p.wait("tensor", ("vS", vvD + 2 * (dt - 1)))
                for cc in range(CC):
                    p.op("tensor", "matmul", pa, w_a[:, cc, dt * 128:(dt + 1) * 128],
                         bufa[:, cc, 0:512], start=(cc == 0), stop=(cc == CC - 1),
                         inc=("mm", 1) if cc == CC - 1 else None)
                if dt >= 2:
                    p.wait("sync", ("vS", vvD + 2 * (dt - 1)))
                p.dma("sync", xres if dt % 2 == 0 else xres2,
                      x1[dt * 128:(dt + 1) * 128, tc * 512:(tc + 1) * 512],
                      ("sxa" if dt % 2 == 0 else "sxb", 16))
                p.wait("vector", ("mm", p.n("mm")), ("sr", ld_rr),
                       ("sxa" if dt % 2 == 0 else "sxb", p.n("sxa" if dt % 2 == 0 else "sxb")))
                p.op("vector", "tensor_tensor", dbuf, pa, rwin[:, dt, :], ALU.mult)
                p.op("vector", "tensor_tensor", s_f[:, dt, :], dbuf,
                     xres if dt % 2 == 0 else xres2, ALU.add)
            p.wait("tensor", ("vS", p.n("vS")))
            p.wait("scalar", ("vS", p.n("vS")))
            emit_ln(p, s_f, s_f, lnb)
            if tc >= 4 or fused:
                oc0 = tc * 512 if fused else (tc - 4) * 512
                p.wait("gpsimd", ("vS", p.n("vS")))
                for dt in range(CC):
                    p.dma("gpsimd", x2[dt * 128:(dt + 1) * 128, oc0:oc0 + 512],
                          s_f[:, dt, :], ("so", 16))
                p.wait("vector", ("so", p.n("so")))
            vs_d[tc] = p.n("vS")
        p.wait("gpsimd", ("so", p.n("so")))


def build_phase2():
    nc = bass.Bass("TRN2", target_bir_lowering=False)
    ctx = contextlib.ExitStack()
    x1 = din(nc, "x1", [D, T])
    wr = din(nc, "wr", [D, D])
    wk = din(nc, "wk", [D, D])
    wv = din(nc, "wv", [D, D])
    mur = din(nc, "mur", [128, CC])
    muk = din(nc, "muk", [128, CC])
    bval = din(nc, "bval", [128, 1])
    ocf = din(nc, "ocf", [128, 1])
    orf = din(nc, "orf", [1, 128])
    x2 = dout(nc, "x2", [D, HALF])

    x1b = dscr(nc, "x1b", [CC, 128, T + 1])
    xr_s = dscr(nc, "xr_s", [CC, 128, T])
    xk_s = dscr(nc, "xk_s", [CC, 128, T])
    r_s = dscr(nc, "r_s", [CC, 128, T])
    ka_s = dscr(nc, "ka_s", [CC, 128, T])

    with ctx:
        p = Plan(nc, ctx)
        p.sem("ld0", "ga", "gw", "gw2", "sa", "sb", "sr", "sxa", "sxb", "stB", "stC", "stK", "stV", "stV2", "ka", "kb", "kc2", "kd", "va", "vb", "vc", "vd", "so", "mm", "vS", "sS", "gS", "g1a", "g1b", "g2a", "g2b", "sxw")
        emit_phase2(p, nc, x1, wr, wk, wv, mur, muk, bval, ocf, orf, x2, x1b, xr_s, xk_s, r_s, ka_s)
        p.build()
    return nc


def emit_phase3(p, nc, x2, wqm, wkm, wvm, wom, bqc, bkc, bvr, boc, sinq, cosq, sink, cosk, pbias, zbias, trim, ocf, orf, x3, x2b, k_scr, v_scr, fused=False):
    ctx = contextlib.ExitStack()
    with ctx:
        cbuf = sb(ctx, nc, "cbuf", [128, T], BF16)
        wqb = sb(ctx, nc, "wqb", [128, CC, D], BF16)
        wkb = sb(ctx, nc, "wkb", [128, CC, D], BF16)
        wvb = sb(ctx, nc, "wvb", [128, CC, D], BF16)
        wob = sb(ctx, nc, "wob", [128, CC, D], BF16)
        bqb = sb(ctx, nc, "bqb", [128, CC], F32)
        bkb = sb(ctx, nc, "bkb", [128, CC], F32)
        bvb = sb(ctx, nc, "bvb", [1, D], BF16)
        bob = sb(ctx, nc, "bob", [128, CC], F32)
        sqb = sb(ctx, nc, "sqb", [32, HALF], BF16)
        cqb = sb(ctx, nc, "cqb", [32, HALF], BF16)
        skb = sb(ctx, nc, "skb", [32, T], BF16)
        ckb = sb(ctx, nc, "ckb", [32, T], BF16)
        pbb = sb(ctx, nc, "pbb", [128, 1], F32)
        zbb = sb(ctx, nc, "zbb", [128, 1], F32)
        trib = sb(ctx, nc, "trib", [128, 128], BF16)
        orb = sb(ctx, nc, "orb", [1, 128], BF16)
        o64 = sb(ctx, nc, "o64", [1, 64], F32)
        ocfb = sb(ctx, nc, "ocfb", [128, 1], F32)
        orfb = sb(ctx, nc, "orfb", [1, 128], F32)
        xwk = sb(ctx, nc, "xwk", [128, CC, 512], BF16)
        xwk2 = sb(ctx, nc, "xwk2", [128, CC, 512], BF16)
        kraw = sb(ctx, nc, "kraw", [64, 2, 512], BF16)
        krot = sb(ctx, nc, "krot", [64, 2, 512], BF16)
        visb = sb(ctx, nc, "visb", [128, H * 65], BF16)
        visb2 = sb(ctx, nc, "visb2", [128, H * 65], BF16)
        qbuf = sb(ctx, nc, "qbuf", [64, H, 512], BF16)
        kchs = [sb(ctx, nc, f"kch{i}", [64, 4, 128], BF16) for i in range(4)]
        vchs = [sb(ctx, nc, f"vch{i}", [128, 4 * 65], BF16) for i in range(4)]
        prbufs = [sb(ctx, nc, f"prbx{i}", [128, 512], BF16) for i in range(8)]
        rcrow = sb(ctx, nc, "rcrow", [1, 512], F32)
        rcs = sb(ctx, nc, "rcs", [64, 512], BF16)
        comb = sb(ctx, nc, "comb", [128, CC, 512], BF16)
        s_f = sb(ctx, nc, "s_f", [128, CC, 512], F32)
        xres = sb(ctx, nc, "xres", [128, 512], F32)
        xres2 = sb(ctx, nc, "xres2", [128, 512], F32)
        sq = sb(ctx, nc, "sq", [128, 512], F32)
        sq2 = sb(ctx, nc, "sq2", [128, 512], F32)
        rows = [sb(ctx, nc, f"row{i}", [1, 512], F32) for i in range(4)]
        pA = [psum(ctx, nc, f"pA{i}", [128, 512]) for i in range(2)]
        pS = [psum(ctx, nc, f"pS{i}", [128, 512]) for i in range(2)]
        soP = [psum(ctx, nc, f"soP{i}", [65, 512]) for i in range(4)]
        lnb = dict(psS=(soP[0][0:1, :], soP[1][0:1, :]), psB=(pA[0], pA[1]),
                   sq=(sq, sq2), rows=rows, ones_col_f=ocfb, ones_row_f=orfb)

        for dst, src in ((bqb, bqc), (bkb, bkc), (bob, boc), (pbb, pbias), (zbb, zbias),
                         (ocfb, ocf), (orfb, orf)):
            p.dma("sync", dst, src, ("ld0", 16))
        for dst, src in ((bvb, bvr), (sqb, sinq), (cqb, cosq), (skb, sink), (ckb, cosk),
                         (trib, trim), (orb, orf)):
            p.dma("gpsimd", dst, src, ("ga", 16))
        p.gop("memset", o64, 1.0)
        p.op("vector", "memset", visb.rearrange("q (h m) -> q h m", m=65)[:, :, 64:65], 1.0)
        p.op("vector", "memset", visb2.rearrange("q (h m) -> q h m", m=65)[:, :, 64:65], 1.0)
        for cc in range(CC):
            p.dma("gpsimd", cbuf, x2[cc * 128:(cc + 1) * 128, :], ("ga", 16))
            p.wait("gpsimd", ("ga", p.n("ga")))
            p.dma("gpsimd", x2b[cc, :, :], cbuf, ("ga", 16))
            p.wait("gpsimd", ("ga", p.n("ga")))
        gaA = p.n("ga")
        for cc in range(CC):
            p.dma("gpsimd", wqb[:, cc, :], wqm[cc * 128:(cc + 1) * 128, :], ("gw", 16))
            p.dma("gpsimd", wkb[:, cc, :], wkm[cc * 128:(cc + 1) * 128, :], ("gw", 16))
            p.dma("gpsimd", wvb[:, cc, :], wvm[cc * 128:(cc + 1) * 128, :], ("gw", 16))
            p.dma("gpsimd", wob[:, cc, :], wom[cc * 128:(cc + 1) * 128, :], ("gw", 16))
        gwT = p.n("gw")

        def rope(dst, srcb, sint, cost, col0, w):
            for hh in range(2):
                d_ = krot if dst is None else dst
                k1 = srcb[0:32, hh, :]
                ss = sint[:, col0:col0 + w]
                cs = cost[:, col0:col0 + w]
                ta = sq[0:32, 0:w]
                tb = sq2[0:32, 0:w]
                t0 = rcs[0:32, 0:w]
                p.op("vector", "tensor_copy", t0, srcb[32:64, hh, :])
                p.op("vector", "tensor_tensor", ta, cs, k1, ALU.mult)
                p.op("vector", "tensor_tensor", tb, ss, t0, ALU.mult)
                p.op("vector", "tensor_tensor", d_[0:32, hh, :], ta, tb, ALU.subtract)
                p.op("vector", "tensor_tensor", ta, ss, k1, ALU.mult)
                p.op("vector", "tensor_tensor", tb, cs, t0, ALU.mult)
                p.op("vector", "tensor_tensor", ta, ta, tb, ALU.add)
                p.op("vector", "tensor_copy", d_[32:64, hh, :], ta)

        p.wait("sync", ("ga", gaA), ("ld0", p.n("ld0")))
        p.wait("tensor", ("gw", gwT), ("ga", gaA))
        vs_s1 = {}
        for tcx in range(CC):
            bufa = xwk if tcx % 2 == 0 else xwk2
            snm = "sa" if tcx % 2 == 0 else "sb"
            if tcx >= 2:
                p.wait("sync", ("mm", vs_s1[tcx - 2]))
            for cc in range(CC):
                p.dma("sync", bufa[:, cc, :], x2b[cc, :, tcx * 512:(tcx + 1) * 512], (snm, 16))
            p.wait("tensor", (snm, p.n(snm)))
            for dt in range(CC):
                pa = pA[dt % 2]
                p.wait("tensor", ("sS", p.n("sS")))
                for cc in range(CC):
                    p.op("tensor", "matmul", pa, wkb[:, cc, dt * 128:(dt + 1) * 128],
                         bufa[:, cc, :], start=(cc == 0), stop=(cc == CC - 1),
                         inc=("mm", 1) if cc == CC - 1 else None)
                p.wait("scalar", ("mm", p.n("mm")), ("vS", p.n("vS")))
                p.op("scalar", "activation", kraw[0:64, 0, :], pa[0:64, :], AF.Identity,
                     bias=bkb[0:64, dt:dt + 1])
                p.op("scalar", "activation", kraw[0:64, 1, :], pa[64:128, :], AF.Identity,
                     bias=bkb[64:128, dt:dt + 1])
                p.wait("vector", ("sS", p.n("sS")), ("stK", p.n("stK")))
                rope(krot, kraw, skb, ckb, tcx * 512, 512)
                p.wait("gpsimd", ("vS", p.n("vS")))
                for kj in range(4):
                    p.dma("gpsimd",
                          k_scr.rearrange("a q (h n) -> a q h n", n=128)[tcx * 4 + kj, 0:64, 2 * dt:2 * dt + 2, :],
                          krot[:, :, kj * 128:(kj + 1) * 128], ("stK", 16))
            for tt4 in range(4):
                vb_ = visb if tt4 % 2 == 0 else visb2
                vsem = "stV" if tt4 % 2 == 0 else "stV2"
                p.wait("scalar", (vsem, p.n(vsem)))
                for dh2 in range(2):
                    pa = pA[dh2]
                    p.wait("tensor", ("sS", p.n("sS")))
                    for cc in range(CC):
                        p.op("tensor", "matmul", pa, bufa[:, cc, tt4 * 128:(tt4 + 1) * 128],
                             wvb[:, cc, dh2 * 512:(dh2 + 1) * 512],
                             start=(cc == 0), stop=False)
                    p.op("tensor", "matmul", pa, orb, bvb[:, dh2 * 512:(dh2 + 1) * 512],
                         start=False, stop=True, inc=("mm", 1))
                    p.wait("scalar", ("mm", p.n("mm")))
                    vtgt = vb_.rearrange("q (h m) -> q h m", m=65)
                    p.op("scalar", "copy",
                         vtgt[:, dh2 * 8:(dh2 + 1) * 8, 0:64],
                         pa.rearrange("q (h m) -> q h m", m=64))
                p.wait("gpsimd", ("sS", p.n("sS")))
                p.dma("gpsimd", v_scr[tcx * 4 + tt4], vb_, (vsem, 16))
            vs_s1[tcx] = p.n("mm")
        stK_T, stV_T, stV2_T = p.n("stK"), p.n("stV"), p.n("stV2")

        p.wait("sync", ("stK", stK_T), ("stV", stV_T), ("stV2", stV2_T))
        for g in range(4, 8):
            Kg = 4 * g + 4
            snm = "sa" if g % 2 == 0 else "sb"
            bufa = xwk if g % 2 == 0 else xwk2
            if g == 4:
                p.wait("sync", ("mm", p.n("mm")))
            else:
                p.wait("sync", ("mm", mm_gend))
            for cc in range(CC):
                p.dma("sync", bufa[:, cc, :], x2b[cc, :, g * 512:(g + 1) * 512], (snm, 16))
            p.wait("tensor", (snm, p.n(snm)), ("vS", p.n("vS")), ("sS", p.n("sS")))
            for dt in range(CC):
                pa = pA[dt % 2]
                if dt >= 2:
                    p.wait("tensor", ("vS", p.n("vS")))
                for cc in range(CC):
                    p.op("tensor", "matmul", pa, wqb[:, cc, dt * 128:(dt + 1) * 128],
                         bufa[:, cc, :], start=(cc == 0), stop=(cc == CC - 1),
                         inc=("mm", 1) if cc == CC - 1 else None)
                p.wait("scalar", ("mm", p.n("mm")), ("vS", p.n("vS")))
                p.op("scalar", "activation", kraw[0:64, 0, :], pa[0:64, :], AF.Identity,
                     bias=bqb[0:64, dt:dt + 1])
                p.op("scalar", "activation", kraw[0:64, 1, :], pa[64:128, :], AF.Identity,
                     bias=bqb[64:128, dt:dt + 1])
                p.wait("vector", ("sS", p.n("sS")))
                rope(qbuf.rearrange("q (a h) w -> q a h w", a=1)[:, 0, 2 * dt:2 * dt + 2, :],
                     kraw, sqb, cqb, (g - 4) * 512, 512)
            qdone = p.n("vS")
            scfree = [None, None, None, None]
            prfree = {(i, j): None for i in range(4) for j in range(2)}
            pready = {}
            av_mm = {}
            for hb in range(4):
                for kc in range(Kg):
                    knm = ("ka", "kb", "kc2", "kd")[kc % 4]
                    vnm = ("va", "vb", "vc", "vd")[kc % 4]
                    kb_ = kchs[kc % 4]
                    vb_ = vchs[kc % 4]
                    if kc >= 4:
                        p.wait("sync", ("mm", av_mm[kc - 4]))
                    elif hb > 0 or g > 4:
                        p.wait("sync", ("mm", mm_kprev))
                    p.dma("sync", kb_.rearrange("q a n -> q (a n)"),
                          k_scr[kc, 0:64, hb * 512:(hb + 1) * 512], (knm, 16))
                    p.dma("sync", vb_, v_scr[kc, :, hb * 260:(hb + 1) * 260], (vnm, 16))
                    p.wait("tensor", (knm, p.n(knm)), (vnm, p.n(vnm)), ("vS", qdone))
                    dk0 = Kg - 4
                    scb = [pS[0], pS[1], pA[0], pA[1]]
                    for hin in range(4):
                        h = hb * 4 + hin
                        if scfree[hin] is not None:
                            p.wait("tensor", scfree[hin])
                        p.op("tensor", "matmul", scb[hin], kb_[0:64, hin, :], qbuf[0:64, h, :],
                             start=True, stop=True, inc=("mm", 1))
                        mmsc = p.n("mm")
                        if prfree[(hin, kc % 2)] is not None:
                            p.wait("scalar", prfree[(hin, kc % 2)])
                        p.wait("scalar", ("mm", mmsc))
                        prbh = prbufs[hin * 2 + kc % 2]
                        p.op("scalar", "activation", prbh, scb[hin], AF.Exp, scale=0.125,
                             bias=pbb if kc < 16 else zbb)
                        scfree[hin] = ("sS", p.n("sS"))
                        pready[(hin, kc % 2)] = ("sS", p.n("sS"))
                        if kc >= dk0:
                            jd = kc - dk0
                            p.wait("vector", ("sS", p.n("sS")))
                            if jd > 0:
                                p.op("vector", "memset", prbh[:, 0:jd * 128], 0.0)
                            p.op("vector", "tensor_tensor", prbh[:, jd * 128:(jd + 1) * 128],
                                 prbh[:, jd * 128:(jd + 1) * 128], trib, ALU.mult)
                            pready[(hin, kc % 2)] = ("vS", p.n("vS"))
                    if kc > 0:
                        vprev = vchs[(kc - 1) % 4]
                        for hin in range(4):
                            p.wait("tensor", pready[(hin, (kc - 1) % 2)])
                            p.op("tensor", "matmul", soP[hin],
                                 vprev[:, hin * 65:(hin + 1) * 65],
                                 prbufs[hin * 2 + (kc - 1) % 2],
                                 start=(kc - 1 == 0), stop=False, inc=("mm", 1))
                            prfree[(hin, (kc - 1) % 2)] = ("mm", p.n("mm"))
                        av_mm[kc - 1] = p.n("mm")
                    if kc == Kg - 1:
                        vlast = vchs[kc % 4]
                        for hin in range(4):
                            p.wait("tensor", pready[(hin, kc % 2)])
                            p.op("tensor", "matmul", soP[hin],
                                 vlast[:, hin * 65:(hin + 1) * 65],
                                 prbufs[hin * 2 + kc % 2],
                                 start=(kc == 0), stop=True, inc=("mm", 1))
                            prfree[(hin, kc % 2)] = ("mm", p.n("mm"))
                        av_mm[kc] = p.n("mm")
                mm_kprev = p.n("mm")
                p.wait("vector", ("mm", p.n("mm")))
                for hin in range(4):
                    h = hb * 4 + hin
                    p.op("vector", "reciprocal", rcrow, soP[hin][64:65, :])
                    p.wait("tensor", ("vS", p.n("vS")))
                    if scfree[0] is not None:
                        p.wait("tensor", scfree[0])
                    p.op("tensor", "matmul", pS[0][0:64, :], o64, rcrow, start=True, stop=True,
                         inc=("mm", 1))
                    p.wait("scalar", ("mm", p.n("mm")))
                    p.op("scalar", "copy", rcs, pS[0][0:64, :])
                    scfree[0] = ("sS", p.n("sS"))
                    p.wait("vector", ("sS", p.n("sS")))
                    p.op("vector", "tensor_tensor",
                         comb[(h % 2) * 64:(h % 2) * 64 + 64, h // 2, :],
                         soP[hin][0:64, :], rcs, ALU.mult)
                vs_hbend = p.n("vS")
                p.wait("tensor", ("vS", vs_hbend))
            vs_pre = p.n("vS")
            for dt2 in range(CC):
                pa = pA[dt2 % 2]
                if dt2 >= 2:
                    p.wait("tensor", ("vS", vs_pre + dt2 - 1))
                for cc in range(CC):
                    p.op("tensor", "matmul", pa, wob[:, cc, dt2 * 128:(dt2 + 1) * 128],
                         comb[:, cc, :], start=(cc == 0), stop=(cc == CC - 1),
                         inc=("mm", 1) if cc == CC - 1 else None)
                if dt2 >= 2:
                    p.wait("sync", ("vS", vs_pre + dt2 - 1))
                elif g > 4:
                    p.wait("sync", ("vS", vs_pre))
                p.dma("sync", xres if dt2 % 2 == 0 else xres2,
                      x2[dt2 * 128:(dt2 + 1) * 128, g * 512:(g + 1) * 512],
                      ("sxa" if dt2 % 2 == 0 else "sxb", 16))
                p.wait("vector", ("mm", p.n("mm")),
                       ("sxa" if dt2 % 2 == 0 else "sxb", p.n("sxa" if dt2 % 2 == 0 else "sxb")))
                p.op("vector", "scalar_tensor_tensor", s_f[:, dt2, :], pa,
                     bob[:, dt2:dt2 + 1], xres if dt2 % 2 == 0 else xres2, ALU.add, ALU.add)
            p.wait("tensor", ("vS", p.n("vS")))
            p.wait("scalar", ("vS", p.n("vS")))
            emit_ln(p, s_f, s_f, lnb)
            p.wait("gpsimd", ("vS", p.n("vS")))
            for dt in range(CC):
                p.dma("gpsimd", x3[dt * 128:(dt + 1) * 128, (g - 4) * 512:(g - 3) * 512],
                      s_f[:, dt, :], ("so", 16))
            p.wait("vector", ("so", p.n("so")))
            mm_gend = p.n("mm")
        p.wait("gpsimd", ("so", p.n("so")))


def build_phase3():
    nc = bass.Bass("TRN2", target_bir_lowering=False)
    ctx = contextlib.ExitStack()
    x2 = din(nc, "x2", [D, T])
    wqm = din(nc, "wqm", [D, D])
    wkm = din(nc, "wkm", [D, D])
    wvm = din(nc, "wvm", [D, D])
    wom = din(nc, "wom", [D, D])
    bqc = din(nc, "bqc", [128, CC])
    bkc = din(nc, "bkc", [128, CC])
    bvr = din(nc, "bvr", [1, D])
    boc = din(nc, "boc", [128, CC])
    sinq = din(nc, "sinq", [32, HALF])
    cosq = din(nc, "cosq", [32, HALF])
    sink = din(nc, "sink", [32, T])
    cosk = din(nc, "cosk", [32, T])
    pbias = din(nc, "pbias", [128, 1])
    zbias = din(nc, "zbias", [128, 1])
    trim = din(nc, "trim", [128, 128])
    ocf = din(nc, "ocf", [128, 1])
    orf = din(nc, "orf", [1, 128])
    x3 = dout(nc, "x3", [D, HALF])
    x2b = dscr(nc, "x2b", [CC, 128, T])
    k_scr = dscr(nc, "k_scr", [NT, 64, H * 128])
    v_scr = dscr(nc, "v_scr", [NT, 128, H * 65])

    with ctx:
        p = Plan(nc, ctx)
        p.sem("ld0", "ga", "gw", "gw2", "sa", "sb", "sr", "sxa", "sxb", "stB", "stC", "stK", "stV", "stV2", "ka", "kb", "kc2", "kd", "va", "vb", "vc", "vd", "so", "mm", "vS", "sS", "gS", "g1a", "g1b", "g2a", "g2b", "sxw")
        emit_phase3(p, nc, x2, wqm, wkm, wvm, wom, bqc, bkc, bvr, boc, sinq, cosq, sink, cosk, pbias, zbias, trim, ocf, orf, x3, x2b, k_scr, v_scr)
        p.build()
    return nc


def emit_phase4(p, nc, x3, wrt, w1t, b1c, w2t, b2m, ws1t, bs1c, ws2t, bs2c, ident, oneh, ocf, orf, x4, x3b, fused=False):
    ctx = contextlib.ExitStack()
    with ctx:
        cbuf = sb(ctx, nc, "cbuf", [128, HALF], BF16)
        wrtb = sb(ctx, nc, "wrtb", [128, CC, NE], BF16)
        b1cb = sb(ctx, nc, "b1cb", [128, NE * 32], F32)
        bs1cb = sb(ctx, nc, "bs1cb", [128, 32], F32)
        bs2cb = sb(ctx, nc, "bs2cb", [128, CC], F32)
        b2b = sb(ctx, nc, "b2b", [NE, D], BF16)
        idb = sb(ctx, nc, "idb", [128, 128], BF16)
        onehb = sb(ctx, nc, "onehb", [NE, NE * 128], BF16)
        orb = sb(ctx, nc, "orb", [1, 128], BF16)
        ocfb = sb(ctx, nc, "ocfb", [128, 1], F32)
        orfb = sb(ctx, nc, "orfb", [1, 128], F32)
        gatesT = sb(ctx, nc, "gatesT", [NE, HALF], BF16)
        gb = sb(ctx, nc, "gb", [128, 1024], BF16)
        probs = sb(ctx, nc, "probs", [128, NE], F32)
        pm = sb(ctx, nc, "pm", [128, NE], F32)
        keep = sb(ctx, nc, "keep", [128, NE], F32)
        gcol = sb(ctx, nc, "gcol", [128, 1], F32)
        gcol2 = sb(ctx, nc, "gcol2", [128, 1], F32)
        gbf = sb(ctx, nc, "gbf", [128, NE], BF16)
        xw3 = sb(ctx, nc, "xw3", [128, CC, HYT], BF16)
        w1blk = sb(ctx, nc, "w1blk", [128, CC, 128], BF16)
        w1blk2 = sb(ctx, nc, "w1blk2", [128, CC, 128], BF16)
        w2blk = sb(ctx, nc, "w2blk", [128, 32, 128], BF16)
        w2blk2 = sb(ctx, nc, "w2blk2", [128, 32, 128], BF16)
        hE = sb(ctx, nc, "hE", [128, 32, HYT], BF16)
        htmp = sb(ctx, nc, "htmp", [128, 512], BF16)
        htmp2 = sb(ctx, nc, "htmp2", [128, 512], BF16)
        acc = sb(ctx, nc, "acc", [128, CC, HYT], F32)
        s_f = sb(ctx, nc, "s_f", [128, CC, 512], F32)
        xres = sb(ctx, nc, "xres", [128, 512], F32)
        xres2 = sb(ctx, nc, "xres2", [128, 512], F32)
        sq = sb(ctx, nc, "sq", [128, 512], F32)
        sq2 = sb(ctx, nc, "sq2", [128, 512], F32)
        rows = [sb(ctx, nc, f"row{i}", [1, 512], F32) for i in range(4)]
        pA = [psum(ctx, nc, f"pA{i}", [128, 512]) for i in range(4)]
        psS0 = psum(ctx, nc, "psS0", [1, 512])
        psS1 = psum(ctx, nc, "psS1", [1, 512])
        psT = psum(ctx, nc, "psT", [8, 128], BF16)
        lnb = dict(psS=(psS0, psS1), psB=(pA[0], pA[1]), sq=(sq, sq2),
                   rows=rows, ones_col_f=ocfb, ones_row_f=orfb)

        for dst, src in ((b1cb, b1c), (bs1cb, bs1c), (bs2cb, bs2c), (ocfb, ocf), (orfb, orf)):
            p.dma("sync", dst, src, ("ld0", 16))
        p.dma("gpsimd", idb, ident, ("ga", 16))
        p.dma("gpsimd", onehb, oneh, ("ga", 16))
        p.dma("gpsimd", orb, orf, ("ga", 16))
        p.dma("gpsimd", b2b, b2m, ("ga", 16))
        for cc in range(CC):
            p.dma("gpsimd", wrtb[:, cc, :], wrt[cc * 128:(cc + 1) * 128, :], ("ga", 16))
        for cc in range(CC):
            p.dma("gpsimd", cbuf, x3[cc * 128:(cc + 1) * 128, :], ("ga", 16))
            p.wait("gpsimd", ("ga", p.n("ga")))
            p.dma("gpsimd", x3b[cc, :, :], cbuf, ("ga", 16))
            p.wait("gpsimd", ("ga", p.n("ga")))
        gaA = p.n("ga")

        p.wait("sync", ("ga", gaA), ("ld0", p.n("ld0")))
        mm_hyend = 0
        psfree = [None, None, None, None]
        hfree = [None, None]
        for hy in range(NH):
            if hy > 0:
                p.wait("sync", ("mm", mm_hyend))
            for cc in range(CC):
                p.dma("sync", xw3[:, cc, :], x3b[cc, :, hy * HYT:(hy + 1) * HYT], ("sxw", 16))
            ldw = p.n("sxw")
            p.wait("tensor", ("sxw", ldw))
            for tt in range(HYT // 128):
                pr = pA[0][:, 0:NE]
                p.wait("tensor", ("vS", p.n("vS")), ("sS", p.n("sS")))
                if psfree[0] is not None:
                    p.wait("tensor", psfree[0])
                    psfree[0] = None
                for cc in range(CC):
                    p.op("tensor", "matmul", pr, xw3[:, cc, tt * 128:(tt + 1) * 128],
                         wrtb[:, cc, :], start=(cc == 0), stop=(cc == CC - 1),
                         inc=("mm", 1) if cc == CC - 1 else None)
                p.wait("vector", ("mm", p.n("mm")))
                p.op("vector", "tensor_reduce", gcol, pr, AX.X, ALU.max, negate=True)
                p.wait("scalar", ("vS", p.n("vS")))
                p.op("scalar", "activation", probs, pr, AF.Exp, bias=gcol)
                p.wait("vector", ("sS", p.n("sS")))
                p.op("vector", "tensor_reduce", gcol, probs, AX.X, ALU.add)
                p.op("vector", "reciprocal", gcol2, gcol)
                p.op("vector", "tensor_scalar", probs, probs, gcol2, None, ALU.mult)
                p.op("vector", "tensor_reduce", gcol, probs, AX.X, ALU.max)
                p.op("vector", "tensor_scalar", keep, probs, gcol, None, ALU.is_ge)
                p.op("vector", "scalar_tensor_tensor", pm, keep, -2.0, probs, ALU.mult, ALU.add)
                p.op("vector", "tensor_reduce", gcol, pm, AX.X, ALU.max)
                p.op("vector", "tensor_scalar", keep, probs, gcol, None, ALU.is_ge)
                p.op("vector", "tensor_tensor", gbf, probs, keep, ALU.mult)
                p.wait("tensor", ("vS", p.n("vS")))
                p.op("tensor", "transpose", psT, gbf, idb, inc=("mm", 1))
                p.wait("scalar", ("mm", p.n("mm")))
                p.op("scalar", "copy", gatesT[:, hy * HYT + tt * 128: hy * HYT + (tt + 1) * 128], psT)
            gts = p.n("sS")
            for e in range(NE + 1):
                if e < NE:
                    p.wait("tensor", ("sS", gts), ("vS", p.n("vS")), ("sS", p.n("sS")))
                    if psfree[1] is not None:
                        p.wait("tensor", psfree[1])
                    p.op("tensor", "matmul", pA[1], onehb[:, e * 128:(e + 1) * 128],
                         gatesT[0:NE, hy * HYT: hy * HYT + 512],
                         start=True, stop=True, inc=("mm", 1))
                    p.wait("scalar", ("mm", p.n("mm")))
                    p.op("scalar", "copy", gb[:, 0:512], pA[1])
                    p.wait("tensor", ("sS", p.n("sS")))
                    p.op("tensor", "matmul", pA[1], onehb[:, e * 128:(e + 1) * 128],
                         gatesT[0:NE, hy * HYT + 512: hy * HYT + 1024],
                         start=True, stop=True, inc=("mm", 1))
                    p.wait("scalar", ("mm", p.n("mm")))
                    p.op("scalar", "copy", gb[:, 512:1024], pA[1])
                    psfree[1] = ("sS", p.n("sS"))
                mmA0 = p.n("mm")
                if e > 0:
                    p.wait("scalar", ("mm", mm_Bend))
                    p.wait("vector", ("mm", mm_Bend))
                gidx = 0
                for ft in range(32):
                    wb_ = w1blk if ft % 2 == 0 else w1blk2
                    gnm = "g1a" if ft % 2 == 0 else "g1b"
                    if ft >= 2:
                        p.wait("gpsimd", ("mm", mmA0 + (ft - 1) * 2))
                    elif e > 0 or hy > 0:
                        p.wait("gpsimd", ("mm", mmA0))
                    src = (w1t[e, ft] if e < NE else ws1t[ft]).rearrange("q (cc n) -> q cc n", n=128)
                    p.dma("gpsimd", wb_, src, (gnm, 16))
                    p.wait("tensor", (gnm, p.n(gnm)))
                    for th in range(2):
                        bank = gidx % 4
                        ph = pA[bank]
                        if psfree[bank] is not None:
                            p.wait("tensor", psfree[bank])
                        for cc in range(CC):
                            p.op("tensor", "matmul", ph, wb_[:, cc, :],
                                 xw3[:, cc, th * 512:(th + 1) * 512],
                                 start=(cc == 0), stop=(cc == CC - 1),
                                 inc=("mm", 1) if cc == CC - 1 else None)
                        bias = b1cb[:, e * 32 + ft: e * 32 + ft + 1] if e < NE else bs1cb[:, ft:ft + 1]
                        p.wait("scalar", ("mm", p.n("mm")))
                        if e < NE:
                            hb_ = htmp if gidx % 2 == 0 else htmp2
                            if hfree[gidx % 2] is not None:
                                p.wait("scalar", hfree[gidx % 2])
                            p.op("scalar", "activation", hb_, ph, AF.Gelu_apprx_tanh, bias=bias)
                            psfree[bank] = ("sS", p.n("sS"))
                            p.wait("vector", ("sS", p.n("sS")))
                            p.op("vector", "tensor_tensor", hE[:, ft, th * 512:(th + 1) * 512],
                                 hb_, gb[:, th * 512:(th + 1) * 512], ALU.mult)
                            hfree[gidx % 2] = ("vS", p.n("vS"))
                        else:
                            p.op("scalar", "activation", hE[:, ft, th * 512:(th + 1) * 512],
                                 ph, AF.Gelu_apprx_tanh, bias=bias)
                            psfree[bank] = ("sS", p.n("sS"))
                        gidx += 1
                hdone_v, hdone_s = p.n("vS"), p.n("sS")
                mmB0 = p.n("mm")
                gidx = 0
                for dt in range(CC):
                    wb2 = w2blk if dt % 2 == 0 else w2blk2
                    gnm = "g2a" if dt % 2 == 0 else "g2b"
                    if dt >= 2:
                        p.wait("gpsimd", ("mm", mmB0 + (dt - 1) * 2))
                    elif e > 0 or hy > 0:
                        p.wait("gpsimd", ("mm", mmB0))
                    src = w2t[e, dt] if e < NE else ws2t[dt]
                    p.dma("gpsimd", wb2, src, (gnm, 16))
                    p.wait("tensor", (gnm, p.n(gnm)), ("vS", hdone_v), ("sS", hdone_s))
                    for th in range(2):
                        bank = gidx % 4
                        ph = pA[bank]
                        gidx += 1
                        if psfree[bank] is not None:
                            p.wait("tensor", psfree[bank])
                        for ft in range(32):
                            p.op("tensor", "matmul", ph, wb2[:, ft, :],
                                 hE[:, ft, th * 512:(th + 1) * 512],
                                 start=(ft == 0), stop=(e != 0 and ft == 31),
                                 inc=("mm", 1) if (e != 0 and ft == 31) else None)
                        if e == 0:
                            p.op("tensor", "matmul", ph, b2b[0:NE, dt * 128:(dt + 1) * 128],
                                 gatesT[0:NE, hy * HYT + th * 512: hy * HYT + (th + 1) * 512],
                                 start=False, stop=True, inc=("mm", 1))
                        p.wait("vector", ("mm", p.n("mm")))
                        asl = acc[:, dt, th * 512:(th + 1) * 512]
                        if e == 0:
                            p.op("vector", "tensor_copy", asl, ph)
                        elif e < NE:
                            p.op("vector", "tensor_tensor", asl, asl, ph, ALU.add)
                        else:
                            p.op("vector", "scalar_tensor_tensor", sq, ph, bs2cb[:, dt:dt + 1], asl,
                                 ALU.add, ALU.add)
                            p.op("vector", "tensor_copy", asl, sq)
                        psfree[bank] = ("vS", p.n("vS"))
                mm_Bend = p.n("mm")
            mm_hyend = p.n("mm")
            for j in range(2):
                col0 = hy * HYT + j * 512
                vs_pre = p.n("vS")
                for dt in range(CC):
                    if dt >= 2:
                        p.wait("sync", ("vS", vs_pre + dt - 1))
                    elif hy > 0 or j > 0:
                        p.wait("sync", ("vS", vs_pre))
                    p.dma("sync", xres if dt % 2 == 0 else xres2,
                          x3[dt * 128:(dt + 1) * 128, col0:col0 + 512],
                          ("sxa" if dt % 2 == 0 else "sxb", 16))
                    p.wait("vector", ("sxa" if dt % 2 == 0 else "sxb",
                                      p.n("sxa" if dt % 2 == 0 else "sxb")))
                    p.op("vector", "scalar_tensor_tensor", s_f[:, dt, :],
                         acc[:, dt, j * 512:(j + 1) * 512], MOE_SCALE,
                         xres if dt % 2 == 0 else xres2, ALU.mult, ALU.add)
                p.wait("tensor", ("vS", p.n("vS")))
                p.wait("scalar", ("vS", p.n("vS")))
                emit_ln(p, s_f, s_f, lnb)
                p.wait("gpsimd", ("vS", p.n("vS")))
                for dt in range(CC):
                    p.dma("gpsimd", x4[dt * 128:(dt + 1) * 128, col0:col0 + 512],
                          s_f[:, dt, :], ("so", 16))
                p.wait("vector", ("so", p.n("so")))
                del vs_pre
        p.wait("gpsimd", ("so", p.n("so")))


def build_phase4():
    nc = bass.Bass("TRN2", target_bir_lowering=False)
    ctx = contextlib.ExitStack()
    x3 = din(nc, "x3", [D, HALF])
    wrt = din(nc, "wrt", [D, NE])
    w1t = din(nc, "w1t", [NE, 32, 128, D])
    b1c = din(nc, "b1c", [128, NE * 32])
    w2t = din(nc, "w2t", [NE, CC, 128, 32, 128])
    b2m = din(nc, "b2m", [NE, D])
    ws1t = din(nc, "ws1t", [32, 128, D])
    bs1c = din(nc, "bs1c", [128, 32])
    ws2t = din(nc, "ws2t", [CC, 128, 32, 128])
    bs2c = din(nc, "bs2c", [128, CC])
    ident = din(nc, "ident", [128, 128])
    oneh = din(nc, "oneh", [NE, NE * 128])
    ocf = din(nc, "ocf", [128, 1])
    orf = din(nc, "orf", [1, 128])
    x4 = dout(nc, "x4", [D, HALF])
    x3b = dscr(nc, "x3b", [CC, 128, HALF])

    with ctx:
        p = Plan(nc, ctx)
        p.sem("ld0", "ga", "gw", "gw2", "sa", "sb", "sr", "sxa", "sxb", "stB", "stC", "stK", "stV", "stV2", "ka", "kb", "kc2", "kd", "va", "vb", "vc", "vd", "so", "mm", "vS", "sS", "gS", "g1a", "g1b", "g2a", "g2b", "sxw")
        emit_phase4(p, nc, x3, wrt, w1t, b1c, w2t, b2m, ws1t, bs1c, ws2t, bs2c, ident, oneh, ocf, orf, x4, x3b)
        p.build()
    return nc


def emit_phase1(p, nc, xin, wqkv, qbc, kbc, vbr, wquad, memv, memvbc, memg, memgbr, mscr, imix, declog, wout, obc, delta, negm, trow, dcol, signs, ident, oneh32, bval, pbias, zbias, trim, ocf, orf, x1o, xb, k_scr, v_scr, fused=False):
    ctx = contextlib.ExitStack()
    with ctx:
        cbuf = sb(ctx, nc, "cbuf", [128, T], BF16)
        zcol = sb(ctx, nc, "zcol", [128, 1], BF16)
        wkv = sb(ctx, nc, "wkv", [128, CC, 2 * D], BF16)
        wquadb = sb(ctx, nc, "wquadb", [128, CC, 256], BF16)
        memvb = sb(ctx, nc, "memvb", [128, CC, D], BF16)
        memgb = sb(ctx, nc, "memgb", [128, CC, H], BF16)
        woutb = sb(ctx, nc, "woutb", [128, CC, D], BF16)
        qbb = sb(ctx, nc, "qbb", [128, CC], F32)
        kbb = sb(ctx, nc, "kbb", [128, CC], F32)
        vbb = sb(ctx, nc, "vbb", [1, D], BF16)
        mvbb = sb(ctx, nc, "mvbb", [128, CC], F32)
        mgbb = sb(ctx, nc, "mgbb", [1, H], BF16)
        obb = sb(ctx, nc, "obb", [128, CC], F32)
        mscb = sb(ctx, nc, "mscb", [1, H], F32)
        imixb = sb(ctx, nc, "imixb", [1, 1], F32)
        declb = sb(ctx, nc, "declb", [1, H], F32)
        deltab = sb(ctx, nc, "deltab", [128, 128], F32)
        negmb = sb(ctx, nc, "negmb", [128, 128], F32)
        trowb = sb(ctx, nc, "trowb", [1, 128], F32)
        dcolb = sb(ctx, nc, "dcolb", [128, 1], F32)
        signsb = sb(ctx, nc, "signsb", [128, 96], F32)
        idb = sb(ctx, nc, "idb", [128, 128], BF16)
        oh32b = sb(ctx, nc, "oh32b", [32, 32 * 128], BF16)
        bvalb = sb(ctx, nc, "bvalb", [128, 1], F32)
        pbb = sb(ctx, nc, "pbb", [128, 1], F32)
        zbb = sb(ctx, nc, "zbb", [128, 1], F32)
        trib = sb(ctx, nc, "trib", [128, 128], BF16)
        orb = sb(ctx, nc, "orb", [1, 128], BF16)
        o64 = sb(ctx, nc, "o64", [1, 64], F32)
        ocfb = sb(ctx, nc, "ocfb", [128, 1], F32)
        orfb = sb(ctx, nc, "orfb", [1, 128], F32)
        Ah = sb(ctx, nc, "Ah", [128, H * 128], BF16)
        dvh = sb(ctx, nc, "dvh", [1, H * 128], BF16)
        drh = sb(ctx, nc, "drh", [128, H], BF16)
        d128 = sb(ctx, nc, "d128", [1, H], F32)
        lndr = sb(ctx, nc, "lndr", [1, H], F32)
        lndc = sb(ctx, nc, "lndc", [128, H], F32)
        alphc = sb(ctx, nc, "alphc", [128, 1], F32)
        mscbc = sb(ctx, nc, "mscbc", [128, H], F32)
        carry = sb(ctx, nc, "carry", [1, 576], BF16)
        q4 = sb(ctx, nc, "q4", [128, 256], F32)
        wl = sb(ctx, nc, "wl", [128, 96], F32)
        rd = sb(ctx, nc, "rdb", [128, 96], F32)
        jw = sb(ctx, nc, "jw", [128, 96], F32)
        t96a = sb(ctx, nc, "t96a", [128, 96], F32)
        t96b = sb(ctx, nc, "t96b", [128, 96], F32)
        t16a = sb(ctx, nc, "t16a", [128, 16], F32)
        t16b = sb(ctx, nc, "t16b", [128, 16], F32)
        t16c = sb(ctx, nc, "t16c", [128, 16], F32)
        outer = sb(ctx, nc, "outer", [128, 576], BF16)
        msb = sb(ctx, nc, "msb", [128, 576], F32)
        gated = sb(ctx, nc, "gated", [128, NT], F32)
        gT = sb(ctx, nc, "gT", [32, 128], BF16)
        gb = sb(ctx, nc, "gb", [128, 512], BF16)
        mvs = sb(ctx, nc, "mvs", [128, 512], BF16)
        xmw = sb(ctx, nc, "xmw", [128, CC, 129], BF16)
        xmw2 = sb(ctx, nc, "xmw2", [128, CC, 129], BF16)
        xwk = sb(ctx, nc, "xwk", [128, CC, 513], BF16)
        xwk2 = sb(ctx, nc, "xwk2", [128, CC, 513], BF16)
        kraw = sb(ctx, nc, "kraw", [64, 2, 512], BF16)
        visb = sb(ctx, nc, "visb", [128, H * 65], BF16)
        visb2 = sb(ctx, nc, "visb2", [128, H * 65], BF16)
        qbuf = sb(ctx, nc, "qbuf", [64, H, 512], BF16)
        kchs = [sb(ctx, nc, f"kch{i}", [64, 4, 128], BF16) for i in range(4)]
        vchs = [sb(ctx, nc, f"vch{i}", [128, 4 * 65], BF16) for i in range(4)]
        prbufs = [sb(ctx, nc, f"prbx{i}", [128, 512], BF16) for i in range(8)]
        rcrow = sb(ctx, nc, "rcrow", [1, 512], F32)
        rcs = sb(ctx, nc, "rcs", [64, 512], BF16)
        comb = sb(ctx, nc, "comb", [128, CC, 512], BF16)
        s_f = sb(ctx, nc, "s_f", [128, CC, 512], F32)
        xres = sb(ctx, nc, "xres", [128, 512], F32)
        xres2 = sb(ctx, nc, "xres2", [128, 512], F32)
        sq = sb(ctx, nc, "sq", [128, 512], F32)
        sq2 = sb(ctx, nc, "sq2", [128, 512], F32)
        rows = [sb(ctx, nc, f"row{i}", [1, 512], F32) for i in range(4)]
        pA = [psum(ctx, nc, f"pA{i}", [128, 512]) for i in range(2)]
        pS = [psum(ctx, nc, f"pS{i}", [128, 512]) for i in range(2)]
        soP = [psum(ctx, nc, f"soP{i}", [65, 512]) for i in range(4)]
        lnb = dict(psS=(soP[0][0:1, :], soP[1][0:1, :]), psB=(pA[0], pA[1]),
                   sq=(sq, sq2), rows=rows, ones_col_f=ocfb, ones_row_f=orfb)

        for dst, src in ((qbb, qbc), (kbb, kbc), (obb, obc), (mvbb, memvbc), (mscb, mscr), (imixb, imix),
                         (declb, declog), (deltab, delta), (negmb, negm), (trowb, trow),
                         (dcolb, dcol), (signsb, signs), (bvalb, bval), (pbb, pbias),
                         (zbb, zbias), (ocfb, ocf), (orfb, orf)):
            p.dma("sync", dst, src, ("ld0", 16))
        for dst, src in ((vbb, vbr), (mgbb, memgbr), (idb, ident),
                         (oh32b, oneh32), (trib, trim), (orb, orf)):
            p.dma("gpsimd", dst, src, ("ga", 16))
        p.gop("memset", zcol, 0.0)
        p.gop("memset", o64, 1.0)
        p.op("vector", "memset", visb.rearrange("q (h m) -> q h m", m=65)[:, :, 64:65], 1.0)
        p.op("vector", "memset", visb2.rearrange("q (h m) -> q h m", m=65)[:, :, 64:65], 1.0)
        p.op("vector", "memset", carry, 0.0)
        for cc in range(CC):
            p.dma("gpsimd", cbuf, xin[cc * 128:(cc + 1) * 128, :], ("ga", 16))
            p.wait("gpsimd", ("ga", p.n("ga")))
            p.dma("gpsimd", xb[cc, :, 1:T + 1], cbuf, ("ga", 16))
            p.dma("gpsimd", xb[cc, :, 0:1], zcol, ("ga", 16), allow_slow_non_contiguous=True)
            p.wait("gpsimd", ("ga", p.n("ga")))
        gaA = p.n("ga")
        for cc in range(CC):
            p.dma("gpsimd", wkv[:, cc, :], wqkv[cc * 128:(cc + 1) * 128, D:3 * D], ("gw", 16))
            p.dma("gpsimd", wquadb[:, cc, :], wquad[cc * 128:(cc + 1) * 128, :], ("gw", 16))
            p.dma("gpsimd", memvb[:, cc, :], memv[cc * 128:(cc + 1) * 128, :], ("gw", 16))
            p.dma("gpsimd", memgb[:, cc, :], memg[cc * 128:(cc + 1) * 128, :], ("gw", 16))
            p.dma("gpsimd", woutb[:, cc, :], wout[cc * 128:(cc + 1) * 128, :], ("gw", 16))
        gwT = p.n("gw")

        p.wait("scalar", ("ld0", p.n("ld0")))
        p.op("scalar", "activation", lndr, declb, AF.Sigmoid)
        p.op("scalar", "activation", d128, imixb.broadcast_to([1, H]), AF.Sigmoid)
        p.wait("vector", ("sS", p.n("sS")))
        p.op("vector", "tensor_scalar", lndr, lndr, 1e-30, None, ALU.max)
        p.wait("scalar", ("vS", p.n("vS")))
        p.op("scalar", "activation", lndr, lndr, AF.Ln)
        p.wait("tensor", ("sS", p.n("sS")), ("ga", gaA))
        p.op("tensor", "matmul", pS[0][:, 0:H], orfb, lndr, start=True, stop=True)
        p.op("tensor", "matmul", pS[1][:, 0:1], orfb, d128[:, 0:1], start=True, stop=True)
        p.op("tensor", "matmul", pS[1][:, 1:1 + H], orfb, mscb, start=True, stop=True, inc=("mm", 1))
        p.wait("scalar", ("mm", p.n("mm")))
        p.op("scalar", "copy", lndc, pS[0][:, 0:H])
        p.op("scalar", "copy", alphc, pS[1][:, 0:1])
        p.op("scalar", "copy", mscbc, pS[1][:, 1:1 + H])
        p.wait("vector", ("sS", p.n("sS")))
        p.op("vector", "tensor_scalar", d128, lndr, 128.0, None, ALU.mult)
        p.wait("scalar", ("vS", p.n("vS")))
        p.op("scalar", "activation", d128, d128, AF.Exp)
        for h in range(H):
            p.op("vector", "tensor_scalar", Ah[:, h * 128:(h + 1) * 128], deltab,
                 lndc[:, h:h + 1], None, ALU.mult)
            p.op("vector", "tensor_tensor", Ah[:, h * 128:(h + 1) * 128],
                 Ah[:, h * 128:(h + 1) * 128], negmb, ALU.add)
            p.wait("scalar", ("vS", p.n("vS")))
            p.op("scalar", "activation", Ah[:, h * 128:(h + 1) * 128],
                 Ah[:, h * 128:(h + 1) * 128], AF.Exp)
            p.op("scalar", "activation", dvh[:, h * 128:(h + 1) * 128], trowb, AF.Exp,
                 scale=lndr[:, h:h + 1])
            p.op("scalar", "activation", drh[:, h:h + 1], dcolb, AF.Exp,
                 scale=lndc[:, h:h + 1])
        dec_done_s, dec_done_v = p.n("sS"), p.n("vS")

        p.wait("sync", ("ga", gaA))
        I6 = [0, 0, 0, 1, 1, 2]
        J6 = [1, 2, 3, 2, 3, 3]
        mm_mend = {}
        for ct in range(NT):
            bufm = xmw if ct % 2 == 0 else xmw2
            snm = "sa" if ct % 2 == 0 else "sb"
            if ct >= 2:
                p.wait("sync", ("mm", mm_mend[ct - 2]))
            for cc in range(CC):
                p.dma("sync", bufm[:, cc, :], xb[cc, :, ct * 128:ct * 128 + 129], (snm, 16))
            p.wait("tensor", (snm, p.n(snm)), ("gw", gwT))
            pa = pA[ct % 2]
            if ct >= 2:
                p.wait("tensor", ("sS", q4_done[ct - 2] + 1))
            for cc in range(CC):
                p.op("tensor", "matmul", pa[:, 0:64], bufm[:, cc, 0:128],
                     wquadb[:, cc, 0:64], start=(cc == 0), stop=(cc == CC - 1))
            for cc in range(CC):
                p.op("tensor", "matmul", pa[:, 64:256], bufm[:, cc, 1:129],
                     wquadb[:, cc, 64:256], start=(cc == 0), stop=(cc == CC - 1),
                     inc=("mm", 1) if cc == CC - 1 else None)
            if ct == 16:
                p.wait("vector", ("mm", p.n("mm")))
                p.op("vector", "tensor_scalar", pa[0:1, 0:64], pa[0:1, 0:64],
                     bvalb[0:1, :], None, ALU.mult)
                p.wait("scalar", ("vS", p.n("vS")))
            p.wait("scalar", ("mm", p.n("mm")))
            p.op("scalar", "copy", q4, pa[:, 0:256])
            q4_done = q4_done if ct > 0 else {}
            q4_done[ct] = p.n("sS")
            p.wait("tensor", ("sS", p.n("sS")))
            for cc in range(CC):
                p.op("tensor", "matmul", pa[:, 256:256 + H], bufm[:, cc, 1:129],
                     memgb[:, cc, :], start=(cc == 0), stop=False)
            p.op("tensor", "matmul", pa[:, 256:256 + H], orb, mgbb,
                 start=False, stop=True, inc=("mm", 1))
            p.wait("scalar", ("mm", p.n("mm")), ("vS", p.n("vS")))
            p.op("scalar", "activation", t16c, pa[:, 256:256 + H], AF.Sigmoid)
            p.wait("vector", ("sS", p.n("sS")))
            for (dst, o1, o2) in ((wl, 0, 64), (rd, 128, 192)):
                q4v = q4.rearrange("q (g h i) -> q g h i", g=4, i=4)
                g1, g2 = o1 // 64, o2 // 64
                dstv = dst.rearrange("q (h k) -> q h k", k=6)
                for kk in range(6):
                    i, j = PAIRS[kk]
                    p.op("vector", "tensor_tensor", t16a, q4v[:, g1, :, i], q4v[:, g2, :, j], ALU.mult)
                    p.op("vector", "tensor_tensor", t16b, q4v[:, g1, :, j], q4v[:, g2, :, i], ALU.mult)
                    p.op("vector", "tensor_tensor", dstv[:, :, kk], t16a, t16b, ALU.subtract)
                p.op("vector", "tensor_tensor", t96a, dst, dst, ALU.mult)
                t96v = t96a.rearrange("q (h k) -> q h k", k=6)
                p.op("vector", "tensor_tensor", t16a, t96v[:, :, 0], t96v[:, :, 1], ALU.add)
                for kk in range(2, 6):
                    p.op("vector", "tensor_tensor", t16a, t16a, t96v[:, :, kk], ALU.add)
                p.wait("scalar", ("vS", p.n("vS")))
                p.op("scalar", "activation", t16b, t16a, AF.Sqrt)
                p.wait("vector", ("sS", p.n("sS")))
                p.op("vector", "tensor_scalar", t16b, t16b, 1e-12, None, ALU.max)
                p.op("vector", "reciprocal", t16a, t16b)
                for kk in range(6):
                    p.op("vector", "tensor_tensor", dstv[:, :, kk], dstv[:, :, kk], t16a, ALU.mult)
            wlv = wl.rearrange("q (h k) -> q h k", k=6)
            p.op("vector", "tensor_tensor", jw.rearrange("q (h k) -> q h k", k=6),
                 wlv[:, :, ::-1], signsb.rearrange("q (h k) -> q h k", k=6), ALU.mult)
            jv = jw.rearrange("q (h i) -> q h i", i=6)
            p.op("vector", "tensor_tensor", outer.rearrange("q (h i j) -> q h i j", i=6, j=6),
                 jv.unsqueeze(3).broadcast_to([128, 16, 6, 6]),
                 jv.unsqueeze(2).broadcast_to([128, 16, 6, 6]), ALU.mult)
            p.wait("tensor", ("vS", p.n("vS")), ("sS", dec_done_s))
            if ct > 0:
                p.wait("tensor", ("vS", carry_v[ct - 1]))
            for h in range(H):
                pM = pS[h // 8]
                p.op("tensor", "matmul", pM[:, (h % 8) * 36:(h % 8) * 36 + 36],
                     Ah[:, h * 128:(h + 1) * 128], outer[:, h * 36:(h + 1) * 36],
                     start=True, stop=False)
                p.op("tensor", "matmul", pM[:, (h % 8) * 36:(h % 8) * 36 + 36],
                     dvh[0:1, h * 128:(h + 1) * 128], carry[:, h * 36:(h + 1) * 36],
                     start=False, stop=True)
            for h in range(H):
                cP = soP[2 + h // 8]
                p.op("tensor", "matmul", cP[0:1, (h % 8) * 36:(h % 8) * 36 + 36],
                     drh[:, h:h + 1], outer[:, h * 36:(h + 1) * 36],
                     start=True, stop=True, inc=("mm", 1) if h == H - 1 else None)
            p.wait("vector", ("mm", p.n("mm")))
            for h in range(H):
                cP = soP[2 + h // 8]
                p.op("vector", "scalar_tensor_tensor", carry[:, h * 36:(h + 1) * 36],
                     carry[:, h * 36:(h + 1) * 36], d128[:, h:h + 1],
                     cP[0:1, (h % 8) * 36:(h % 8) * 36 + 36], ALU.mult, ALU.add)
            carry_v = carry_v if ct > 0 else {}
            carry_v[ct] = p.n("vS")
            p.wait("scalar", ("vS", p.n("vS")))
            p.op("scalar", "copy", msb[:, 0:288], pS[0][:, 0:288])
            p.op("scalar", "copy", msb[:, 288:576], pS[1][:, 0:288])
            p.wait("vector", ("sS", p.n("sS")))
            rv = rd.rearrange("q (h i) -> q h i", i=6)
            mv_ = msb.rearrange("q (g h i j) -> q g h i j", g=2, h=8, i=6)
            rv2 = rd.rearrange("q (g h i) -> q g h i", g=2, h=8)
            rmv = t96a.rearrange("q (g h j) -> q g h j", g=2, h=8)
            for i in range(6):
                p.op("vector", "tensor_tensor", t96b.rearrange("q (g h j) -> q g h j", g=2, h=8),
                     rv2[:, :, :, i].unsqueeze(3).broadcast_to([128, 2, 8, 6]),
                     mv_[:, :, :, i, :], ALU.mult)
                if i == 0:
                    p.op("vector", "tensor_copy", t96a, t96b)
                else:
                    p.op("vector", "tensor_tensor", t96a, t96a, t96b, ALU.add)
            p.op("vector", "tensor_tensor", t96b, t96a, rd, ALU.mult)
            tv = t96b.rearrange("q (h k) -> q h k", k=6)
            p.op("vector", "tensor_tensor", t16a, tv[:, :, 0], tv[:, :, 1], ALU.add)
            for kk in range(2, 6):
                p.op("vector", "tensor_tensor", t16a, t16a, tv[:, :, kk], ALU.add)
            p.op("vector", "tensor_tensor", t96b, t96a, t96a, ALU.mult)
            p.op("vector", "tensor_tensor", t16b, tv[:, :, 0], tv[:, :, 1], ALU.add)
            for kk in range(2, 6):
                p.op("vector", "tensor_tensor", t16b, t16b, tv[:, :, kk], ALU.add)
            p.op("vector", "tensor_tensor", t16b, t16b, t16a, ALU.subtract)
            p.op("vector", "scalar_tensor_tensor", t16a, t16b, alphc, t16a, ALU.mult, ALU.add)
            p.op("vector", "tensor_tensor", t16a, t16a, mscbc, ALU.mult)
            p.wait("scalar", ("vS", p.n("vS")))
            p.op("scalar", "activation", t16b, t16a, AF.Sigmoid)
            p.wait("vector", ("sS", p.n("sS")))
            p.op("vector", "tensor_tensor", t16a, t16b, t16c, ALU.mult)
            p.op("vector", "tensor_reduce", gated[:, ct:ct + 1], t16a, AX.X, ALU.add)
            p.op("vector", "tensor_scalar", gated[:, ct:ct + 1], gated[:, ct:ct + 1],
                 1.0 / H, None, ALU.mult)
            mm_mend[ct] = p.n("mm")
        p.wait("vector", ("vS", p.n("vS")))
        p.op("vector", "tensor_copy", cbuf[:, 0:NT], gated)
        p.wait("tensor", ("vS", p.n("vS")), ("sS", p.n("sS")))
        p.op("tensor", "transpose", pS[0][0:32, 0:128].bitcast(BF16)[:, 0:128], cbuf[:, 0:NT], idb, inc=("mm", 1))
        p.wait("scalar", ("mm", p.n("mm")))
        p.op("scalar", "copy", gT[:, 0:128], pS[0][0:32, 0:128].bitcast(BF16)[:, 0:128])
        gT_done = p.n("sS")

        vs_s1 = {}
        for tcx in range(CC):
            bufa = xwk if tcx % 2 == 0 else xwk2
            snm = "sa" if tcx % 2 == 0 else "sb"
            if tcx >= 2:
                p.wait("sync", ("mm", vs_s1[tcx - 2]))
            else:
                p.wait("sync", ("mm", p.n("mm")))
            for cc in range(CC):
                p.dma("sync", bufa[:, cc, 0:512], xb[cc, :, tcx * 512 + 1:(tcx + 1) * 512 + 1], (snm, 16))
            p.wait("tensor", (snm, p.n(snm)))
            if tcx == 0:
                p.wait("tensor", ("vS", p.n("vS")), ("sS", p.n("sS")))
            for dt in range(CC):
                pa = pA[dt % 2]
                p.wait("tensor", ("sS", p.n("sS")))
                for cc in range(CC):
                    p.op("tensor", "matmul", pa, wkv[:, cc, dt * 128:(dt + 1) * 128],
                         bufa[:, cc, 0:512], start=(cc == 0), stop=(cc == CC - 1),
                         inc=("mm", 1) if cc == CC - 1 else None)
                p.wait("scalar", ("mm", p.n("mm")), ("stK", p.n("stK")))
                p.op("scalar", "activation", kraw[0:64, 0, :], pa[0:64, :], AF.Identity,
                     bias=kbb[0:64, dt:dt + 1])
                p.op("scalar", "activation", kraw[0:64, 1, :], pa[64:128, :], AF.Identity,
                     bias=kbb[64:128, dt:dt + 1])
                p.wait("gpsimd", ("sS", p.n("sS")))
                for kj in range(4):
                    p.dma("gpsimd",
                          k_scr.rearrange("a q (h n) -> a q h n", n=128)[tcx * 4 + kj, 0:64, 2 * dt:2 * dt + 2, :],
                          kraw[:, :, kj * 128:(kj + 1) * 128], ("stK", 16))
            for tt4 in range(4):
                vb_ = visb if tt4 % 2 == 0 else visb2
                vsem = "stV" if tt4 % 2 == 0 else "stV2"
                p.wait("scalar", (vsem, p.n(vsem)))
                for dh2 in range(2):
                    pa = pA[dh2]
                    p.wait("tensor", ("sS", p.n("sS")))
                    for cc in range(CC):
                        p.op("tensor", "matmul", pa, bufa[:, cc, tt4 * 128:(tt4 + 1) * 128],
                             wkv[:, cc, D + dh2 * 512:D + (dh2 + 1) * 512],
                             start=(cc == 0), stop=False)
                    p.op("tensor", "matmul", pa, orb, vbb[:, dh2 * 512:(dh2 + 1) * 512],
                         start=False, stop=True, inc=("mm", 1))
                    p.wait("scalar", ("mm", p.n("mm")))
                    vtgt = vb_.rearrange("q (h m) -> q h m", m=65)
                    p.op("scalar", "copy",
                         vtgt[:, dh2 * 8:(dh2 + 1) * 8, 0:64],
                         pa.rearrange("q (h m) -> q h m", m=64))
                p.wait("gpsimd", ("sS", p.n("sS")))
                p.dma("gpsimd", v_scr[tcx * 4 + tt4], vb_, (vsem, 16))
            vs_s1[tcx] = p.n("mm")
        stK_T, stV_T, stV2_T = p.n("stK"), p.n("stV"), p.n("stV2")
        p.waitn("gpsimd", "mm")
        for cc in range(CC):
            p.dma("gpsimd", wkv[:, cc, 0:D], wqkv[cc * 128:(cc + 1) * 128, 0:D], ("gw2", 16))
        gw2T = p.n("gw2")

        p.wait("sync", ("stK", stK_T), ("stV", stV_T), ("stV2", stV2_T))
        p.wait("tensor", ("gw2", gw2T))
        for g in range(8):
            Kg = 4 * g + 4
            snm = "sa" if g % 2 == 0 else "sb"
            bufa = xwk if g % 2 == 0 else xwk2
            if g == 0:
                p.wait("sync", ("mm", p.n("mm")))
            else:
                p.wait("sync", ("mm", mm_gend))
            for cc in range(CC):
                p.dma("sync", bufa[:, cc, 0:512], xb[cc, :, g * 512 + 1:(g + 1) * 512 + 1], (snm, 16))
            p.wait("tensor", (snm, p.n(snm)), ("vS", p.n("vS")), ("sS", p.n("sS")))
            for dt in range(CC):
                pa = pA[dt % 2]
                if dt >= 2:
                    p.wait("tensor", ("sS", p.n("sS")))
                for cc in range(CC):
                    p.op("tensor", "matmul", pa, wkv[:, cc, dt * 128:(dt + 1) * 128],
                         bufa[:, cc, 0:512], start=(cc == 0), stop=(cc == CC - 1),
                         inc=("mm", 1) if cc == CC - 1 else None)
                p.wait("scalar", ("mm", p.n("mm")))
                p.op("scalar", "activation", qbuf[0:64, 2 * dt, :], pa[0:64, :], AF.Identity,
                     bias=qbb[0:64, dt:dt + 1])
                p.op("scalar", "activation", qbuf[0:64, 2 * dt + 1, :], pa[64:128, :], AF.Identity,
                     bias=qbb[64:128, dt:dt + 1])
            qdone = p.n("sS")
            p.wait("tensor", ("sS", qdone))
            scfree = [None, None, None, None]
            prfree = {(i, j): None for i in range(4) for j in range(2)}
            pready = {}
            av_mm = {}
            for hb in range(4):
                for kc in range(Kg):
                    knm = ("ka", "kb", "kc2", "kd")[kc % 4]
                    vnm = ("va", "vb", "vc", "vd")[kc % 4]
                    kb_ = kchs[kc % 4]
                    vb_ = vchs[kc % 4]
                    if kc >= 4:
                        p.wait("sync", ("mm", av_mm[kc - 4]))
                    elif hb > 0 or g > 0:
                        p.wait("sync", ("mm", mm_kprev))
                    p.dma("sync", kb_.rearrange("q a n -> q (a n)"),
                          k_scr[kc, 0:64, hb * 512:(hb + 1) * 512], (knm, 16))
                    p.dma("sync", vb_, v_scr[kc, :, hb * 260:(hb + 1) * 260], (vnm, 16))
                    p.wait("tensor", (knm, p.n(knm)), (vnm, p.n(vnm)))
                    dk0 = Kg - 4
                    scb = [pS[0], pS[1], pA[0], pA[1]]
                    for hin in range(4):
                        h = hb * 4 + hin
                        if scfree[hin] is not None:
                            p.wait("tensor", scfree[hin])
                        p.op("tensor", "matmul", scb[hin], kb_[0:64, hin, :], qbuf[0:64, h, :],
                             start=True, stop=True, inc=("mm", 1))
                        mmsc = p.n("mm")
                        if prfree[(hin, kc % 2)] is not None:
                            p.wait("scalar", prfree[(hin, kc % 2)])
                        p.wait("scalar", ("mm", mmsc))
                        prbh = prbufs[hin * 2 + kc % 2]
                        p.op("scalar", "activation", prbh, scb[hin], AF.Exp, scale=0.125,
                             bias=pbb if (kc < 16 and g >= 4) else zbb)
                        scfree[hin] = ("sS", p.n("sS"))
                        pready[(hin, kc % 2)] = ("sS", p.n("sS"))
                        if kc >= dk0:
                            jd = kc - dk0
                            p.wait("vector", ("sS", p.n("sS")))
                            if jd > 0:
                                p.op("vector", "memset", prbh[:, 0:jd * 128], 0.0)
                            p.op("vector", "tensor_tensor", prbh[:, jd * 128:(jd + 1) * 128],
                                 prbh[:, jd * 128:(jd + 1) * 128], trib, ALU.mult)
                            pready[(hin, kc % 2)] = ("vS", p.n("vS"))
                    if kc > 0:
                        vprev = vchs[(kc - 1) % 4]
                        for hin in range(4):
                            p.wait("tensor", pready[(hin, (kc - 1) % 2)])
                            p.op("tensor", "matmul", soP[hin],
                                 vprev[:, hin * 65:(hin + 1) * 65],
                                 prbufs[hin * 2 + (kc - 1) % 2],
                                 start=(kc - 1 == 0), stop=False, inc=("mm", 1))
                            prfree[(hin, (kc - 1) % 2)] = ("mm", p.n("mm"))
                        av_mm[kc - 1] = p.n("mm")
                    if kc == Kg - 1:
                        vlast = vchs[kc % 4]
                        for hin in range(4):
                            p.wait("tensor", pready[(hin, kc % 2)])
                            p.op("tensor", "matmul", soP[hin],
                                 vlast[:, hin * 65:(hin + 1) * 65],
                                 prbufs[hin * 2 + kc % 2],
                                 start=(kc == 0), stop=True, inc=("mm", 1))
                            prfree[(hin, kc % 2)] = ("mm", p.n("mm"))
                        av_mm[kc] = p.n("mm")
                mm_kprev = p.n("mm")
                p.wait("vector", ("mm", p.n("mm")))
                for hin in range(4):
                    h = hb * 4 + hin
                    p.op("vector", "reciprocal", rcrow, soP[hin][64:65, :])
                    p.wait("tensor", ("vS", p.n("vS")))
                    if scfree[0] is not None:
                        p.wait("tensor", scfree[0])
                    p.op("tensor", "matmul", pS[0][0:64, :], o64, rcrow, start=True, stop=True,
                         inc=("mm", 1))
                    p.wait("scalar", ("mm", p.n("mm")))
                    p.op("scalar", "copy", rcs, pS[0][0:64, :])
                    scfree[0] = ("sS", p.n("sS"))
                    p.wait("vector", ("sS", p.n("sS")))
                    p.op("vector", "tensor_tensor",
                         comb[(h % 2) * 64:(h % 2) * 64 + 64, h // 2, :],
                         soP[hin][0:64, :], rcs, ALU.mult)
                p.wait("tensor", ("vS", p.n("vS")))
            p.wait("tensor", ("sS", gT_done))
            for j in range(4):
                p.op("tensor", "matmul", pS[1][:, j * 128:(j + 1) * 128],
                     oh32b[:, (4 * g + j) * 128:(4 * g + j + 1) * 128], gT,
                     start=(j == 0), stop=(j == 3), inc=("mm", 1) if j == 3 else None)
            p.wait("scalar", ("mm", p.n("mm")))
            p.op("scalar", "copy", gb, pS[1])
            gbd = p.n("sS")
            for dt in range(CC):
                pa = pA[dt % 2]
                if dt >= 2:
                    p.wait("tensor", ("vS", p.n("vS")))
                for cc in range(CC):
                    p.op("tensor", "matmul", pa, memvb[:, cc, dt * 128:(dt + 1) * 128],
                         bufa[:, cc, 0:512], start=(cc == 0), stop=(cc == CC - 1),
                         inc=("mm", 1) if cc == CC - 1 else None)
                p.wait("scalar", ("mm", p.n("mm")), ("sS", gbd), ("vS", p.n("vS")))
                p.op("scalar", "activation", mvs, pa, AF.Identity, bias=mvbb[:, dt:dt + 1])
                p.wait("vector", ("sS", p.n("sS")))
                p.op("vector", "tensor_tensor", prbufs[0], mvs, gb, ALU.mult)
                p.op("vector", "tensor_tensor", comb[:, dt, :], comb[:, dt, :], prbufs[0], ALU.add)
            p.wait("tensor", ("vS", p.n("vS")))
            vs_pre = p.n("vS")
            for dt2 in range(CC):
                pa = pA[dt2 % 2]
                if dt2 >= 2:
                    p.wait("tensor", ("vS", vs_pre + dt2 - 1))
                for cc in range(CC):
                    p.op("tensor", "matmul", pa, woutb[:, cc, dt2 * 128:(dt2 + 1) * 128],
                         comb[:, cc, :], start=(cc == 0), stop=(cc == CC - 1),
                         inc=("mm", 1) if cc == CC - 1 else None)
                if dt2 >= 2:
                    p.wait("sync", ("vS", vs_pre + dt2 - 1))
                elif g > 0:
                    p.wait("sync", ("vS", vs_pre))
                p.dma("sync", xres if dt2 % 2 == 0 else xres2,
                      xin[dt2 * 128:(dt2 + 1) * 128, g * 512:(g + 1) * 512],
                      ("sxa" if dt2 % 2 == 0 else "sxb", 16))
                p.wait("vector", ("mm", p.n("mm")),
                       ("sxa" if dt2 % 2 == 0 else "sxb", p.n("sxa" if dt2 % 2 == 0 else "sxb")))
                p.op("vector", "scalar_tensor_tensor", s_f[:, dt2, :], pa,
                     obb[:, dt2:dt2 + 1], xres if dt2 % 2 == 0 else xres2, ALU.add, ALU.add)
            p.wait("tensor", ("vS", p.n("vS")))
            p.wait("scalar", ("vS", p.n("vS")))
            emit_ln(p, s_f, s_f, lnb)
            if g >= 4 or fused:
                oc0 = g * 512 if fused else (g - 4) * 512
                p.wait("gpsimd", ("vS", p.n("vS")))
                for dt in range(CC):
                    p.dma("gpsimd", x1o[dt * 128:(dt + 1) * 128, oc0:oc0 + 512],
                          s_f[:, dt, :], ("so", 16))
                p.wait("vector", ("so", p.n("so")))
            mm_gend = p.n("mm")
        p.wait("gpsimd", ("so", p.n("so")))


def build_phase1():
    nc = bass.Bass("TRN2", target_bir_lowering=False)
    ctx = contextlib.ExitStack()
    xin = din(nc, "xin", [D, T])
    wqkv = din(nc, "wqkv", [D, 3 * D])
    qbc = din(nc, "qbc", [128, CC])
    kbc = din(nc, "kbc", [128, CC])
    vbr = din(nc, "vbr", [1, D])
    wquad = din(nc, "wquad", [D, 256])
    memv = din(nc, "memv", [D, D])
    memvbc = din(nc, "memvbc", [128, CC])
    memg = din(nc, "memg", [D, H])
    memgbr = din(nc, "memgbr", [1, H])
    mscr = din(nc, "mscr", [1, H])
    imix = din(nc, "imix", [1, 1])
    declog = din(nc, "declog", [1, H])
    wout = din(nc, "wout", [D, D])
    obc = din(nc, "obc", [128, CC])
    delta = din(nc, "delta", [128, 128])
    negm = din(nc, "negm", [128, 128])
    trow = din(nc, "trow", [1, 128])
    dcol = din(nc, "dcol", [128, 1])
    signs = din(nc, "signs", [128, 96])
    ident = din(nc, "ident", [128, 128])
    oneh32 = din(nc, "oneh32", [32, 32 * 128])
    bval = din(nc, "bval", [128, 1])
    pbias = din(nc, "pbias", [128, 1])
    zbias = din(nc, "zbias", [128, 1])
    trim = din(nc, "trim", [128, 128])
    ocf = din(nc, "ocf", [128, 1])
    orf = din(nc, "orf", [1, 128])
    x1o = dout(nc, "x1o", [D, HALF])
    xb = dscr(nc, "xb", [CC, 128, T + 1])
    k_scr = dscr(nc, "k_scr", [NT, 64, H * 128])
    v_scr = dscr(nc, "v_scr", [NT, 128, H * 65])

    with ctx:
        p = Plan(nc, ctx)
        p.sem("ld0", "ga", "gw", "gw2", "sa", "sb", "sr", "sxa", "sxb", "stB", "stC", "stK", "stV", "stV2", "ka", "kb", "kc2", "kd", "va", "vb", "vc", "vd", "so", "mm", "vS", "sS", "gS", "g1a", "g1b", "g2a", "g2b", "sxw")
        emit_phase1(p, nc, xin, wqkv, qbc, kbc, vbr, wquad, memv, memvbc, memg, memgbr, mscr, imix, declog, wout, obc, delta, negm, trow, dcol, signs, ident, oneh32, bval, pbias, zbias, trim, ocf, orf, x1o, xb, k_scr, v_scr)
        p.build()
    return nc


def build_fused():
    nc = bass.Bass("TRN2", target_bir_lowering=False)
    ctx = contextlib.ExitStack()
    ident = din(nc, "ident", [128, 128])
    trim = din(nc, "trim", [128, 128])
    ocf = din(nc, "ocf", [128, 1])
    orf = din(nc, "orf", [1, 128])
    bval = din(nc, "bval", [128, 1])
    pbias = din(nc, "pbias", [128, 1])
    zbias = din(nc, "zbias", [128, 1])
    xin = din(nc, "xin", [D, T])
    wqkv = din(nc, "wqkv", [D, 3 * D])
    qbc = din(nc, "qbc", [128, CC])
    kbc = din(nc, "kbc", [128, CC])
    vbr = din(nc, "vbr", [1, D])
    wquad = din(nc, "wquad", [D, 256])
    memv = din(nc, "memv", [D, D])
    memvbc = din(nc, "memvbc", [128, CC])
    memg = din(nc, "memg", [D, H])
    memgbr = din(nc, "memgbr", [1, H])
    mscr = din(nc, "mscr", [1, H])
    imix = din(nc, "imix", [1, 1])
    declog = din(nc, "declog", [1, H])
    wout = din(nc, "wout", [D, D])
    obc = din(nc, "obc", [128, CC])
    delta = din(nc, "delta", [128, 128])
    negm = din(nc, "negm", [128, 128])
    trow = din(nc, "trow", [1, 128])
    dcol = din(nc, "dcol", [128, 1])
    signs = din(nc, "signs", [128, 96])
    oneh32 = din(nc, "oneh32", [32, 32 * 128])
    wr = din(nc, "wr", [D, D])
    wk = din(nc, "wk", [D, D])
    wv = din(nc, "wv", [D, D])
    mur = din(nc, "mur", [128, CC])
    muk = din(nc, "muk", [128, CC])
    wqm = din(nc, "wqm", [D, D])
    wkm = din(nc, "wkm", [D, D])
    wvm = din(nc, "wvm", [D, D])
    wom = din(nc, "wom", [D, D])
    bqc = din(nc, "bqc", [128, CC])
    bkc = din(nc, "bkc", [128, CC])
    bvr = din(nc, "bvr", [1, D])
    boc = din(nc, "boc", [128, CC])
    sinq = din(nc, "sinq", [32, HALF])
    cosq = din(nc, "cosq", [32, HALF])
    sink = din(nc, "sink", [32, T])
    cosk = din(nc, "cosk", [32, T])
    wrt = din(nc, "wrt", [D, NE])
    w1t = din(nc, "w1t", [NE, 32, 128, D])
    b1c = din(nc, "b1c", [128, NE * 32])
    w2t = din(nc, "w2t", [NE, CC, 128, 32, 128])
    b2m = din(nc, "b2m", [NE, D])
    ws1t = din(nc, "ws1t", [32, 128, D])
    bs1c = din(nc, "bs1c", [128, 32])
    ws2t = din(nc, "ws2t", [CC, 128, 32, 128])
    bs2c = din(nc, "bs2c", [128, CC])
    oneh = din(nc, "oneh", [NE, NE * 128])
    x1i = dscr(nc, "x1i", [D, T], F32)
    x2i = dscr(nc, "x2i", [D, T], F32)
    x3i = dscr(nc, "x3i", [D, HALF], F32)
    xb1 = dscr(nc, "xb1", [CC, 128, T + 1])
    ks1 = dscr(nc, "ks1", [NT, 64, H * 128])
    vs1 = dscr(nc, "vs1", [NT, 128, H * 65])
    x1b = dscr(nc, "x1b", [CC, 128, T + 1])
    xr_s = dscr(nc, "xr_s", [CC, 128, T])
    xk_s = dscr(nc, "xk_s", [CC, 128, T])
    r_s = dscr(nc, "r_s", [CC, 128, T])
    ka_s = dscr(nc, "ka_s", [CC, 128, T])
    x2b = dscr(nc, "x2b", [CC, 128, T])
    ks3 = dscr(nc, "ks3", [NT, 64, H * 128])
    vs3 = dscr(nc, "vs3", [NT, 128, H * 65])
    x3b = dscr(nc, "x3b", [CC, 128, HALF])
    x4 = dout(nc, "x4", [D, HALF])
    with ctx:
        p = Plan(nc, ctx)
        p.sem(*BARRIER_SEMS)
        emit_phase1(p, nc, xin, wqkv, qbc, kbc, vbr, wquad, memv, memvbc, memg, memgbr,
                    mscr, imix, declog, wout, obc, delta, negm, trow, dcol, signs, ident,
                    oneh32, bval, pbias, zbias, trim, ocf, orf, x1i, xb1, ks1, vs1, fused=True)
        barrier(p)
        emit_phase2(p, nc, x1i, wr, wk, wv, mur, muk, bval, ocf, orf, x2i,
                    x1b, xr_s, xk_s, r_s, ka_s, fused=True)
        barrier(p)
        emit_phase3(p, nc, x2i, wqm, wkm, wvm, wom, bqc, bkc, bvr, boc, sinq, cosq,
                    sink, cosk, pbias, zbias, trim, ocf, orf, x3i, x2b, ks3, vs3, fused=True)
        barrier(p)
        emit_phase4(p, nc, x3i, wrt, w1t, b1c, w2t, b2m, ws1t, bs1c, ws2t, bs2c,
                    ident, oneh, ocf, orf, x4, x3b, fused=True)
        p.build()
    return nc


_CACHE = {}
_USE_FUSED = True


def _cols(b):
    return np.ascontiguousarray(b.reshape(-1, 128).T.astype(np.float32))


def kernel(L0_attn_x, L0_attn_qkv_w, L0_attn_qkv_b, L0_attn_w1w, L0_attn_w2w,
           L0_attn_r1w, L0_attn_r2w, L0_attn_memv_w, L0_attn_memv_b, L0_attn_memg_w,
           L0_attn_memg_b, L0_attn_mem_scale, L0_attn_iter_mix, L0_attn_out_w,
           L0_attn_out_b, L0_attn_decay_logits, L0_ffn_W_r, L0_ffn_W_k, L0_ffn_W_v,
           L0_ffn_mu_r, L0_ffn_mu_k, L1_attn_Wq, L1_attn_bq, L1_attn_Wk, L1_attn_bk,
           L1_attn_Wv, L1_attn_bv, L1_attn_Wo, L1_attn_bo, L1_ffn_w_router, L1_ffn_W1,
           L1_ffn_b1, L1_ffn_W2, L1_ffn_b2, L1_ffn_Ws1, L1_ffn_bs1, L1_ffn_Ws2,
           L1_ffn_bs2):
    f32 = np.float32
    x = np.asarray(L0_attn_x, f32)
    ones_c = np.ones((128, 1), f32)
    ones_r = np.ones((1, 128), f32)
    delta = (np.arange(128)[None, :] - np.arange(128)[:, None]).astype(f32)
    negm = np.where(delta >= 1, 0.0, NEG).astype(f32)
    signs = np.tile(np.array(JSIGN, f32), H)[None, :].repeat(128, 0)
    oneh32 = np.zeros((32, 32 * 128), f32)
    for t in range(32):
        oneh32[t, t * 128:(t + 1) * 128] = 1
    onehE = np.zeros((NE, NE * 128), f32)
    for e in range(NE):
        onehE[e, e * 128:(e + 1) * 128] = 1
    trim = np.triu(np.ones((128, 128), f32))
    ident = np.eye(128, dtype=f32)
    zb = np.zeros((128, 1), f32)
    pos = np.arange(T, dtype=np.float64)
    inv_freq = np.exp(np.arange(0, DH, 2, dtype=np.float64) * (-np.log(10000.0) / DH))
    ang = pos[:, None] * inv_freq[None, :]
    sin_t = np.sin(ang).astype(f32).T.copy()
    cos_t = np.cos(ang).astype(f32).T.copy()

    qb = np.asarray(L0_attn_qkv_b, f32)
    common1 = dict(
        wqkv=np.ascontiguousarray(np.asarray(L0_attn_qkv_w, f32).T),
        qbc=_cols(qb[0:D]), kbc=_cols(qb[D:2 * D]), vbr=qb[2 * D:].reshape(1, D).copy(),
        wquad=np.ascontiguousarray(np.concatenate(
            [np.asarray(L0_attn_w1w, f32).T, np.asarray(L0_attn_w2w, f32).T,
             np.asarray(L0_attn_r1w, f32).T, np.asarray(L0_attn_r2w, f32).T], 1)),
        memv=np.ascontiguousarray(np.asarray(L0_attn_memv_w, f32).T),
        memvbc=_cols(np.asarray(L0_attn_memv_b, f32)),
        memg=np.ascontiguousarray(np.asarray(L0_attn_memg_w, f32).T),
        memgbr=np.asarray(L0_attn_memg_b, f32).reshape(1, H).copy(),
        mscr=np.asarray(L0_attn_mem_scale, f32).reshape(1, H).copy(),
        imix=np.asarray(L0_attn_iter_mix, f32).reshape(1, 1).copy(),
        declog=np.asarray(L0_attn_decay_logits, f32).reshape(1, H).copy(),
        wout=np.ascontiguousarray(np.asarray(L0_attn_out_w, f32).T),
        obc=_cols(np.asarray(L0_attn_out_b, f32)),
        delta=delta, negm=negm,
        trow=np.arange(128, dtype=f32).reshape(1, -1),
        dcol=(128 - np.arange(128, dtype=f32)).reshape(-1, 1),
        signs=signs, ident=ident, oneh32=oneh32, trim=trim, zbias=zb,
        ocf=ones_c, orf=ones_r)
    common2 = dict(
        wr=np.ascontiguousarray(np.asarray(L0_ffn_W_r, f32).T),
        wk=np.ascontiguousarray(np.asarray(L0_ffn_W_k, f32).T),
        wv=np.ascontiguousarray(np.asarray(L0_ffn_W_v, f32).T),
        mur=_cols(np.asarray(L0_ffn_mu_r, f32)), muk=_cols(np.asarray(L0_ffn_mu_k, f32)))
    common3 = dict(
        wqm=np.ascontiguousarray(np.asarray(L1_attn_Wq, f32)),
        wkm=np.ascontiguousarray(np.asarray(L1_attn_Wk, f32)),
        wvm=np.ascontiguousarray(np.asarray(L1_attn_Wv, f32)),
        wom=np.ascontiguousarray(np.asarray(L1_attn_Wo, f32)),
        bqc=_cols(np.asarray(L1_attn_bq, f32)), bkc=_cols(np.asarray(L1_attn_bk, f32)),
        bvr=np.asarray(L1_attn_bv, f32).reshape(1, D).copy(),
        boc=_cols(np.asarray(L1_attn_bo, f32)))
    W1 = np.asarray(L1_ffn_W1, f32)
    W2 = np.asarray(L1_ffn_W2, f32)
    w1t = np.ascontiguousarray(W1.reshape(NE, 32, 128, CC, 128).transpose(0, 1, 4, 3, 2)
                               .reshape(NE, 32, 128, D))
    w2t = np.ascontiguousarray(W2.reshape(NE, CC, 128, 32, 128).transpose(0, 1, 4, 3, 2))
    Ws1 = np.asarray(L1_ffn_Ws1, f32)
    Ws2 = np.asarray(L1_ffn_Ws2, f32)
    common4 = dict(
        wrt=np.ascontiguousarray(np.asarray(L1_ffn_w_router, f32).T),
        w1t=w1t, b1c=np.ascontiguousarray(np.asarray(L1_ffn_b1, f32).reshape(NE * 32, 128).T),
        w2t=w2t, b2m=np.asarray(L1_ffn_b2, f32),
        ws1t=np.ascontiguousarray(Ws1.reshape(32, 128, CC, 128).transpose(0, 3, 2, 1)
                                  .reshape(32, 128, D)),
        bs1c=_cols(np.asarray(L1_ffn_bs1, f32)),
        ws2t=np.ascontiguousarray(Ws2.reshape(CC, 128, 32, 128).transpose(0, 3, 2, 1)),
        bs2c=_cols(np.asarray(L1_ffn_bs2, f32)),
        oneh=onehE)

    def core_extras(b, h):
        xT = np.ascontiguousarray(x[b].T)
        own = slice(h * HALF, (h + 1) * HALF)
        d = {}
        if h == 0:
            d["xin"] = np.ascontiguousarray(
                np.concatenate([np.zeros((D, HALF), f32), xT[:, 0:HALF]], 1))
            d["bval"] = zb
            d["pbias"] = np.full((128, 1), NEG, f32)
        else:
            d["xin"] = xT
            d["bval"] = ones_c
            d["pbias"] = zb
        d["sinq"] = np.ascontiguousarray(sin_t[:, own])
        d["cosq"] = np.ascontiguousarray(cos_t[:, own])
        d["sink"] = np.ascontiguousarray(np.concatenate([sin_t[:, 0:HALF], sin_t[:, own]], 1))
        d["cosk"] = np.ascontiguousarray(np.concatenate([cos_t[:, 0:HALF], cos_t[:, own]], 1))
        return d

    if _USE_FUSED:
        if "fused" not in _CACHE:
            _CACHE["fused"] = build_fused()
        ncF = _CACHE["fused"]
        ims = []
        for b in range(B):
            for h in (0, 1):
                d = {}
                d.update(common1)
                d.update(common2)
                d.update(common3)
                d.update(common4)
                d.update(core_extras(b, h))
                ims.append(d)
        r = run_bass_kernel_spmd(ncF, ims, core_ids=list(range(NC)))
        out = np.stack([np.concatenate(
            [r.results[2 * b]["x4"], r.results[2 * b + 1]["x4"]], 1).T for b in range(B)])
        return np.ascontiguousarray(out.astype(np.float32))

    if "ncs" not in _CACHE:
        _CACHE["ncs"] = (build_phase1(), build_phase2(), build_phase3(), build_phase4())
    nc1, nc2, nc3, nc4 = _CACHE["ncs"]
    im1 = []
    for b in range(B):
        for h in (0, 1):
            d = dict(common1)
            d.update({k: v for k, v in core_extras(b, h).items()
                      if k in ("xin", "bval", "pbias")})
            im1.append(d)
    r1 = run_bass_kernel_spmd(nc1, im1, core_ids=list(range(NC)))
    x1T = [np.concatenate([r1.results[2 * b]["x1o"], r1.results[2 * b + 1]["x1o"]], 1)
           for b in range(B)]
    im2 = []
    for b in range(B):
        for h in (0, 1):
            d = dict(common2)
            d["ocf"] = ones_c
            d["orf"] = ones_r
            if h == 0:
                d["x1"] = np.ascontiguousarray(
                    np.concatenate([x1T[b][:, 0:HALF], x1T[b][:, 0:HALF]], 1))
                d["bval"] = zb
            else:
                d["x1"] = np.ascontiguousarray(x1T[b])
                d["bval"] = ones_c
            im2.append(d)
    r2 = run_bass_kernel_spmd(nc2, im2, core_ids=list(range(NC)))
    x2T = [np.concatenate([r2.results[2 * b]["x2"], r2.results[2 * b + 1]["x2"]], 1)
           for b in range(B)]
    im3 = []
    for b in range(B):
        for h in (0, 1):
            d = dict(common3)
            ce = core_extras(b, h)
            d.update({k: ce[k] for k in ("pbias", "sinq", "cosq", "sink", "cosk")})
            d.update(dict(trim=trim, zbias=zb, ocf=ones_c, orf=ones_r))
            if h == 0:
                d["x2"] = np.ascontiguousarray(
                    np.concatenate([x2T[b][:, 0:HALF], x2T[b][:, 0:HALF]], 1))
            else:
                d["x2"] = np.ascontiguousarray(x2T[b])
            im3.append(d)
    r3 = run_bass_kernel_spmd(nc3, im3, core_ids=list(range(NC)))
    im4 = []
    for c in range(NC):
        d = dict(common4)
        d.update(dict(ident=ident, ocf=ones_c, orf=ones_r))
        d["x3"] = np.ascontiguousarray(r3.results[c]["x3"])
        im4.append(d)
    r4 = run_bass_kernel_spmd(nc4, im4, core_ids=list(range(NC)))
    out = np.stack([np.concatenate([r4.results[2 * b]["x4"], r4.results[2 * b + 1]["x4"]], 1).T
                    for b in range(B)])
    return np.ascontiguousarray(out.astype(np.float32))


# version 07
# speedup vs baseline: 1.7407x; 1.0373x over previous
import contextlib
import numpy as np
import concourse.bass as bass
import concourse.mybir as mybir
from concourse.bass_utils import run_bass_kernel_spmd

F32 = mybir.dt.float32
BF16 = mybir.dt.bfloat16
AF = mybir.ActivationFunctionType
ALU = mybir.AluOpType
AX = mybir.AxisListType

B, T, D, H, DH = 4, 4096, 1024, 16, 64
HALF = T // 2
NC = 8
CC = D // 128
NT = T // 128
NG = T // 512
DFF = 4096
NE = 8
MOE_SCALE = float(1.0 / np.sqrt(1 + 2.0 / 8.0))
NH = 2
HYT = HALF // NH
PAIRS = [(0, 1), (0, 2), (0, 3), (1, 2), (1, 3), (2, 3)]
JSIGN = [1.0, -1.0, 1.0, 1.0, -1.0, 1.0]
NEG = -1e9

ENGINES = ("sync", "gpsimd", "tensor", "vector", "scalar")


class Plan:
    def __init__(self, nc, ctx):
        self.nc, self.ctx = nc, ctx
        self.ops = {e: [] for e in ENGINES}
        self.sems = {}

    def sem(self, *names):
        for name in names:
            h = self.ctx.enter_context(self.nc.semaphore(name))
            self.sems[name] = [h, 0]

    def n(self, name):
        return self.sems[name][1]

    def wait(self, eng, *pairs):
        for (nm, v) in pairs:
            self.ops[eng].append(("w", self.sems[nm][0], v))

    def waitn(self, eng, *names):
        for nm in names:
            self.ops[eng].append(("w", self.sems[nm][0], self.sems[nm][1]))

    def op(self, eng, method, *args, inc=None, **kw):
        if eng == "vector" or eng == "scalar":
            assert inc is None or inc[0] in ("vS", "sS")
            nm = "vS" if eng == "vector" else "sS"
            rec = self.sems[nm]
            if rec[1] > 0:
                self.ops[eng].append(("w", rec[0], rec[1]))
            rec[1] += 1
            self.ops[eng].append(("o", method, args, kw, rec[0], 1))
            return
        if inc is not None:
            nm, k = inc
            rec = self.sems[nm]
            rec[1] += k
            self.ops[eng].append(("o", method, args, kw, rec[0], k))
        else:
            self.ops[eng].append(("o", method, args, kw, None, 0))

    def gop(self, method, *args, **kw):
        rec = self.sems["gS"]
        if rec[1] > 0:
            self.ops["gpsimd"].append(("w", rec[0], rec[1]))
        rec[1] += 1
        self.ops["gpsimd"].append(("o", method, args, kw, rec[0], 1))

    def dma(self, eng, out, in_, inc, **kw):
        self.op(eng, "dma_start", inc=inc, out=out, in_=in_, **kw)

    def build(self):
        blk = self.ctx.enter_context(self.nc.Block())
        for eng in ENGINES:
            lst = self.ops[eng]
            if not lst:
                continue

            def runner(e, lst=lst):
                for it in lst:
                    if it[0] == "w":
                        e.wait_ge(it[1], it[2])
                    else:
                        _, method, args, kw, h, k = it
                        ins = getattr(e, method)(*args, **kw)
                        if h is not None:
                            ins.then_inc(h, k)
            getattr(blk, eng)(runner)


def din(nc, name, shape, dtype=F32):
    return nc.dram_tensor(name, list(shape), dtype, kind="ExternalInput").ap()


def dout(nc, name, shape, dtype=F32):
    return nc.dram_tensor(name, list(shape), dtype, kind="ExternalOutput").ap()


def dscr(nc, name, shape, dtype=BF16):
    return nc.dram_tensor(name, list(shape), dtype, kind="Internal").ap()


_NAMEC = [0]


def sb(ctx, nc, name, shape, dtype):
    _NAMEC[0] += 1
    return ctx.enter_context(nc.sbuf_tensor(f"{name}_{_NAMEC[0]}", list(shape), dtype)).ap()


def psum(ctx, nc, name, shape, dtype=F32):
    _NAMEC[0] += 1
    return ctx.enter_context(nc.psum_tensor(f"{name}_{_NAMEC[0]}", list(shape), dtype)).ap()


def bounce_cast(p, scr, src, buf, ld, st, ncc=CC, tcols=T, pad=False):
    for cc in range(ncc):
        self_ld0 = p.n(ld)
        p.wait("gpsimd", (st, max(0, p.n(st) - 16)))
        p.dma("gpsimd", buf, src[cc * 128:(cc + 1) * 128, :], (ld, 16))
        p.wait("gpsimd", (ld, p.n(ld)))
        if pad:
            p.dma("gpsimd", scr[cc, :, 1:tcols + 1], buf, (st, 16))
        else:
            p.dma("gpsimd", scr[cc, :, :], buf, (st, 16))
        del self_ld0


def emit_ln(p, s, out, bufs, nblk=CC, inv_n=1.0 / D):
    psS0, psS1 = bufs["psS"]
    psB0, psB1 = bufs["psB"]
    sq0, sq1 = bufs["sq"]
    r_m, r_v, r_sd, r_i = bufs["rows"]
    mm0 = p.n("mm")
    for dt in range(nblk):
        p.op("tensor", "matmul", psS0, bufs["ones_col_f"], s[:, dt, :],
             start=(dt == 0), stop=(dt == nblk - 1))
        if dt >= 2:
            p.wait("scalar", ("mm", mm0 + dt - 1))
        p.op("scalar", "square", sq0 if dt % 2 == 0 else sq1, s[:, dt, :])
        p.wait("tensor", ("sS", p.n("sS")))
        p.op("tensor", "matmul", psS1, bufs["ones_col_f"], sq0 if dt % 2 == 0 else sq1,
             start=(dt == 0), stop=(dt == nblk - 1), inc=("mm", 1))
    p.wait("vector", ("mm", p.n("mm")))
    p.op("vector", "tensor_scalar", r_m, psS0, inv_n, None, ALU.mult)
    p.op("vector", "tensor_scalar", r_v, psS1, inv_n, None, ALU.mult)
    p.op("vector", "tensor_tensor", r_sd, r_m, r_m, ALU.mult)
    p.op("vector", "tensor_tensor", r_v, r_v, r_sd, ALU.subtract)
    p.op("vector", "tensor_scalar", r_v, r_v, 0.0, 1e-5, ALU.max, ALU.add)
    p.wait("scalar", ("vS", p.n("vS")))
    p.op("scalar", "activation", r_sd, r_v, AF.Sqrt)
    p.wait("vector", ("sS", p.n("sS")))
    p.op("vector", "reciprocal", r_i, r_sd)
    p.wait("tensor", ("vS", p.n("vS")))
    p.op("tensor", "matmul", psB0, bufs["ones_row_f"], r_m, start=True, stop=True)
    p.op("tensor", "matmul", psB1, bufs["ones_row_f"], r_i, start=True, stop=True, inc=("mm", 1))
    p.wait("vector", ("mm", p.n("mm")))
    for dt in range(nblk):
        p.op("vector", "tensor_tensor", sq0 if dt % 2 == 0 else sq1, s[:, dt, :], psB0, ALU.subtract)
        p.op("vector", "tensor_tensor", out[:, dt, :], sq0 if dt % 2 == 0 else sq1, psB1, ALU.mult)


BARRIER_SEMS = ["ld0", "ga", "gw", "gw2", "sa", "sb", "sr", "sxa", "sxb", "stB", "stC", "stK", "stV", "stV2", "ka", "kb", "kc2", "kd", "va", "vb", "vc", "vd", "so", "mm", "vS", "sS", "gS", "g1a", "g1b", "g2a", "g2b", "sxw"]


def barrier(p):
    vals = [(nm, p.n(nm)) for nm in BARRIER_SEMS]
    for eng in ENGINES:
        for nm, v in vals:
            if v > 0:
                p.ops[eng].append(("w", p.sems[nm][0], v))


def emit_phase2(p, nc, x1, wr, wk, wv, mur, muk, bval, ocf, orf, x2, x1b, xr_s, xk_s, r_s, ka_s, fused=False):
    ctx = contextlib.ExitStack()
    with ctx:
        cbuf = sb(ctx, nc, "cbuf", [128, T], BF16)
        zcol = sb(ctx, nc, "zcol", [128, 1], BF16)
        w_a = sb(ctx, nc, "w_a", [128, CC, 1024], BF16)
        w_b = sb(ctx, nc, "w_b", [128, CC, 1024], BF16)
        w_c = sb(ctx, nc, "w_c", [128, CC, 1024], BF16)
        murb = sb(ctx, nc, "murb", [128, CC], F32)
        mukb = sb(ctx, nc, "mukb", [128, CC], F32)
        bvb = sb(ctx, nc, "bvb", [128, 1], F32)
        ocfb = sb(ctx, nc, "ocfb", [128, 1], F32)
        orfb = sb(ctx, nc, "orfb", [1, 128], F32)
        xw = sb(ctx, nc, "xw", [128, CC, 513], BF16)
        xw2 = sb(ctx, nc, "xw2", [128, CC, 513], BF16)
        xrb = [sb(ctx, nc, f"xrb{i}", [128, CC, 512], BF16) for i in range(2)]
        xkb = [sb(ctx, nc, f"xkb{i}", [128, CC, 512], BF16) for i in range(2)]
        rbf = [sb(ctx, nc, f"rbf{i}", [128, CC, 512], BF16) for i in range(2)]
        kab = [sb(ctx, nc, f"kab{i}", [128, CC, 512], BF16) for i in range(2)]
        dbuf = sb(ctx, nc, "dbuf", [128, 512], F32)
        tcol = sb(ctx, nc, "tcol", [128, CC], F32)
        tcol2 = sb(ctx, nc, "tcol2", [128, CC], F32)
        relu_t = sb(ctx, nc, "relu_t", [128, 512], F32)
        relu_t2 = sb(ctx, nc, "relu_t2", [128, 512], F32)
        s_f = sb(ctx, nc, "s_f", [128, CC, 512], F32)
        xres = sb(ctx, nc, "xres", [128, 512], F32)
        xres2 = sb(ctx, nc, "xres2", [128, 512], F32)
        sq = sb(ctx, nc, "sq", [128, 512], F32)
        sq2 = sb(ctx, nc, "sq2", [128, 512], F32)
        rows = [sb(ctx, nc, f"row{i}", [1, 512], F32) for i in range(4)]
        pB = [psum(ctx, nc, f"pB{i}", [128, 512]) for i in range(4)]
        psS0 = psum(ctx, nc, "psS0", [1, 512])
        psS1 = psum(ctx, nc, "psS1", [1, 512])
        lnb = dict(psS=(psS0, psS1), psB=(pB[0], pB[1]), sq=(sq, sq2),
                   rows=rows, ones_col_f=ocfb, ones_row_f=orfb)

        for dst, src in ((murb, mur), (mukb, muk), (bvb, bval), (ocfb, ocf), (orfb, orf)):
            p.dma("sync", dst, src, ("ld0", 16))
        p.gop("memset", zcol, 0.0)
        for cc in range(CC):
            p.dma("gpsimd", cbuf, x1[cc * 128:(cc + 1) * 128, :], ("ga", 16))
            p.wait("gpsimd", ("ga", p.n("ga")))
            p.dma("gpsimd", x1b[cc, :, 1:T + 1], cbuf, ("ga", 16))
            p.dma("gpsimd", x1b[cc, :, 0:1], zcol, ("ga", 16), allow_slow_non_contiguous=True)
            p.wait("gpsimd", ("ga", p.n("ga")))
        gaA = p.n("ga")
        for cc in range(CC):
            p.dma("gpsimd", w_a[:, cc, :], wr[cc * 128:(cc + 1) * 128, :], ("gw", 16))
            p.dma("gpsimd", w_b[:, cc, :], wk[cc * 128:(cc + 1) * 128, :], ("gw", 16))
            p.dma("gpsimd", w_c[:, cc, :], wv[cc * 128:(cc + 1) * 128, :], ("gw", 16))
        gwT = p.n("gw")

        p.wait("sync", ("ga", gaA), ("ld0", p.n("ld0")))
        p.wait("tensor", ("gw", gwT))
        psfree = [None, None, None, None]
        mm_tc = {}
        xr_done = {}
        for tc in range(NG):
            bufa = xw if tc % 2 == 0 else xw2
            snm = "sa" if tc % 2 == 0 else "sb"
            if tc >= 2:
                p.wait("sync", ("mm", mm_tc[tc - 2]))
            for cc in range(CC):
                p.dma("sync", bufa[:, cc, :], x1b[cc, :, tc * 512:tc * 512 + 513], (snm, 16))
            xr_, xk_ = xrb[tc % 2], xkb[tc % 2]
            rb_, ka_ = rbf[tc % 2], kab[tc % 2]
            p.wait("vector", (snm, p.n(snm)))
            if tc >= 2:
                p.wait("vector", ("mm", mm_tc[tc - 2]))
            for cc in range(CC):
                cur = bufa[:, cc, 1:513]
                prv = bufa[:, cc, 0:512]
                p.op("vector", "tensor_tensor", dbuf, cur, prv, ALU.subtract)
                if tc == 4:
                    p.op("vector", "tensor_scalar", tcol[:, cc:cc + 1], bufa[:, cc, 0:1], bvb, None, ALU.mult)
                    p.op("vector", "tensor_tensor", dbuf[:, 0:1], bufa[:, cc, 1:2], tcol[:, cc:cc + 1], ALU.subtract)
                p.op("vector", "scalar_tensor_tensor", xr_[:, cc, :], dbuf, murb[:, cc:cc + 1], prv, ALU.mult, ALU.add)
                if tc == 4:
                    p.op("vector", "tensor_tensor", tcol2[:, cc:cc + 1], dbuf[:, 0:1], murb[:, cc:cc + 1], ALU.mult)
                    p.op("vector", "tensor_tensor", xr_[:, cc, 0:1], tcol2[:, cc:cc + 1], tcol[:, cc:cc + 1], ALU.add)
                p.op("vector", "scalar_tensor_tensor", xk_[:, cc, :], dbuf, mukb[:, cc:cc + 1], prv, ALU.mult, ALU.add)
                if tc == 4:
                    p.op("vector", "tensor_tensor", tcol2[:, cc:cc + 1], dbuf[:, 0:1], mukb[:, cc:cc + 1], ALU.mult)
                    p.op("vector", "tensor_tensor", xk_[:, cc, 0:1], tcol2[:, cc:cc + 1], tcol[:, cc:cc + 1], ALU.add)
            xr_done[tc] = p.n("vS")
            p.wait("tensor", ("vS", xr_done[tc]))
            for dt in range(CC):
                bank = dt % 4
                if psfree[bank] is not None:
                    p.wait("tensor", psfree[bank])
                for cc in range(CC):
                    p.op("tensor", "matmul", pB[bank], w_a[:, cc, dt * 128:(dt + 1) * 128],
                         xr_[:, cc, :], start=(cc == 0), stop=(cc == CC - 1),
                         inc=("mm", 1) if cc == CC - 1 else None)
                p.wait("scalar", ("mm", p.n("mm")))
                p.op("scalar", "activation", rb_[:, dt, :], pB[bank], AF.Sigmoid)
                psfree[bank] = ("sS", p.n("sS"))
            for dt in range(CC):
                bank = dt % 4
                if psfree[bank] is not None:
                    p.wait("tensor", psfree[bank])
                for cc in range(CC):
                    p.op("tensor", "matmul", pB[bank], w_b[:, cc, dt * 128:(dt + 1) * 128],
                         xk_[:, cc, :], start=(cc == 0), stop=(cc == CC - 1),
                         inc=("mm", 1) if cc == CC - 1 else None)
                rt_ = relu_t if dt % 2 == 0 else relu_t2
                if dt >= 2:
                    p.wait("vector", ("sS", p.n("sS")))
                p.wait("vector", ("mm", p.n("mm")))
                p.op("vector", "tensor_scalar", rt_, pB[bank], 0.0, None, ALU.max)
                psfree[bank] = ("vS", p.n("vS"))
                p.wait("scalar", ("vS", p.n("vS")))
                p.op("scalar", "activation", ka_[:, dt, :], rt_, AF.Square)
            kadone = p.n("sS")
            p.wait("tensor", ("sS", kadone))
            vsD = p.n("vS")
            for dt in range(CC):
                bank = dt % 4
                if psfree[bank] is not None:
                    p.wait("tensor", psfree[bank])
                for cc in range(CC):
                    p.op("tensor", "matmul", pB[bank], w_c[:, cc, dt * 128:(dt + 1) * 128],
                         ka_[:, cc, :], start=(cc == 0), stop=(cc == CC - 1),
                         inc=("mm", 1) if cc == CC - 1 else None)
                if dt >= 2:
                    p.wait("sync", ("vS", vsD + 2 * (dt - 1)))
                elif tc > 0:
                    p.wait("sync", ("vS", vsD))
                p.dma("sync", xres if dt % 2 == 0 else xres2,
                      x1[dt * 128:(dt + 1) * 128, tc * 512:(tc + 1) * 512],
                      ("sxa" if dt % 2 == 0 else "sxb", 16))
                p.wait("vector", ("mm", p.n("mm")),
                       ("sxa" if dt % 2 == 0 else "sxb", p.n("sxa" if dt % 2 == 0 else "sxb")))
                p.op("vector", "tensor_tensor", dbuf, pB[bank], rb_[:, dt, :], ALU.mult)
                p.op("vector", "tensor_tensor", s_f[:, dt, :], dbuf,
                     xres if dt % 2 == 0 else xres2, ALU.add)
                psfree[bank] = ("vS", p.n("vS"))
            p.wait("tensor", ("vS", p.n("vS")))
            p.wait("scalar", ("vS", p.n("vS")))
            emit_ln(p, s_f, s_f, lnb)
            if tc >= 4 or fused:
                oc0 = tc * 512 if fused else (tc - 4) * 512
                p.wait("gpsimd", ("vS", p.n("vS")))
                for dt in range(CC):
                    p.dma("gpsimd", x2[dt * 128:(dt + 1) * 128, oc0:oc0 + 512],
                          s_f[:, dt, :], ("so", 16))
                p.wait("vector", ("so", p.n("so")))
            mm_tc[tc] = p.n("mm")
        p.wait("gpsimd", ("so", p.n("so")))


def build_phase2():
    nc = bass.Bass("TRN2", target_bir_lowering=False)
    ctx = contextlib.ExitStack()
    x1 = din(nc, "x1", [D, T])
    wr = din(nc, "wr", [D, D])
    wk = din(nc, "wk", [D, D])
    wv = din(nc, "wv", [D, D])
    mur = din(nc, "mur", [128, CC])
    muk = din(nc, "muk", [128, CC])
    bval = din(nc, "bval", [128, 1])
    ocf = din(nc, "ocf", [128, 1])
    orf = din(nc, "orf", [1, 128])
    x2 = dout(nc, "x2", [D, HALF])

    x1b = dscr(nc, "x1b", [CC, 128, T + 1])
    xr_s = dscr(nc, "xr_s", [CC, 128, T])
    xk_s = dscr(nc, "xk_s", [CC, 128, T])
    r_s = dscr(nc, "r_s", [CC, 128, T])
    ka_s = dscr(nc, "ka_s", [CC, 128, T])

    with ctx:
        p = Plan(nc, ctx)
        p.sem("ld0", "ga", "gw", "gw2", "sa", "sb", "sr", "sxa", "sxb", "stB", "stC", "stK", "stV", "stV2", "ka", "kb", "kc2", "kd", "va", "vb", "vc", "vd", "so", "mm", "vS", "sS", "gS", "g1a", "g1b", "g2a", "g2b", "sxw")
        emit_phase2(p, nc, x1, wr, wk, wv, mur, muk, bval, ocf, orf, x2, x1b, xr_s, xk_s, r_s, ka_s)
        p.build()
    return nc


def emit_phase3(p, nc, x2, wqm, wkm, wvm, wom, bqc, bkc, bvr, boc, sinq, cosq, sink, cosk, pbias, zbias, trim, ocf, orf, x3, x2b, k_scr, v_scr, fused=False):
    ctx = contextlib.ExitStack()
    with ctx:
        cbuf = sb(ctx, nc, "cbuf", [128, T], BF16)
        wqb = sb(ctx, nc, "wqb", [128, CC, D], BF16)
        wkb = sb(ctx, nc, "wkb", [128, CC, D], BF16)
        wvb = sb(ctx, nc, "wvb", [128, CC, D], BF16)
        wob = sb(ctx, nc, "wob", [128, CC, D], BF16)
        bqb = sb(ctx, nc, "bqb", [128, CC], F32)
        bkb = sb(ctx, nc, "bkb", [128, CC], F32)
        bvb = sb(ctx, nc, "bvb", [1, D], BF16)
        bob = sb(ctx, nc, "bob", [128, CC], F32)
        sqb = sb(ctx, nc, "sqb", [32, HALF], BF16)
        cqb = sb(ctx, nc, "cqb", [32, HALF], BF16)
        skb = sb(ctx, nc, "skb", [32, T], BF16)
        ckb = sb(ctx, nc, "ckb", [32, T], BF16)
        pbb = sb(ctx, nc, "pbb", [128, 1], F32)
        zbb = sb(ctx, nc, "zbb", [128, 1], F32)
        trib = sb(ctx, nc, "trib", [128, 128], BF16)
        orb = sb(ctx, nc, "orb", [1, 128], BF16)
        o64 = sb(ctx, nc, "o64", [1, 64], F32)
        ocfb = sb(ctx, nc, "ocfb", [128, 1], F32)
        orfb = sb(ctx, nc, "orfb", [1, 128], F32)
        xwk = sb(ctx, nc, "xwk", [128, CC, 512], BF16)
        xwk2 = sb(ctx, nc, "xwk2", [128, CC, 512], BF16)
        kraw = sb(ctx, nc, "kraw", [64, 2, 512], BF16)
        krot = sb(ctx, nc, "krot", [64, 2, 512], BF16)
        visb = sb(ctx, nc, "visb", [128, H * 65], BF16)
        visb2 = sb(ctx, nc, "visb2", [128, H * 65], BF16)
        qbuf = sb(ctx, nc, "qbuf", [64, H, 512], BF16)
        kchs = [sb(ctx, nc, f"kch{i}", [64, 4, 128], BF16) for i in range(4)]
        vchs = [sb(ctx, nc, f"vch{i}", [128, 4 * 65], BF16) for i in range(4)]
        prbufs = [sb(ctx, nc, f"prbx{i}", [128, 512], BF16) for i in range(8)]
        rcrow = sb(ctx, nc, "rcrow", [1, 512], F32)
        rcs = sb(ctx, nc, "rcs", [64, 512], BF16)
        comb = sb(ctx, nc, "comb", [128, CC, 512], BF16)
        s_f = sb(ctx, nc, "s_f", [128, CC, 512], F32)
        xres = sb(ctx, nc, "xres", [128, 512], F32)
        xres2 = sb(ctx, nc, "xres2", [128, 512], F32)
        sq = sb(ctx, nc, "sq", [128, 512], F32)
        sq2 = sb(ctx, nc, "sq2", [128, 512], F32)
        rows = [sb(ctx, nc, f"row{i}", [1, 512], F32) for i in range(4)]
        pA = [psum(ctx, nc, f"pA{i}", [128, 512]) for i in range(2)]
        pS = [psum(ctx, nc, f"pS{i}", [128, 512]) for i in range(2)]
        soP = [psum(ctx, nc, f"soP{i}", [65, 512]) for i in range(4)]
        lnb = dict(psS=(soP[0][0:1, :], soP[1][0:1, :]), psB=(pA[0], pA[1]),
                   sq=(sq, sq2), rows=rows, ones_col_f=ocfb, ones_row_f=orfb)

        for dst, src in ((bqb, bqc), (bkb, bkc), (bob, boc), (pbb, pbias), (zbb, zbias),
                         (ocfb, ocf), (orfb, orf)):
            p.dma("sync", dst, src, ("ld0", 16))
        for dst, src in ((bvb, bvr), (sqb, sinq), (cqb, cosq), (skb, sink), (ckb, cosk),
                         (trib, trim), (orb, orf)):
            p.dma("gpsimd", dst, src, ("ga", 16))
        p.gop("memset", o64, 1.0)
        p.op("vector", "memset", visb.rearrange("q (h m) -> q h m", m=65)[:, :, 64:65], 1.0)
        p.op("vector", "memset", visb2.rearrange("q (h m) -> q h m", m=65)[:, :, 64:65], 1.0)
        for cc in range(CC):
            p.dma("gpsimd", cbuf, x2[cc * 128:(cc + 1) * 128, :], ("ga", 16))
            p.wait("gpsimd", ("ga", p.n("ga")))
            p.dma("gpsimd", x2b[cc, :, :], cbuf, ("ga", 16))
            p.wait("gpsimd", ("ga", p.n("ga")))
        gaA = p.n("ga")
        for cc in range(CC):
            p.dma("gpsimd", wqb[:, cc, :], wqm[cc * 128:(cc + 1) * 128, :], ("gw", 16))
            p.dma("gpsimd", wkb[:, cc, :], wkm[cc * 128:(cc + 1) * 128, :], ("gw", 16))
            p.dma("gpsimd", wvb[:, cc, :], wvm[cc * 128:(cc + 1) * 128, :], ("gw", 16))
            p.dma("gpsimd", wob[:, cc, :], wom[cc * 128:(cc + 1) * 128, :], ("gw", 16))
        gwT = p.n("gw")

        def rope(dst, srcb, sint, cost, col0, w):
            for hh in range(2):
                d_ = krot if dst is None else dst
                k1 = srcb[0:32, hh, :]
                ss = sint[:, col0:col0 + w]
                cs = cost[:, col0:col0 + w]
                ta = sq[0:32, 0:w]
                tb = sq2[0:32, 0:w]
                t0 = rcs[0:32, 0:w]
                p.op("vector", "tensor_copy", t0, srcb[32:64, hh, :])
                p.op("vector", "tensor_tensor", ta, cs, k1, ALU.mult)
                p.op("vector", "tensor_tensor", tb, ss, t0, ALU.mult)
                p.op("vector", "tensor_tensor", d_[0:32, hh, :], ta, tb, ALU.subtract)
                p.op("vector", "tensor_tensor", ta, ss, k1, ALU.mult)
                p.op("vector", "tensor_tensor", tb, cs, t0, ALU.mult)
                p.op("vector", "tensor_tensor", ta, ta, tb, ALU.add)
                p.op("vector", "tensor_copy", d_[32:64, hh, :], ta)

        p.wait("sync", ("ga", gaA), ("ld0", p.n("ld0")))
        p.wait("tensor", ("gw", gwT), ("ga", gaA))
        vs_s1 = {}
        for tcx in range(CC):
            bufa = xwk if tcx % 2 == 0 else xwk2
            snm = "sa" if tcx % 2 == 0 else "sb"
            if tcx >= 2:
                p.wait("sync", ("mm", vs_s1[tcx - 2]))
            for cc in range(CC):
                p.dma("sync", bufa[:, cc, :], x2b[cc, :, tcx * 512:(tcx + 1) * 512], (snm, 16))
            p.wait("tensor", (snm, p.n(snm)))
            for dt in range(CC):
                pa = pA[dt % 2]
                p.wait("tensor", ("sS", p.n("sS")))
                for cc in range(CC):
                    p.op("tensor", "matmul", pa, wkb[:, cc, dt * 128:(dt + 1) * 128],
                         bufa[:, cc, :], start=(cc == 0), stop=(cc == CC - 1),
                         inc=("mm", 1) if cc == CC - 1 else None)
                p.wait("scalar", ("mm", p.n("mm")), ("vS", p.n("vS")))
                p.op("scalar", "activation", kraw[0:64, 0, :], pa[0:64, :], AF.Identity,
                     bias=bkb[0:64, dt:dt + 1])
                p.op("scalar", "activation", kraw[0:64, 1, :], pa[64:128, :], AF.Identity,
                     bias=bkb[64:128, dt:dt + 1])
                p.wait("vector", ("sS", p.n("sS")), ("stK", p.n("stK")))
                rope(krot, kraw, skb, ckb, tcx * 512, 512)
                p.wait("gpsimd", ("vS", p.n("vS")))
                for kj in range(4):
                    p.dma("gpsimd",
                          k_scr.rearrange("a q (h n) -> a q h n", n=128)[tcx * 4 + kj, 0:64, 2 * dt:2 * dt + 2, :],
                          krot[:, :, kj * 128:(kj + 1) * 128], ("stK", 16))
            for tt4 in range(4):
                vb_ = visb if tt4 % 2 == 0 else visb2
                vsem = "stV" if tt4 % 2 == 0 else "stV2"
                p.wait("scalar", (vsem, p.n(vsem)))
                for dh2 in range(2):
                    pa = pA[dh2]
                    p.wait("tensor", ("sS", p.n("sS")))
                    for cc in range(CC):
                        p.op("tensor", "matmul", pa, bufa[:, cc, tt4 * 128:(tt4 + 1) * 128],
                             wvb[:, cc, dh2 * 512:(dh2 + 1) * 512],
                             start=(cc == 0), stop=False)
                    p.op("tensor", "matmul", pa, orb, bvb[:, dh2 * 512:(dh2 + 1) * 512],
                         start=False, stop=True, inc=("mm", 1))
                    p.wait("scalar", ("mm", p.n("mm")))
                    vtgt = vb_.rearrange("q (h m) -> q h m", m=65)
                    p.op("scalar", "copy",
                         vtgt[:, dh2 * 8:(dh2 + 1) * 8, 0:64],
                         pa.rearrange("q (h m) -> q h m", m=64))
                p.wait("gpsimd", ("sS", p.n("sS")))
                p.dma("gpsimd", v_scr[tcx * 4 + tt4], vb_, (vsem, 16))
            vs_s1[tcx] = p.n("mm")
        stK_T, stV_T, stV2_T = p.n("stK"), p.n("stV"), p.n("stV2")

        p.wait("sync", ("stK", stK_T), ("stV", stV_T), ("stV2", stV2_T))
        for g in range(4, 8):
            Kg = 4 * g + 4
            snm = "sa" if g % 2 == 0 else "sb"
            bufa = xwk if g % 2 == 0 else xwk2
            if g == 4:
                p.wait("sync", ("mm", p.n("mm")))
            else:
                p.wait("sync", ("mm", mm_gend))
            for cc in range(CC):
                p.dma("sync", bufa[:, cc, :], x2b[cc, :, g * 512:(g + 1) * 512], (snm, 16))
            p.wait("tensor", (snm, p.n(snm)), ("vS", p.n("vS")), ("sS", p.n("sS")))
            for dt in range(CC):
                pa = pA[dt % 2]
                if dt >= 2:
                    p.wait("tensor", ("vS", p.n("vS")))
                for cc in range(CC):
                    p.op("tensor", "matmul", pa, wqb[:, cc, dt * 128:(dt + 1) * 128],
                         bufa[:, cc, :], start=(cc == 0), stop=(cc == CC - 1),
                         inc=("mm", 1) if cc == CC - 1 else None)
                p.wait("scalar", ("mm", p.n("mm")), ("vS", p.n("vS")))
                p.op("scalar", "activation", kraw[0:64, 0, :], pa[0:64, :], AF.Identity,
                     bias=bqb[0:64, dt:dt + 1])
                p.op("scalar", "activation", kraw[0:64, 1, :], pa[64:128, :], AF.Identity,
                     bias=bqb[64:128, dt:dt + 1])
                p.wait("vector", ("sS", p.n("sS")))
                rope(qbuf.rearrange("q (a h) w -> q a h w", a=1)[:, 0, 2 * dt:2 * dt + 2, :],
                     kraw, sqb, cqb, (g - 4) * 512, 512)
            qdone = p.n("vS")
            scfree = [None, None, None, None]
            prfree = {(i, j): None for i in range(4) for j in range(2)}
            pready = {}
            av_mm = {}
            for hb in range(4):
                for kc in range(Kg):
                    knm = ("ka", "kb", "kc2", "kd")[kc % 4]
                    vnm = ("va", "vb", "vc", "vd")[kc % 4]
                    kb_ = kchs[kc % 4]
                    vb_ = vchs[kc % 4]
                    if kc >= 4:
                        p.wait("sync", ("mm", av_mm[kc - 4]))
                    elif hb > 0 or g > 4:
                        p.wait("sync", ("mm", mm_kprev))
                    p.dma("sync", kb_.rearrange("q a n -> q (a n)"),
                          k_scr[kc, 0:64, hb * 512:(hb + 1) * 512], (knm, 16))
                    p.dma("sync", vb_, v_scr[kc, :, hb * 260:(hb + 1) * 260], (vnm, 16))
                    p.wait("tensor", (knm, p.n(knm)), (vnm, p.n(vnm)), ("vS", qdone))
                    dk0 = Kg - 4
                    scb = [pS[0], pS[1], pA[0], pA[1]]
                    for hin in range(4):
                        h = hb * 4 + hin
                        if scfree[hin] is not None:
                            p.wait("tensor", scfree[hin])
                        p.op("tensor", "matmul", scb[hin], kb_[0:64, hin, :], qbuf[0:64, h, :],
                             start=True, stop=True, inc=("mm", 1))
                        mmsc = p.n("mm")
                        if prfree[(hin, kc % 2)] is not None:
                            p.wait("scalar", prfree[(hin, kc % 2)])
                        p.wait("scalar", ("mm", mmsc))
                        prbh = prbufs[hin * 2 + kc % 2]
                        p.op("scalar", "activation", prbh, scb[hin], AF.Exp, scale=0.125,
                             bias=pbb if kc < 16 else zbb)
                        scfree[hin] = ("sS", p.n("sS"))
                        pready[(hin, kc % 2)] = ("sS", p.n("sS"))
                        if kc >= dk0:
                            jd = kc - dk0
                            p.wait("vector", ("sS", p.n("sS")))
                            if jd > 0:
                                p.op("vector", "memset", prbh[:, 0:jd * 128], 0.0)
                            p.op("vector", "tensor_tensor", prbh[:, jd * 128:(jd + 1) * 128],
                                 prbh[:, jd * 128:(jd + 1) * 128], trib, ALU.mult)
                            pready[(hin, kc % 2)] = ("vS", p.n("vS"))
                    if kc > 0:
                        vprev = vchs[(kc - 1) % 4]
                        for hin in range(4):
                            p.wait("tensor", pready[(hin, (kc - 1) % 2)])
                            p.op("tensor", "matmul", soP[hin],
                                 vprev[:, hin * 65:(hin + 1) * 65],
                                 prbufs[hin * 2 + (kc - 1) % 2],
                                 start=(kc - 1 == 0), stop=False, inc=("mm", 1))
                            prfree[(hin, (kc - 1) % 2)] = ("mm", p.n("mm"))
                        av_mm[kc - 1] = p.n("mm")
                    if kc == Kg - 1:
                        vlast = vchs[kc % 4]
                        for hin in range(4):
                            p.wait("tensor", pready[(hin, kc % 2)])
                            p.op("tensor", "matmul", soP[hin],
                                 vlast[:, hin * 65:(hin + 1) * 65],
                                 prbufs[hin * 2 + kc % 2],
                                 start=(kc == 0), stop=True, inc=("mm", 1))
                            prfree[(hin, kc % 2)] = ("mm", p.n("mm"))
                        av_mm[kc] = p.n("mm")
                mm_kprev = p.n("mm")
                p.wait("vector", ("mm", p.n("mm")))
                for hin in range(4):
                    h = hb * 4 + hin
                    p.op("vector", "reciprocal", rcrow, soP[hin][64:65, :])
                    p.wait("tensor", ("vS", p.n("vS")))
                    if scfree[0] is not None:
                        p.wait("tensor", scfree[0])
                    p.op("tensor", "matmul", pS[0][0:64, :], o64, rcrow, start=True, stop=True,
                         inc=("mm", 1))
                    p.wait("scalar", ("mm", p.n("mm")))
                    p.op("scalar", "copy", rcs, pS[0][0:64, :])
                    scfree[0] = ("sS", p.n("sS"))
                    p.wait("vector", ("sS", p.n("sS")))
                    p.op("vector", "tensor_tensor",
                         comb[(h % 2) * 64:(h % 2) * 64 + 64, h // 2, :],
                         soP[hin][0:64, :], rcs, ALU.mult)
                vs_hbend = p.n("vS")
                p.wait("tensor", ("vS", vs_hbend))
            vs_pre = p.n("vS")
            for dt2 in range(CC):
                pa = pA[dt2 % 2]
                if dt2 >= 2:
                    p.wait("tensor", ("vS", vs_pre + dt2 - 1))
                for cc in range(CC):
                    p.op("tensor", "matmul", pa, wob[:, cc, dt2 * 128:(dt2 + 1) * 128],
                         comb[:, cc, :], start=(cc == 0), stop=(cc == CC - 1),
                         inc=("mm", 1) if cc == CC - 1 else None)
                if dt2 >= 2:
                    p.wait("sync", ("vS", vs_pre + dt2 - 1))
                elif g > 4:
                    p.wait("sync", ("vS", vs_pre))
                p.dma("sync", xres if dt2 % 2 == 0 else xres2,
                      x2[dt2 * 128:(dt2 + 1) * 128, g * 512:(g + 1) * 512],
                      ("sxa" if dt2 % 2 == 0 else "sxb", 16))
                p.wait("vector", ("mm", p.n("mm")),
                       ("sxa" if dt2 % 2 == 0 else "sxb", p.n("sxa" if dt2 % 2 == 0 else "sxb")))
                p.op("vector", "scalar_tensor_tensor", s_f[:, dt2, :], pa,
                     bob[:, dt2:dt2 + 1], xres if dt2 % 2 == 0 else xres2, ALU.add, ALU.add)
            p.wait("tensor", ("vS", p.n("vS")))
            p.wait("scalar", ("vS", p.n("vS")))
            emit_ln(p, s_f, s_f, lnb)
            p.wait("gpsimd", ("vS", p.n("vS")))
            for dt in range(CC):
                p.dma("gpsimd", x3[dt * 128:(dt + 1) * 128, (g - 4) * 512:(g - 3) * 512],
                      s_f[:, dt, :], ("so", 16))
            p.wait("vector", ("so", p.n("so")))
            mm_gend = p.n("mm")
        p.wait("gpsimd", ("so", p.n("so")))


def build_phase3():
    nc = bass.Bass("TRN2", target_bir_lowering=False)
    ctx = contextlib.ExitStack()
    x2 = din(nc, "x2", [D, T])
    wqm = din(nc, "wqm", [D, D])
    wkm = din(nc, "wkm", [D, D])
    wvm = din(nc, "wvm", [D, D])
    wom = din(nc, "wom", [D, D])
    bqc = din(nc, "bqc", [128, CC])
    bkc = din(nc, "bkc", [128, CC])
    bvr = din(nc, "bvr", [1, D])
    boc = din(nc, "boc", [128, CC])
    sinq = din(nc, "sinq", [32, HALF])
    cosq = din(nc, "cosq", [32, HALF])
    sink = din(nc, "sink", [32, T])
    cosk = din(nc, "cosk", [32, T])
    pbias = din(nc, "pbias", [128, 1])
    zbias = din(nc, "zbias", [128, 1])
    trim = din(nc, "trim", [128, 128])
    ocf = din(nc, "ocf", [128, 1])
    orf = din(nc, "orf", [1, 128])
    x3 = dout(nc, "x3", [D, HALF])
    x2b = dscr(nc, "x2b", [CC, 128, T])
    k_scr = dscr(nc, "k_scr", [NT, 64, H * 128])
    v_scr = dscr(nc, "v_scr", [NT, 128, H * 65])

    with ctx:
        p = Plan(nc, ctx)
        p.sem("ld0", "ga", "gw", "gw2", "sa", "sb", "sr", "sxa", "sxb", "stB", "stC", "stK", "stV", "stV2", "ka", "kb", "kc2", "kd", "va", "vb", "vc", "vd", "so", "mm", "vS", "sS", "gS", "g1a", "g1b", "g2a", "g2b", "sxw")
        emit_phase3(p, nc, x2, wqm, wkm, wvm, wom, bqc, bkc, bvr, boc, sinq, cosq, sink, cosk, pbias, zbias, trim, ocf, orf, x3, x2b, k_scr, v_scr)
        p.build()
    return nc


def emit_phase4(p, nc, x3, wrt, w1t, b1c, w2t, b2m, ws1t, bs1c, ws2t, bs2c, ident, oneh, ocf, orf, x4, x3b, fused=False):
    ctx = contextlib.ExitStack()
    with ctx:
        cbuf = sb(ctx, nc, "cbuf", [128, HALF], BF16)
        wrtb = sb(ctx, nc, "wrtb", [128, CC, NE], BF16)
        b1cb = sb(ctx, nc, "b1cb", [128, NE * 32], F32)
        bs1cb = sb(ctx, nc, "bs1cb", [128, 32], F32)
        bs2cb = sb(ctx, nc, "bs2cb", [128, CC], F32)
        b2b = sb(ctx, nc, "b2b", [NE, D], BF16)
        idb = sb(ctx, nc, "idb", [128, 128], BF16)
        onehb = sb(ctx, nc, "onehb", [NE, NE * 128], BF16)
        orb = sb(ctx, nc, "orb", [1, 128], BF16)
        ocfb = sb(ctx, nc, "ocfb", [128, 1], F32)
        orfb = sb(ctx, nc, "orfb", [1, 128], F32)
        gatesT = sb(ctx, nc, "gatesT", [NE, HALF], BF16)
        gb = sb(ctx, nc, "gb", [128, 1024], BF16)
        probs = sb(ctx, nc, "probs", [128, NE], F32)
        pm = sb(ctx, nc, "pm", [128, NE], F32)
        keep = sb(ctx, nc, "keep", [128, NE], F32)
        gcol = sb(ctx, nc, "gcol", [128, 1], F32)
        gcol2 = sb(ctx, nc, "gcol2", [128, 1], F32)
        gbf = sb(ctx, nc, "gbf", [128, NE], BF16)
        xw3 = sb(ctx, nc, "xw3", [128, CC, HYT], BF16)
        w1blk = sb(ctx, nc, "w1blk", [128, CC, 128], BF16)
        w1blk2 = sb(ctx, nc, "w1blk2", [128, CC, 128], BF16)
        w2blk = sb(ctx, nc, "w2blk", [128, 32, 128], BF16)
        w2blk2 = sb(ctx, nc, "w2blk2", [128, 32, 128], BF16)
        hE = sb(ctx, nc, "hE", [128, 32, HYT], BF16)
        htmp = sb(ctx, nc, "htmp", [128, 512], BF16)
        htmp2 = sb(ctx, nc, "htmp2", [128, 512], BF16)
        acc = sb(ctx, nc, "acc", [128, CC, HYT], F32)
        s_f = sb(ctx, nc, "s_f", [128, CC, 512], F32)
        xres = sb(ctx, nc, "xres", [128, 512], F32)
        xres2 = sb(ctx, nc, "xres2", [128, 512], F32)
        sq = sb(ctx, nc, "sq", [128, 512], F32)
        sq2 = sb(ctx, nc, "sq2", [128, 512], F32)
        rows = [sb(ctx, nc, f"row{i}", [1, 512], F32) for i in range(4)]
        pA = [psum(ctx, nc, f"pA{i}", [128, 512]) for i in range(4)]
        psS0 = psum(ctx, nc, "psS0", [1, 512])
        psS1 = psum(ctx, nc, "psS1", [1, 512])
        psT = psum(ctx, nc, "psT", [8, 128], BF16)
        lnb = dict(psS=(psS0, psS1), psB=(pA[0], pA[1]), sq=(sq, sq2),
                   rows=rows, ones_col_f=ocfb, ones_row_f=orfb)

        for dst, src in ((b1cb, b1c), (bs1cb, bs1c), (bs2cb, bs2c), (ocfb, ocf), (orfb, orf)):
            p.dma("sync", dst, src, ("ld0", 16))
        p.dma("gpsimd", idb, ident, ("ga", 16))
        p.dma("gpsimd", onehb, oneh, ("ga", 16))
        p.dma("gpsimd", orb, orf, ("ga", 16))
        p.dma("gpsimd", b2b, b2m, ("ga", 16))
        for cc in range(CC):
            p.dma("gpsimd", wrtb[:, cc, :], wrt[cc * 128:(cc + 1) * 128, :], ("ga", 16))
        for cc in range(CC):
            p.dma("gpsimd", cbuf, x3[cc * 128:(cc + 1) * 128, :], ("ga", 16))
            p.wait("gpsimd", ("ga", p.n("ga")))
            p.dma("gpsimd", x3b[cc, :, :], cbuf, ("ga", 16))
            p.wait("gpsimd", ("ga", p.n("ga")))
        gaA = p.n("ga")

        p.wait("sync", ("ga", gaA), ("ld0", p.n("ld0")))
        mm_hyend = 0
        psfree = [None, None, None, None]
        hfree = [None, None]
        for hy in range(NH):
            if hy > 0:
                p.wait("sync", ("mm", mm_hyend))
            for cc in range(CC):
                p.dma("sync", xw3[:, cc, :], x3b[cc, :, hy * HYT:(hy + 1) * HYT], ("sxw", 16))
            ldw = p.n("sxw")
            p.wait("tensor", ("sxw", ldw))
            for tt in range(HYT // 128):
                pr = pA[0][:, 0:NE]
                p.wait("tensor", ("vS", p.n("vS")), ("sS", p.n("sS")))
                if psfree[0] is not None:
                    p.wait("tensor", psfree[0])
                    psfree[0] = None
                for cc in range(CC):
                    p.op("tensor", "matmul", pr, xw3[:, cc, tt * 128:(tt + 1) * 128],
                         wrtb[:, cc, :], start=(cc == 0), stop=(cc == CC - 1),
                         inc=("mm", 1) if cc == CC - 1 else None)
                p.wait("vector", ("mm", p.n("mm")))
                p.op("vector", "tensor_reduce", gcol, pr, AX.X, ALU.max, negate=True)
                p.wait("scalar", ("vS", p.n("vS")))
                p.op("scalar", "activation", probs, pr, AF.Exp, bias=gcol)
                p.wait("vector", ("sS", p.n("sS")))
                p.op("vector", "tensor_reduce", gcol, probs, AX.X, ALU.add)
                p.op("vector", "reciprocal", gcol2, gcol)
                p.op("vector", "tensor_scalar", probs, probs, gcol2, None, ALU.mult)
                p.op("vector", "tensor_reduce", gcol, probs, AX.X, ALU.max)
                p.op("vector", "tensor_scalar", keep, probs, gcol, None, ALU.is_ge)
                p.op("vector", "scalar_tensor_tensor", pm, keep, -2.0, probs, ALU.mult, ALU.add)
                p.op("vector", "tensor_reduce", gcol, pm, AX.X, ALU.max)
                p.op("vector", "tensor_scalar", keep, probs, gcol, None, ALU.is_ge)
                p.op("vector", "tensor_tensor", gbf, probs, keep, ALU.mult)
                p.wait("tensor", ("vS", p.n("vS")))
                p.op("tensor", "transpose", psT, gbf, idb, inc=("mm", 1))
                p.wait("scalar", ("mm", p.n("mm")))
                p.op("scalar", "copy", gatesT[:, hy * HYT + tt * 128: hy * HYT + (tt + 1) * 128], psT)
            gts = p.n("sS")
            for e in range(NE + 1):
                if e < NE:
                    p.wait("tensor", ("sS", gts), ("vS", p.n("vS")), ("sS", p.n("sS")))
                    if psfree[1] is not None:
                        p.wait("tensor", psfree[1])
                    p.op("tensor", "matmul", pA[1], onehb[:, e * 128:(e + 1) * 128],
                         gatesT[0:NE, hy * HYT: hy * HYT + 512],
                         start=True, stop=True, inc=("mm", 1))
                    p.wait("scalar", ("mm", p.n("mm")))
                    p.op("scalar", "copy", gb[:, 0:512], pA[1])
                    p.wait("tensor", ("sS", p.n("sS")))
                    p.op("tensor", "matmul", pA[1], onehb[:, e * 128:(e + 1) * 128],
                         gatesT[0:NE, hy * HYT + 512: hy * HYT + 1024],
                         start=True, stop=True, inc=("mm", 1))
                    p.wait("scalar", ("mm", p.n("mm")))
                    p.op("scalar", "copy", gb[:, 512:1024], pA[1])
                    psfree[1] = ("sS", p.n("sS"))
                mmA0 = p.n("mm")
                if e > 0:
                    p.wait("scalar", ("mm", mm_Bend))
                    p.wait("vector", ("mm", mm_Bend))
                gidx = 0
                for ft in range(32):
                    wb_ = w1blk if ft % 2 == 0 else w1blk2
                    gnm = "g1a" if ft % 2 == 0 else "g1b"
                    if ft >= 2:
                        p.wait("gpsimd", ("mm", mmA0 + (ft - 1) * 2))
                    elif e > 0 or hy > 0:
                        p.wait("gpsimd", ("mm", mmA0))
                    src = (w1t[e, ft] if e < NE else ws1t[ft]).rearrange("q (cc n) -> q cc n", n=128)
                    p.dma("gpsimd", wb_, src, (gnm, 16))
                    p.wait("tensor", (gnm, p.n(gnm)))
                    for th in range(2):
                        bank = gidx % 4
                        ph = pA[bank]
                        if psfree[bank] is not None:
                            p.wait("tensor", psfree[bank])
                        for cc in range(CC):
                            p.op("tensor", "matmul", ph, wb_[:, cc, :],
                                 xw3[:, cc, th * 512:(th + 1) * 512],
                                 start=(cc == 0), stop=(cc == CC - 1),
                                 inc=("mm", 1) if cc == CC - 1 else None)
                        bias = b1cb[:, e * 32 + ft: e * 32 + ft + 1] if e < NE else bs1cb[:, ft:ft + 1]
                        p.wait("scalar", ("mm", p.n("mm")))
                        if e < NE:
                            hb_ = htmp if gidx % 2 == 0 else htmp2
                            if hfree[gidx % 2] is not None:
                                p.wait("scalar", hfree[gidx % 2])
                            p.op("scalar", "activation", hb_, ph, AF.Gelu_apprx_tanh, bias=bias)
                            psfree[bank] = ("sS", p.n("sS"))
                            p.wait("vector", ("sS", p.n("sS")))
                            p.op("vector", "tensor_tensor", hE[:, ft, th * 512:(th + 1) * 512],
                                 hb_, gb[:, th * 512:(th + 1) * 512], ALU.mult)
                            hfree[gidx % 2] = ("vS", p.n("vS"))
                        else:
                            p.op("scalar", "activation", hE[:, ft, th * 512:(th + 1) * 512],
                                 ph, AF.Gelu_apprx_tanh, bias=bias)
                            psfree[bank] = ("sS", p.n("sS"))
                        gidx += 1
                hdone_v, hdone_s = p.n("vS"), p.n("sS")
                mmB0 = p.n("mm")
                gidx = 0
                for dt in range(CC):
                    wb2 = w2blk if dt % 2 == 0 else w2blk2
                    gnm = "g2a" if dt % 2 == 0 else "g2b"
                    if dt >= 2:
                        p.wait("gpsimd", ("mm", mmB0 + (dt - 1) * 2))
                    elif e > 0 or hy > 0:
                        p.wait("gpsimd", ("mm", mmB0))
                    src = w2t[e, dt] if e < NE else ws2t[dt]
                    p.dma("gpsimd", wb2, src, (gnm, 16))
                    p.wait("tensor", (gnm, p.n(gnm)), ("vS", hdone_v), ("sS", hdone_s))
                    for th in range(2):
                        bank = gidx % 4
                        ph = pA[bank]
                        gidx += 1
                        if psfree[bank] is not None:
                            p.wait("tensor", psfree[bank])
                        for ft in range(32):
                            p.op("tensor", "matmul", ph, wb2[:, ft, :],
                                 hE[:, ft, th * 512:(th + 1) * 512],
                                 start=(ft == 0), stop=(e != 0 and ft == 31),
                                 inc=("mm", 1) if (e != 0 and ft == 31) else None)
                        if e == 0:
                            p.op("tensor", "matmul", ph, b2b[0:NE, dt * 128:(dt + 1) * 128],
                                 gatesT[0:NE, hy * HYT + th * 512: hy * HYT + (th + 1) * 512],
                                 start=False, stop=True, inc=("mm", 1))
                        p.wait("vector", ("mm", p.n("mm")))
                        asl = acc[:, dt, th * 512:(th + 1) * 512]
                        if e == 0:
                            p.op("vector", "tensor_copy", asl, ph)
                        elif e < NE:
                            p.op("vector", "tensor_tensor", asl, asl, ph, ALU.add)
                        else:
                            p.op("vector", "scalar_tensor_tensor", sq, ph, bs2cb[:, dt:dt + 1], asl,
                                 ALU.add, ALU.add)
                            p.op("vector", "tensor_copy", asl, sq)
                        psfree[bank] = ("vS", p.n("vS"))
                mm_Bend = p.n("mm")
            mm_hyend = p.n("mm")
            for j in range(2):
                col0 = hy * HYT + j * 512
                vs_pre = p.n("vS")
                for dt in range(CC):
                    if dt >= 2:
                        p.wait("sync", ("vS", vs_pre + dt - 1))
                    elif hy > 0 or j > 0:
                        p.wait("sync", ("vS", vs_pre))
                    p.dma("sync", xres if dt % 2 == 0 else xres2,
                          x3[dt * 128:(dt + 1) * 128, col0:col0 + 512],
                          ("sxa" if dt % 2 == 0 else "sxb", 16))
                    p.wait("vector", ("sxa" if dt % 2 == 0 else "sxb",
                                      p.n("sxa" if dt % 2 == 0 else "sxb")))
                    p.op("vector", "scalar_tensor_tensor", s_f[:, dt, :],
                         acc[:, dt, j * 512:(j + 1) * 512], MOE_SCALE,
                         xres if dt % 2 == 0 else xres2, ALU.mult, ALU.add)
                p.wait("tensor", ("vS", p.n("vS")))
                p.wait("scalar", ("vS", p.n("vS")))
                emit_ln(p, s_f, s_f, lnb)
                p.wait("gpsimd", ("vS", p.n("vS")))
                for dt in range(CC):
                    p.dma("gpsimd", x4[dt * 128:(dt + 1) * 128, col0:col0 + 512],
                          s_f[:, dt, :], ("so", 16))
                p.wait("vector", ("so", p.n("so")))
                del vs_pre
        p.wait("gpsimd", ("so", p.n("so")))


def build_phase4():
    nc = bass.Bass("TRN2", target_bir_lowering=False)
    ctx = contextlib.ExitStack()
    x3 = din(nc, "x3", [D, HALF])
    wrt = din(nc, "wrt", [D, NE])
    w1t = din(nc, "w1t", [NE, 32, 128, D])
    b1c = din(nc, "b1c", [128, NE * 32])
    w2t = din(nc, "w2t", [NE, CC, 128, 32, 128])
    b2m = din(nc, "b2m", [NE, D])
    ws1t = din(nc, "ws1t", [32, 128, D])
    bs1c = din(nc, "bs1c", [128, 32])
    ws2t = din(nc, "ws2t", [CC, 128, 32, 128])
    bs2c = din(nc, "bs2c", [128, CC])
    ident = din(nc, "ident", [128, 128])
    oneh = din(nc, "oneh", [NE, NE * 128])
    ocf = din(nc, "ocf", [128, 1])
    orf = din(nc, "orf", [1, 128])
    x4 = dout(nc, "x4", [D, HALF])
    x3b = dscr(nc, "x3b", [CC, 128, HALF])

    with ctx:
        p = Plan(nc, ctx)
        p.sem("ld0", "ga", "gw", "gw2", "sa", "sb", "sr", "sxa", "sxb", "stB", "stC", "stK", "stV", "stV2", "ka", "kb", "kc2", "kd", "va", "vb", "vc", "vd", "so", "mm", "vS", "sS", "gS", "g1a", "g1b", "g2a", "g2b", "sxw")
        emit_phase4(p, nc, x3, wrt, w1t, b1c, w2t, b2m, ws1t, bs1c, ws2t, bs2c, ident, oneh, ocf, orf, x4, x3b)
        p.build()
    return nc


def emit_phase1(p, nc, xin, wqkv, qbc, kbc, vbr, wquad, memv, memvbc, memg, memgbr, mscr, imix, declog, wout, obc, delta, negm, trow, dcol, signs, ident, oneh32, bval, pbias, zbias, trim, ocf, orf, x1o, xb, k_scr, v_scr, fused=False):
    ctx = contextlib.ExitStack()
    with ctx:
        cbuf = sb(ctx, nc, "cbuf", [128, T], BF16)
        zcol = sb(ctx, nc, "zcol", [128, 1], BF16)
        wkv = sb(ctx, nc, "wkv", [128, CC, 2 * D], BF16)
        wquadb = sb(ctx, nc, "wquadb", [128, CC, 256], BF16)
        memvb = sb(ctx, nc, "memvb", [128, CC, D], BF16)
        memgb = sb(ctx, nc, "memgb", [128, CC, H], BF16)
        woutb = sb(ctx, nc, "woutb", [128, CC, D], BF16)
        qbb = sb(ctx, nc, "qbb", [128, CC], F32)
        kbb = sb(ctx, nc, "kbb", [128, CC], F32)
        vbb = sb(ctx, nc, "vbb", [1, D], BF16)
        mvbb = sb(ctx, nc, "mvbb", [128, CC], F32)
        mgbb = sb(ctx, nc, "mgbb", [1, H], BF16)
        obb = sb(ctx, nc, "obb", [128, CC], F32)
        mscb = sb(ctx, nc, "mscb", [1, H], F32)
        imixb = sb(ctx, nc, "imixb", [1, 1], F32)
        declb = sb(ctx, nc, "declb", [1, H], F32)
        deltab = sb(ctx, nc, "deltab", [128, 128], F32)
        negmb = sb(ctx, nc, "negmb", [128, 128], F32)
        trowb = sb(ctx, nc, "trowb", [1, 128], F32)
        dcolb = sb(ctx, nc, "dcolb", [128, 1], F32)
        signsb = sb(ctx, nc, "signsb", [128, 96], F32)
        idb = sb(ctx, nc, "idb", [128, 128], BF16)
        oh32b = sb(ctx, nc, "oh32b", [32, 32 * 128], BF16)
        bvalb = sb(ctx, nc, "bvalb", [128, 1], F32)
        pbb = sb(ctx, nc, "pbb", [128, 1], F32)
        zbb = sb(ctx, nc, "zbb", [128, 1], F32)
        trib = sb(ctx, nc, "trib", [128, 128], BF16)
        orb = sb(ctx, nc, "orb", [1, 128], BF16)
        o64 = sb(ctx, nc, "o64", [1, 64], F32)
        ocfb = sb(ctx, nc, "ocfb", [128, 1], F32)
        orfb = sb(ctx, nc, "orfb", [1, 128], F32)
        Ah = sb(ctx, nc, "Ah", [128, H * 128], BF16)
        dvh = sb(ctx, nc, "dvh", [1, H * 128], BF16)
        drh = sb(ctx, nc, "drh", [128, H], BF16)
        d128 = sb(ctx, nc, "d128", [1, H], F32)
        lndr = sb(ctx, nc, "lndr", [1, H], F32)
        lndc = sb(ctx, nc, "lndc", [128, H], F32)
        alphc = sb(ctx, nc, "alphc", [128, 1], F32)
        mscbc = sb(ctx, nc, "mscbc", [128, H], F32)
        carry = sb(ctx, nc, "carry", [1, 576], BF16)
        q4 = sb(ctx, nc, "q4", [128, 256], F32)
        wl = sb(ctx, nc, "wl", [128, 96], F32)
        rd = sb(ctx, nc, "rdb", [128, 96], F32)
        jw = sb(ctx, nc, "jw", [128, 96], F32)
        t96a = sb(ctx, nc, "t96a", [128, 96], F32)
        t96b = sb(ctx, nc, "t96b", [128, 96], F32)
        t16a = sb(ctx, nc, "t16a", [128, 16], F32)
        t16b = sb(ctx, nc, "t16b", [128, 16], F32)
        t16c = sb(ctx, nc, "t16c", [128, 16], F32)
        outer = sb(ctx, nc, "outer", [128, 576], BF16)
        msb = sb(ctx, nc, "msb", [128, 576], F32)
        gated = sb(ctx, nc, "gated", [128, NT], F32)
        gT = sb(ctx, nc, "gT", [32, 128], BF16)
        gb = sb(ctx, nc, "gb", [128, 512], BF16)
        mvs = sb(ctx, nc, "mvs", [128, 512], BF16)
        xmw = sb(ctx, nc, "xmw", [128, CC, 129], BF16)
        xmw2 = sb(ctx, nc, "xmw2", [128, CC, 129], BF16)
        xwk = sb(ctx, nc, "xwk", [128, CC, 513], BF16)
        xwk2 = sb(ctx, nc, "xwk2", [128, CC, 513], BF16)
        kraw = sb(ctx, nc, "kraw", [64, 2, 512], BF16)
        visb = sb(ctx, nc, "visb", [128, H * 65], BF16)
        visb2 = sb(ctx, nc, "visb2", [128, H * 65], BF16)
        qbuf = sb(ctx, nc, "qbuf", [64, H, 512], BF16)
        kchs = [sb(ctx, nc, f"kch{i}", [64, 4, 128], BF16) for i in range(4)]
        vchs = [sb(ctx, nc, f"vch{i}", [128, 4 * 65], BF16) for i in range(4)]
        prbufs = [sb(ctx, nc, f"prbx{i}", [128, 512], BF16) for i in range(8)]
        rcrow = sb(ctx, nc, "rcrow", [1, 512], F32)
        rcs = sb(ctx, nc, "rcs", [64, 512], BF16)
        comb = sb(ctx, nc, "comb", [128, CC, 512], BF16)
        s_f = sb(ctx, nc, "s_f", [128, CC, 512], F32)
        xres = sb(ctx, nc, "xres", [128, 512], F32)
        xres2 = sb(ctx, nc, "xres2", [128, 512], F32)
        sq = sb(ctx, nc, "sq", [128, 512], F32)
        sq2 = sb(ctx, nc, "sq2", [128, 512], F32)
        rows = [sb(ctx, nc, f"row{i}", [1, 512], F32) for i in range(4)]
        pA = [psum(ctx, nc, f"pA{i}", [128, 512]) for i in range(2)]
        pS = [psum(ctx, nc, f"pS{i}", [128, 512]) for i in range(2)]
        soP = [psum(ctx, nc, f"soP{i}", [65, 512]) for i in range(4)]
        lnb = dict(psS=(soP[0][0:1, :], soP[1][0:1, :]), psB=(pA[0], pA[1]),
                   sq=(sq, sq2), rows=rows, ones_col_f=ocfb, ones_row_f=orfb)

        for dst, src in ((qbb, qbc), (kbb, kbc), (obb, obc), (mvbb, memvbc), (mscb, mscr), (imixb, imix),
                         (declb, declog), (deltab, delta), (negmb, negm), (trowb, trow),
                         (dcolb, dcol), (signsb, signs), (bvalb, bval), (pbb, pbias),
                         (zbb, zbias), (ocfb, ocf), (orfb, orf)):
            p.dma("sync", dst, src, ("ld0", 16))
        for dst, src in ((vbb, vbr), (mgbb, memgbr), (idb, ident),
                         (oh32b, oneh32), (trib, trim), (orb, orf)):
            p.dma("gpsimd", dst, src, ("ga", 16))
        p.gop("memset", zcol, 0.0)
        p.gop("memset", o64, 1.0)
        p.op("vector", "memset", visb.rearrange("q (h m) -> q h m", m=65)[:, :, 64:65], 1.0)
        p.op("vector", "memset", visb2.rearrange("q (h m) -> q h m", m=65)[:, :, 64:65], 1.0)
        p.op("vector", "memset", carry, 0.0)
        for cc in range(CC):
            p.dma("gpsimd", cbuf, xin[cc * 128:(cc + 1) * 128, :], ("ga", 16))
            p.wait("gpsimd", ("ga", p.n("ga")))
            p.dma("gpsimd", xb[cc, :, 1:T + 1], cbuf, ("ga", 16))
            p.dma("gpsimd", xb[cc, :, 0:1], zcol, ("ga", 16), allow_slow_non_contiguous=True)
            p.wait("gpsimd", ("ga", p.n("ga")))
        gaA = p.n("ga")
        for cc in range(CC):
            p.dma("gpsimd", wkv[:, cc, :], wqkv[cc * 128:(cc + 1) * 128, D:3 * D], ("gw", 16))
            p.dma("gpsimd", wquadb[:, cc, :], wquad[cc * 128:(cc + 1) * 128, :], ("gw", 16))
            p.dma("gpsimd", memvb[:, cc, :], memv[cc * 128:(cc + 1) * 128, :], ("gw", 16))
            p.dma("gpsimd", memgb[:, cc, :], memg[cc * 128:(cc + 1) * 128, :], ("gw", 16))
            p.dma("gpsimd", woutb[:, cc, :], wout[cc * 128:(cc + 1) * 128, :], ("gw", 16))
        gwT = p.n("gw")

        p.wait("scalar", ("ld0", p.n("ld0")))
        p.op("scalar", "activation", lndr, declb, AF.Sigmoid)
        p.op("scalar", "activation", d128, imixb.broadcast_to([1, H]), AF.Sigmoid)
        p.wait("vector", ("sS", p.n("sS")))
        p.op("vector", "tensor_scalar", lndr, lndr, 1e-30, None, ALU.max)
        p.wait("scalar", ("vS", p.n("vS")))
        p.op("scalar", "activation", lndr, lndr, AF.Ln)
        p.wait("tensor", ("sS", p.n("sS")), ("ga", gaA))
        p.op("tensor", "matmul", pS[0][:, 0:H], orfb, lndr, start=True, stop=True)
        p.op("tensor", "matmul", pS[1][:, 0:1], orfb, d128[:, 0:1], start=True, stop=True)
        p.op("tensor", "matmul", pS[1][:, 1:1 + H], orfb, mscb, start=True, stop=True, inc=("mm", 1))
        p.wait("scalar", ("mm", p.n("mm")))
        p.op("scalar", "copy", lndc, pS[0][:, 0:H])
        p.op("scalar", "copy", alphc, pS[1][:, 0:1])
        p.op("scalar", "copy", mscbc, pS[1][:, 1:1 + H])
        p.wait("vector", ("sS", p.n("sS")))
        p.op("vector", "tensor_scalar", d128, lndr, 128.0, None, ALU.mult)
        p.wait("scalar", ("vS", p.n("vS")))
        p.op("scalar", "activation", d128, d128, AF.Exp)
        for h in range(H):
            p.op("vector", "tensor_scalar", Ah[:, h * 128:(h + 1) * 128], deltab,
                 lndc[:, h:h + 1], None, ALU.mult)
            p.op("vector", "tensor_tensor", Ah[:, h * 128:(h + 1) * 128],
                 Ah[:, h * 128:(h + 1) * 128], negmb, ALU.add)
            p.wait("scalar", ("vS", p.n("vS")))
            p.op("scalar", "activation", Ah[:, h * 128:(h + 1) * 128],
                 Ah[:, h * 128:(h + 1) * 128], AF.Exp)
            p.op("scalar", "activation", dvh[:, h * 128:(h + 1) * 128], trowb, AF.Exp,
                 scale=lndr[:, h:h + 1])
            p.op("scalar", "activation", drh[:, h:h + 1], dcolb, AF.Exp,
                 scale=lndc[:, h:h + 1])
        dec_done_s, dec_done_v = p.n("sS"), p.n("vS")

        p.wait("sync", ("ga", gaA))
        I6 = [0, 0, 0, 1, 1, 2]
        J6 = [1, 2, 3, 2, 3, 3]
        mm_mend = {}
        for ct in range(NT):
            bufm = xmw if ct % 2 == 0 else xmw2
            snm = "sa" if ct % 2 == 0 else "sb"
            if ct >= 2:
                p.wait("sync", ("mm", mm_mend[ct - 2]))
            for cc in range(CC):
                p.dma("sync", bufm[:, cc, :], xb[cc, :, ct * 128:ct * 128 + 129], (snm, 16))
            p.wait("tensor", (snm, p.n(snm)), ("gw", gwT))
            pa = pA[ct % 2]
            if ct >= 2:
                p.wait("tensor", ("sS", q4_done[ct - 2] + 1))
            for cc in range(CC):
                p.op("tensor", "matmul", pa[:, 0:64], bufm[:, cc, 0:128],
                     wquadb[:, cc, 0:64], start=(cc == 0), stop=(cc == CC - 1))
            for cc in range(CC):
                p.op("tensor", "matmul", pa[:, 64:256], bufm[:, cc, 1:129],
                     wquadb[:, cc, 64:256], start=(cc == 0), stop=(cc == CC - 1),
                     inc=("mm", 1) if cc == CC - 1 else None)
            if ct == 16:
                p.wait("vector", ("mm", p.n("mm")))
                p.op("vector", "tensor_scalar", pa[0:1, 0:64], pa[0:1, 0:64],
                     bvalb[0:1, :], None, ALU.mult)
                p.wait("scalar", ("vS", p.n("vS")))
            p.wait("scalar", ("mm", p.n("mm")))
            p.op("scalar", "copy", q4, pa[:, 0:256])
            q4_done = q4_done if ct > 0 else {}
            q4_done[ct] = p.n("sS")
            p.wait("tensor", ("sS", p.n("sS")))
            for cc in range(CC):
                p.op("tensor", "matmul", pa[:, 256:256 + H], bufm[:, cc, 1:129],
                     memgb[:, cc, :], start=(cc == 0), stop=False)
            p.op("tensor", "matmul", pa[:, 256:256 + H], orb, mgbb,
                 start=False, stop=True, inc=("mm", 1))
            p.wait("scalar", ("mm", p.n("mm")), ("vS", p.n("vS")))
            p.op("scalar", "activation", t16c, pa[:, 256:256 + H], AF.Sigmoid)
            p.wait("vector", ("sS", p.n("sS")))
            for (dst, o1, o2) in ((wl, 0, 64), (rd, 128, 192)):
                q4v = q4.rearrange("q (g h i) -> q g h i", g=4, i=4)
                g1, g2 = o1 // 64, o2 // 64
                dstv = dst.rearrange("q (h k) -> q h k", k=6)
                for kk in range(6):
                    i, j = PAIRS[kk]
                    p.op("vector", "tensor_tensor", t16a, q4v[:, g1, :, i], q4v[:, g2, :, j], ALU.mult)
                    p.op("vector", "tensor_tensor", t16b, q4v[:, g1, :, j], q4v[:, g2, :, i], ALU.mult)
                    p.op("vector", "tensor_tensor", dstv[:, :, kk], t16a, t16b, ALU.subtract)
                p.op("vector", "tensor_tensor", t96a, dst, dst, ALU.mult)
                t96v = t96a.rearrange("q (h k) -> q h k", k=6)
                p.op("vector", "tensor_tensor", t16a, t96v[:, :, 0], t96v[:, :, 1], ALU.add)
                for kk in range(2, 6):
                    p.op("vector", "tensor_tensor", t16a, t16a, t96v[:, :, kk], ALU.add)
                p.wait("scalar", ("vS", p.n("vS")))
                p.op("scalar", "activation", t16b, t16a, AF.Sqrt)
                p.wait("vector", ("sS", p.n("sS")))
                p.op("vector", "tensor_scalar", t16b, t16b, 1e-12, None, ALU.max)
                p.op("vector", "reciprocal", t16a, t16b)
                for kk in range(6):
                    p.op("vector", "tensor_tensor", dstv[:, :, kk], dstv[:, :, kk], t16a, ALU.mult)
            wlv = wl.rearrange("q (h k) -> q h k", k=6)
            p.op("vector", "tensor_tensor", jw.rearrange("q (h k) -> q h k", k=6),
                 wlv[:, :, ::-1], signsb.rearrange("q (h k) -> q h k", k=6), ALU.mult)
            jv = jw.rearrange("q (h i) -> q h i", i=6)
            p.op("vector", "tensor_tensor", outer.rearrange("q (h i j) -> q h i j", i=6, j=6),
                 jv.unsqueeze(3).broadcast_to([128, 16, 6, 6]),
                 jv.unsqueeze(2).broadcast_to([128, 16, 6, 6]), ALU.mult)
            p.wait("tensor", ("vS", p.n("vS")), ("sS", dec_done_s))
            if ct > 0:
                p.wait("tensor", ("vS", carry_v[ct - 1]))
            for h in range(H):
                pM = pS[h // 8]
                p.op("tensor", "matmul", pM[:, (h % 8) * 36:(h % 8) * 36 + 36],
                     Ah[:, h * 128:(h + 1) * 128], outer[:, h * 36:(h + 1) * 36],
                     start=True, stop=False)
                p.op("tensor", "matmul", pM[:, (h % 8) * 36:(h % 8) * 36 + 36],
                     dvh[0:1, h * 128:(h + 1) * 128], carry[:, h * 36:(h + 1) * 36],
                     start=False, stop=True)
            for h in range(H):
                cP = soP[2 + h // 8]
                p.op("tensor", "matmul", cP[0:1, (h % 8) * 36:(h % 8) * 36 + 36],
                     drh[:, h:h + 1], outer[:, h * 36:(h + 1) * 36],
                     start=True, stop=True, inc=("mm", 1) if h == H - 1 else None)
            p.wait("vector", ("mm", p.n("mm")))
            for h in range(H):
                cP = soP[2 + h // 8]
                p.op("vector", "scalar_tensor_tensor", carry[:, h * 36:(h + 1) * 36],
                     carry[:, h * 36:(h + 1) * 36], d128[:, h:h + 1],
                     cP[0:1, (h % 8) * 36:(h % 8) * 36 + 36], ALU.mult, ALU.add)
            carry_v = carry_v if ct > 0 else {}
            carry_v[ct] = p.n("vS")
            p.wait("scalar", ("vS", p.n("vS")))
            p.op("scalar", "copy", msb[:, 0:288], pS[0][:, 0:288])
            p.op("scalar", "copy", msb[:, 288:576], pS[1][:, 0:288])
            p.wait("vector", ("sS", p.n("sS")))
            rv = rd.rearrange("q (h i) -> q h i", i=6)
            mv_ = msb.rearrange("q (g h i j) -> q g h i j", g=2, h=8, i=6)
            rv2 = rd.rearrange("q (g h i) -> q g h i", g=2, h=8)
            rmv = t96a.rearrange("q (g h j) -> q g h j", g=2, h=8)
            for i in range(6):
                p.op("vector", "tensor_tensor", t96b.rearrange("q (g h j) -> q g h j", g=2, h=8),
                     rv2[:, :, :, i].unsqueeze(3).broadcast_to([128, 2, 8, 6]),
                     mv_[:, :, :, i, :], ALU.mult)
                if i == 0:
                    p.op("vector", "tensor_copy", t96a, t96b)
                else:
                    p.op("vector", "tensor_tensor", t96a, t96a, t96b, ALU.add)
            p.op("vector", "tensor_tensor", t96b, t96a, rd, ALU.mult)
            tv = t96b.rearrange("q (h k) -> q h k", k=6)
            p.op("vector", "tensor_tensor", t16a, tv[:, :, 0], tv[:, :, 1], ALU.add)
            for kk in range(2, 6):
                p.op("vector", "tensor_tensor", t16a, t16a, tv[:, :, kk], ALU.add)
            p.op("vector", "tensor_tensor", t96b, t96a, t96a, ALU.mult)
            p.op("vector", "tensor_tensor", t16b, tv[:, :, 0], tv[:, :, 1], ALU.add)
            for kk in range(2, 6):
                p.op("vector", "tensor_tensor", t16b, t16b, tv[:, :, kk], ALU.add)
            p.op("vector", "tensor_tensor", t16b, t16b, t16a, ALU.subtract)
            p.op("vector", "scalar_tensor_tensor", t16a, t16b, alphc, t16a, ALU.mult, ALU.add)
            p.op("vector", "tensor_tensor", t16a, t16a, mscbc, ALU.mult)
            p.wait("scalar", ("vS", p.n("vS")))
            p.op("scalar", "activation", t16b, t16a, AF.Sigmoid)
            p.wait("vector", ("sS", p.n("sS")))
            p.op("vector", "tensor_tensor", t16a, t16b, t16c, ALU.mult)
            p.op("vector", "tensor_reduce", gated[:, ct:ct + 1], t16a, AX.X, ALU.add)
            p.op("vector", "tensor_scalar", gated[:, ct:ct + 1], gated[:, ct:ct + 1],
                 1.0 / H, None, ALU.mult)
            mm_mend[ct] = p.n("mm")
        p.wait("vector", ("vS", p.n("vS")))
        p.op("vector", "tensor_copy", cbuf[:, 0:NT], gated)
        p.wait("tensor", ("vS", p.n("vS")), ("sS", p.n("sS")))
        p.op("tensor", "transpose", pS[0][0:32, 0:128].bitcast(BF16)[:, 0:128], cbuf[:, 0:NT], idb, inc=("mm", 1))
        p.wait("scalar", ("mm", p.n("mm")))
        p.op("scalar", "copy", gT[:, 0:128], pS[0][0:32, 0:128].bitcast(BF16)[:, 0:128])
        gT_done = p.n("sS")

        vs_s1 = {}
        for tcx in range(CC):
            bufa = xwk if tcx % 2 == 0 else xwk2
            snm = "sa" if tcx % 2 == 0 else "sb"
            if tcx >= 2:
                p.wait("sync", ("mm", vs_s1[tcx - 2]))
            else:
                p.wait("sync", ("mm", p.n("mm")))
            for cc in range(CC):
                p.dma("sync", bufa[:, cc, 0:512], xb[cc, :, tcx * 512 + 1:(tcx + 1) * 512 + 1], (snm, 16))
            p.wait("tensor", (snm, p.n(snm)))
            if tcx == 0:
                p.wait("tensor", ("vS", p.n("vS")), ("sS", p.n("sS")))
            for dt in range(CC):
                pa = pA[dt % 2]
                p.wait("tensor", ("sS", p.n("sS")))
                for cc in range(CC):
                    p.op("tensor", "matmul", pa, wkv[:, cc, dt * 128:(dt + 1) * 128],
                         bufa[:, cc, 0:512], start=(cc == 0), stop=(cc == CC - 1),
                         inc=("mm", 1) if cc == CC - 1 else None)
                p.wait("scalar", ("mm", p.n("mm")), ("stK", p.n("stK")))
                p.op("scalar", "activation", kraw[0:64, 0, :], pa[0:64, :], AF.Identity,
                     bias=kbb[0:64, dt:dt + 1])
                p.op("scalar", "activation", kraw[0:64, 1, :], pa[64:128, :], AF.Identity,
                     bias=kbb[64:128, dt:dt + 1])
                p.wait("gpsimd", ("sS", p.n("sS")))
                for kj in range(4):
                    p.dma("gpsimd",
                          k_scr.rearrange("a q (h n) -> a q h n", n=128)[tcx * 4 + kj, 0:64, 2 * dt:2 * dt + 2, :],
                          kraw[:, :, kj * 128:(kj + 1) * 128], ("stK", 16))
            for tt4 in range(4):
                vb_ = visb if tt4 % 2 == 0 else visb2
                vsem = "stV" if tt4 % 2 == 0 else "stV2"
                p.wait("scalar", (vsem, p.n(vsem)))
                for dh2 in range(2):
                    pa = pA[dh2]
                    p.wait("tensor", ("sS", p.n("sS")))
                    for cc in range(CC):
                        p.op("tensor", "matmul", pa, bufa[:, cc, tt4 * 128:(tt4 + 1) * 128],
                             wkv[:, cc, D + dh2 * 512:D + (dh2 + 1) * 512],
                             start=(cc == 0), stop=False)
                    p.op("tensor", "matmul", pa, orb, vbb[:, dh2 * 512:(dh2 + 1) * 512],
                         start=False, stop=True, inc=("mm", 1))
                    p.wait("scalar", ("mm", p.n("mm")))
                    vtgt = vb_.rearrange("q (h m) -> q h m", m=65)
                    p.op("scalar", "copy",
                         vtgt[:, dh2 * 8:(dh2 + 1) * 8, 0:64],
                         pa.rearrange("q (h m) -> q h m", m=64))
                p.wait("gpsimd", ("sS", p.n("sS")))
                p.dma("gpsimd", v_scr[tcx * 4 + tt4], vb_, (vsem, 16))
            vs_s1[tcx] = p.n("mm")
        stK_T, stV_T, stV2_T = p.n("stK"), p.n("stV"), p.n("stV2")
        p.waitn("gpsimd", "mm")
        for cc in range(CC):
            p.dma("gpsimd", wkv[:, cc, 0:D], wqkv[cc * 128:(cc + 1) * 128, 0:D], ("gw2", 16))
        gw2T = p.n("gw2")

        p.wait("sync", ("stK", stK_T), ("stV", stV_T), ("stV2", stV2_T))
        p.wait("tensor", ("gw2", gw2T))
        for g in range(8):
            Kg = 4 * g + 4
            snm = "sa" if g % 2 == 0 else "sb"
            bufa = xwk if g % 2 == 0 else xwk2
            if g == 0:
                p.wait("sync", ("mm", p.n("mm")))
            else:
                p.wait("sync", ("mm", mm_gend))
            for cc in range(CC):
                p.dma("sync", bufa[:, cc, 0:512], xb[cc, :, g * 512 + 1:(g + 1) * 512 + 1], (snm, 16))
            p.wait("tensor", (snm, p.n(snm)), ("vS", p.n("vS")), ("sS", p.n("sS")))
            for dt in range(CC):
                pa = pA[dt % 2]
                if dt >= 2:
                    p.wait("tensor", ("sS", p.n("sS")))
                for cc in range(CC):
                    p.op("tensor", "matmul", pa, wkv[:, cc, dt * 128:(dt + 1) * 128],
                         bufa[:, cc, 0:512], start=(cc == 0), stop=(cc == CC - 1),
                         inc=("mm", 1) if cc == CC - 1 else None)
                p.wait("scalar", ("mm", p.n("mm")))
                p.op("scalar", "activation", qbuf[0:64, 2 * dt, :], pa[0:64, :], AF.Identity,
                     bias=qbb[0:64, dt:dt + 1])
                p.op("scalar", "activation", qbuf[0:64, 2 * dt + 1, :], pa[64:128, :], AF.Identity,
                     bias=qbb[64:128, dt:dt + 1])
            qdone = p.n("sS")
            p.wait("tensor", ("sS", qdone))
            scfree = [None, None, None, None]
            prfree = {(i, j): None for i in range(4) for j in range(2)}
            pready = {}
            av_mm = {}
            for hb in range(4):
                for kc in range(Kg):
                    knm = ("ka", "kb", "kc2", "kd")[kc % 4]
                    vnm = ("va", "vb", "vc", "vd")[kc % 4]
                    kb_ = kchs[kc % 4]
                    vb_ = vchs[kc % 4]
                    if kc >= 4:
                        p.wait("sync", ("mm", av_mm[kc - 4]))
                    elif hb > 0 or g > 0:
                        p.wait("sync", ("mm", mm_kprev))
                    p.dma("sync", kb_.rearrange("q a n -> q (a n)"),
                          k_scr[kc, 0:64, hb * 512:(hb + 1) * 512], (knm, 16))
                    p.dma("sync", vb_, v_scr[kc, :, hb * 260:(hb + 1) * 260], (vnm, 16))
                    p.wait("tensor", (knm, p.n(knm)), (vnm, p.n(vnm)))
                    dk0 = Kg - 4
                    scb = [pS[0], pS[1], pA[0], pA[1]]
                    for hin in range(4):
                        h = hb * 4 + hin
                        if scfree[hin] is not None:
                            p.wait("tensor", scfree[hin])
                        p.op("tensor", "matmul", scb[hin], kb_[0:64, hin, :], qbuf[0:64, h, :],
                             start=True, stop=True, inc=("mm", 1))
                        mmsc = p.n("mm")
                        if prfree[(hin, kc % 2)] is not None:
                            p.wait("scalar", prfree[(hin, kc % 2)])
                        p.wait("scalar", ("mm", mmsc))
                        prbh = prbufs[hin * 2 + kc % 2]
                        p.op("scalar", "activation", prbh, scb[hin], AF.Exp, scale=0.125,
                             bias=pbb if (kc < 16 and g >= 4) else zbb)
                        scfree[hin] = ("sS", p.n("sS"))
                        pready[(hin, kc % 2)] = ("sS", p.n("sS"))
                        if kc >= dk0:
                            jd = kc - dk0
                            p.wait("vector", ("sS", p.n("sS")))
                            if jd > 0:
                                p.op("vector", "memset", prbh[:, 0:jd * 128], 0.0)
                            p.op("vector", "tensor_tensor", prbh[:, jd * 128:(jd + 1) * 128],
                                 prbh[:, jd * 128:(jd + 1) * 128], trib, ALU.mult)
                            pready[(hin, kc % 2)] = ("vS", p.n("vS"))
                    if kc > 0:
                        vprev = vchs[(kc - 1) % 4]
                        for hin in range(4):
                            p.wait("tensor", pready[(hin, (kc - 1) % 2)])
                            p.op("tensor", "matmul", soP[hin],
                                 vprev[:, hin * 65:(hin + 1) * 65],
                                 prbufs[hin * 2 + (kc - 1) % 2],
                                 start=(kc - 1 == 0), stop=False, inc=("mm", 1))
                            prfree[(hin, (kc - 1) % 2)] = ("mm", p.n("mm"))
                        av_mm[kc - 1] = p.n("mm")
                    if kc == Kg - 1:
                        vlast = vchs[kc % 4]
                        for hin in range(4):
                            p.wait("tensor", pready[(hin, kc % 2)])
                            p.op("tensor", "matmul", soP[hin],
                                 vlast[:, hin * 65:(hin + 1) * 65],
                                 prbufs[hin * 2 + kc % 2],
                                 start=(kc == 0), stop=True, inc=("mm", 1))
                            prfree[(hin, kc % 2)] = ("mm", p.n("mm"))
                        av_mm[kc] = p.n("mm")
                mm_kprev = p.n("mm")
                p.wait("vector", ("mm", p.n("mm")))
                for hin in range(4):
                    h = hb * 4 + hin
                    p.op("vector", "reciprocal", rcrow, soP[hin][64:65, :])
                    p.wait("tensor", ("vS", p.n("vS")))
                    if scfree[0] is not None:
                        p.wait("tensor", scfree[0])
                    p.op("tensor", "matmul", pS[0][0:64, :], o64, rcrow, start=True, stop=True,
                         inc=("mm", 1))
                    p.wait("scalar", ("mm", p.n("mm")))
                    p.op("scalar", "copy", rcs, pS[0][0:64, :])
                    scfree[0] = ("sS", p.n("sS"))
                    p.wait("vector", ("sS", p.n("sS")))
                    p.op("vector", "tensor_tensor",
                         comb[(h % 2) * 64:(h % 2) * 64 + 64, h // 2, :],
                         soP[hin][0:64, :], rcs, ALU.mult)
                p.wait("tensor", ("vS", p.n("vS")))
            p.wait("tensor", ("sS", gT_done))
            for j in range(4):
                p.op("tensor", "matmul", pS[1][:, j * 128:(j + 1) * 128],
                     oh32b[:, (4 * g + j) * 128:(4 * g + j + 1) * 128], gT,
                     start=(j == 0), stop=(j == 3), inc=("mm", 1) if j == 3 else None)
            p.wait("scalar", ("mm", p.n("mm")))
            p.op("scalar", "copy", gb, pS[1])
            gbd = p.n("sS")
            for dt in range(CC):
                pa = pA[dt % 2]
                if dt >= 2:
                    p.wait("tensor", ("vS", p.n("vS")))
                for cc in range(CC):
                    p.op("tensor", "matmul", pa, memvb[:, cc, dt * 128:(dt + 1) * 128],
                         bufa[:, cc, 0:512], start=(cc == 0), stop=(cc == CC - 1),
                         inc=("mm", 1) if cc == CC - 1 else None)
                p.wait("scalar", ("mm", p.n("mm")), ("sS", gbd), ("vS", p.n("vS")))
                p.op("scalar", "activation", mvs, pa, AF.Identity, bias=mvbb[:, dt:dt + 1])
                p.wait("vector", ("sS", p.n("sS")))
                p.op("vector", "tensor_tensor", prbufs[0], mvs, gb, ALU.mult)
                p.op("vector", "tensor_tensor", comb[:, dt, :], comb[:, dt, :], prbufs[0], ALU.add)
            p.wait("tensor", ("vS", p.n("vS")))
            vs_pre = p.n("vS")
            for dt2 in range(CC):
                pa = pA[dt2 % 2]
                if dt2 >= 2:
                    p.wait("tensor", ("vS", vs_pre + dt2 - 1))
                for cc in range(CC):
                    p.op("tensor", "matmul", pa, woutb[:, cc, dt2 * 128:(dt2 + 1) * 128],
                         comb[:, cc, :], start=(cc == 0), stop=(cc == CC - 1),
                         inc=("mm", 1) if cc == CC - 1 else None)
                if dt2 >= 2:
                    p.wait("sync", ("vS", vs_pre + dt2 - 1))
                elif g > 0:
                    p.wait("sync", ("vS", vs_pre))
                p.dma("sync", xres if dt2 % 2 == 0 else xres2,
                      xin[dt2 * 128:(dt2 + 1) * 128, g * 512:(g + 1) * 512],
                      ("sxa" if dt2 % 2 == 0 else "sxb", 16))
                p.wait("vector", ("mm", p.n("mm")),
                       ("sxa" if dt2 % 2 == 0 else "sxb", p.n("sxa" if dt2 % 2 == 0 else "sxb")))
                p.op("vector", "scalar_tensor_tensor", s_f[:, dt2, :], pa,
                     obb[:, dt2:dt2 + 1], xres if dt2 % 2 == 0 else xres2, ALU.add, ALU.add)
            p.wait("tensor", ("vS", p.n("vS")))
            p.wait("scalar", ("vS", p.n("vS")))
            emit_ln(p, s_f, s_f, lnb)
            if g >= 4 or fused:
                oc0 = g * 512 if fused else (g - 4) * 512
                p.wait("gpsimd", ("vS", p.n("vS")))
                for dt in range(CC):
                    p.dma("gpsimd", x1o[dt * 128:(dt + 1) * 128, oc0:oc0 + 512],
                          s_f[:, dt, :], ("so", 16))
                p.wait("vector", ("so", p.n("so")))
            mm_gend = p.n("mm")
        p.wait("gpsimd", ("so", p.n("so")))


def build_phase1():
    nc = bass.Bass("TRN2", target_bir_lowering=False)
    ctx = contextlib.ExitStack()
    xin = din(nc, "xin", [D, T])
    wqkv = din(nc, "wqkv", [D, 3 * D])
    qbc = din(nc, "qbc", [128, CC])
    kbc = din(nc, "kbc", [128, CC])
    vbr = din(nc, "vbr", [1, D])
    wquad = din(nc, "wquad", [D, 256])
    memv = din(nc, "memv", [D, D])
    memvbc = din(nc, "memvbc", [128, CC])
    memg = din(nc, "memg", [D, H])
    memgbr = din(nc, "memgbr", [1, H])
    mscr = din(nc, "mscr", [1, H])
    imix = din(nc, "imix", [1, 1])
    declog = din(nc, "declog", [1, H])
    wout = din(nc, "wout", [D, D])
    obc = din(nc, "obc", [128, CC])
    delta = din(nc, "delta", [128, 128])
    negm = din(nc, "negm", [128, 128])
    trow = din(nc, "trow", [1, 128])
    dcol = din(nc, "dcol", [128, 1])
    signs = din(nc, "signs", [128, 96])
    ident = din(nc, "ident", [128, 128])
    oneh32 = din(nc, "oneh32", [32, 32 * 128])
    bval = din(nc, "bval", [128, 1])
    pbias = din(nc, "pbias", [128, 1])
    zbias = din(nc, "zbias", [128, 1])
    trim = din(nc, "trim", [128, 128])
    ocf = din(nc, "ocf", [128, 1])
    orf = din(nc, "orf", [1, 128])
    x1o = dout(nc, "x1o", [D, HALF])
    xb = dscr(nc, "xb", [CC, 128, T + 1])
    k_scr = dscr(nc, "k_scr", [NT, 64, H * 128])
    v_scr = dscr(nc, "v_scr", [NT, 128, H * 65])

    with ctx:
        p = Plan(nc, ctx)
        p.sem("ld0", "ga", "gw", "gw2", "sa", "sb", "sr", "sxa", "sxb", "stB", "stC", "stK", "stV", "stV2", "ka", "kb", "kc2", "kd", "va", "vb", "vc", "vd", "so", "mm", "vS", "sS", "gS", "g1a", "g1b", "g2a", "g2b", "sxw")
        emit_phase1(p, nc, xin, wqkv, qbc, kbc, vbr, wquad, memv, memvbc, memg, memgbr, mscr, imix, declog, wout, obc, delta, negm, trow, dcol, signs, ident, oneh32, bval, pbias, zbias, trim, ocf, orf, x1o, xb, k_scr, v_scr)
        p.build()
    return nc


def build_fused():
    nc = bass.Bass("TRN2", target_bir_lowering=False)
    ctx = contextlib.ExitStack()
    ident = din(nc, "ident", [128, 128])
    trim = din(nc, "trim", [128, 128])
    ocf = din(nc, "ocf", [128, 1])
    orf = din(nc, "orf", [1, 128])
    bval = din(nc, "bval", [128, 1])
    pbias = din(nc, "pbias", [128, 1])
    zbias = din(nc, "zbias", [128, 1])
    xin = din(nc, "xin", [D, T])
    wqkv = din(nc, "wqkv", [D, 3 * D])
    qbc = din(nc, "qbc", [128, CC])
    kbc = din(nc, "kbc", [128, CC])
    vbr = din(nc, "vbr", [1, D])
    wquad = din(nc, "wquad", [D, 256])
    memv = din(nc, "memv", [D, D])
    memvbc = din(nc, "memvbc", [128, CC])
    memg = din(nc, "memg", [D, H])
    memgbr = din(nc, "memgbr", [1, H])
    mscr = din(nc, "mscr", [1, H])
    imix = din(nc, "imix", [1, 1])
    declog = din(nc, "declog", [1, H])
    wout = din(nc, "wout", [D, D])
    obc = din(nc, "obc", [128, CC])
    delta = din(nc, "delta", [128, 128])
    negm = din(nc, "negm", [128, 128])
    trow = din(nc, "trow", [1, 128])
    dcol = din(nc, "dcol", [128, 1])
    signs = din(nc, "signs", [128, 96])
    oneh32 = din(nc, "oneh32", [32, 32 * 128])
    wr = din(nc, "wr", [D, D])
    wk = din(nc, "wk", [D, D])
    wv = din(nc, "wv", [D, D])
    mur = din(nc, "mur", [128, CC])
    muk = din(nc, "muk", [128, CC])
    wqm = din(nc, "wqm", [D, D])
    wkm = din(nc, "wkm", [D, D])
    wvm = din(nc, "wvm", [D, D])
    wom = din(nc, "wom", [D, D])
    bqc = din(nc, "bqc", [128, CC])
    bkc = din(nc, "bkc", [128, CC])
    bvr = din(nc, "bvr", [1, D])
    boc = din(nc, "boc", [128, CC])
    sinq = din(nc, "sinq", [32, HALF])
    cosq = din(nc, "cosq", [32, HALF])
    sink = din(nc, "sink", [32, T])
    cosk = din(nc, "cosk", [32, T])
    wrt = din(nc, "wrt", [D, NE])
    w1t = din(nc, "w1t", [NE, 32, 128, D])
    b1c = din(nc, "b1c", [128, NE * 32])
    w2t = din(nc, "w2t", [NE, CC, 128, 32, 128])
    b2m = din(nc, "b2m", [NE, D])
    ws1t = din(nc, "ws1t", [32, 128, D])
    bs1c = din(nc, "bs1c", [128, 32])
    ws2t = din(nc, "ws2t", [CC, 128, 32, 128])
    bs2c = din(nc, "bs2c", [128, CC])
    oneh = din(nc, "oneh", [NE, NE * 128])
    x1i = dscr(nc, "x1i", [D, T], F32)
    x2i = dscr(nc, "x2i", [D, T], F32)
    x3i = dscr(nc, "x3i", [D, HALF], F32)
    xb1 = dscr(nc, "xb1", [CC, 128, T + 1])
    ks1 = dscr(nc, "ks1", [NT, 64, H * 128])
    vs1 = dscr(nc, "vs1", [NT, 128, H * 65])
    x1b = dscr(nc, "x1b", [CC, 128, T + 1])
    xr_s = dscr(nc, "xr_s", [CC, 128, T])
    xk_s = dscr(nc, "xk_s", [CC, 128, T])
    r_s = dscr(nc, "r_s", [CC, 128, T])
    ka_s = dscr(nc, "ka_s", [CC, 128, T])
    x2b = dscr(nc, "x2b", [CC, 128, T])
    ks3 = dscr(nc, "ks3", [NT, 64, H * 128])
    vs3 = dscr(nc, "vs3", [NT, 128, H * 65])
    x3b = dscr(nc, "x3b", [CC, 128, HALF])
    x4 = dout(nc, "x4", [D, HALF])
    with ctx:
        p = Plan(nc, ctx)
        p.sem(*BARRIER_SEMS)
        emit_phase1(p, nc, xin, wqkv, qbc, kbc, vbr, wquad, memv, memvbc, memg, memgbr,
                    mscr, imix, declog, wout, obc, delta, negm, trow, dcol, signs, ident,
                    oneh32, bval, pbias, zbias, trim, ocf, orf, x1i, xb1, ks1, vs1, fused=True)
        barrier(p)
        emit_phase2(p, nc, x1i, wr, wk, wv, mur, muk, bval, ocf, orf, x2i,
                    x1b, xr_s, xk_s, r_s, ka_s, fused=True)
        barrier(p)
        emit_phase3(p, nc, x2i, wqm, wkm, wvm, wom, bqc, bkc, bvr, boc, sinq, cosq,
                    sink, cosk, pbias, zbias, trim, ocf, orf, x3i, x2b, ks3, vs3, fused=True)
        barrier(p)
        emit_phase4(p, nc, x3i, wrt, w1t, b1c, w2t, b2m, ws1t, bs1c, ws2t, bs2c,
                    ident, oneh, ocf, orf, x4, x3b, fused=True)
        p.build()
    return nc


_CACHE = {}
_USE_FUSED = True


def _cols(b):
    return np.ascontiguousarray(b.reshape(-1, 128).T.astype(np.float32))


def kernel(L0_attn_x, L0_attn_qkv_w, L0_attn_qkv_b, L0_attn_w1w, L0_attn_w2w,
           L0_attn_r1w, L0_attn_r2w, L0_attn_memv_w, L0_attn_memv_b, L0_attn_memg_w,
           L0_attn_memg_b, L0_attn_mem_scale, L0_attn_iter_mix, L0_attn_out_w,
           L0_attn_out_b, L0_attn_decay_logits, L0_ffn_W_r, L0_ffn_W_k, L0_ffn_W_v,
           L0_ffn_mu_r, L0_ffn_mu_k, L1_attn_Wq, L1_attn_bq, L1_attn_Wk, L1_attn_bk,
           L1_attn_Wv, L1_attn_bv, L1_attn_Wo, L1_attn_bo, L1_ffn_w_router, L1_ffn_W1,
           L1_ffn_b1, L1_ffn_W2, L1_ffn_b2, L1_ffn_Ws1, L1_ffn_bs1, L1_ffn_Ws2,
           L1_ffn_bs2):
    f32 = np.float32
    x = np.asarray(L0_attn_x, f32)
    ones_c = np.ones((128, 1), f32)
    ones_r = np.ones((1, 128), f32)
    delta = (np.arange(128)[None, :] - np.arange(128)[:, None]).astype(f32)
    negm = np.where(delta >= 1, 0.0, NEG).astype(f32)
    signs = np.tile(np.array(JSIGN, f32), H)[None, :].repeat(128, 0)
    oneh32 = np.zeros((32, 32 * 128), f32)
    for t in range(32):
        oneh32[t, t * 128:(t + 1) * 128] = 1
    onehE = np.zeros((NE, NE * 128), f32)
    for e in range(NE):
        onehE[e, e * 128:(e + 1) * 128] = 1
    trim = np.triu(np.ones((128, 128), f32))
    ident = np.eye(128, dtype=f32)
    zb = np.zeros((128, 1), f32)
    pos = np.arange(T, dtype=np.float64)
    inv_freq = np.exp(np.arange(0, DH, 2, dtype=np.float64) * (-np.log(10000.0) / DH))
    ang = pos[:, None] * inv_freq[None, :]
    sin_t = np.sin(ang).astype(f32).T.copy()
    cos_t = np.cos(ang).astype(f32).T.copy()

    qb = np.asarray(L0_attn_qkv_b, f32)
    common1 = dict(
        wqkv=np.ascontiguousarray(np.asarray(L0_attn_qkv_w, f32).T),
        qbc=_cols(qb[0:D]), kbc=_cols(qb[D:2 * D]), vbr=qb[2 * D:].reshape(1, D).copy(),
        wquad=np.ascontiguousarray(np.concatenate(
            [np.asarray(L0_attn_w1w, f32).T, np.asarray(L0_attn_w2w, f32).T,
             np.asarray(L0_attn_r1w, f32).T, np.asarray(L0_attn_r2w, f32).T], 1)),
        memv=np.ascontiguousarray(np.asarray(L0_attn_memv_w, f32).T),
        memvbc=_cols(np.asarray(L0_attn_memv_b, f32)),
        memg=np.ascontiguousarray(np.asarray(L0_attn_memg_w, f32).T),
        memgbr=np.asarray(L0_attn_memg_b, f32).reshape(1, H).copy(),
        mscr=np.asarray(L0_attn_mem_scale, f32).reshape(1, H).copy(),
        imix=np.asarray(L0_attn_iter_mix, f32).reshape(1, 1).copy(),
        declog=np.asarray(L0_attn_decay_logits, f32).reshape(1, H).copy(),
        wout=np.ascontiguousarray(np.asarray(L0_attn_out_w, f32).T),
        obc=_cols(np.asarray(L0_attn_out_b, f32)),
        delta=delta, negm=negm,
        trow=np.arange(128, dtype=f32).reshape(1, -1),
        dcol=(128 - np.arange(128, dtype=f32)).reshape(-1, 1),
        signs=signs, ident=ident, oneh32=oneh32, trim=trim, zbias=zb,
        ocf=ones_c, orf=ones_r)
    common2 = dict(
        wr=np.ascontiguousarray(np.asarray(L0_ffn_W_r, f32).T),
        wk=np.ascontiguousarray(np.asarray(L0_ffn_W_k, f32).T),
        wv=np.ascontiguousarray(np.asarray(L0_ffn_W_v, f32).T),
        mur=_cols(np.asarray(L0_ffn_mu_r, f32)), muk=_cols(np.asarray(L0_ffn_mu_k, f32)))
    common3 = dict(
        wqm=np.ascontiguousarray(np.asarray(L1_attn_Wq, f32)),
        wkm=np.ascontiguousarray(np.asarray(L1_attn_Wk, f32)),
        wvm=np.ascontiguousarray(np.asarray(L1_attn_Wv, f32)),
        wom=np.ascontiguousarray(np.asarray(L1_attn_Wo, f32)),
        bqc=_cols(np.asarray(L1_attn_bq, f32)), bkc=_cols(np.asarray(L1_attn_bk, f32)),
        bvr=np.asarray(L1_attn_bv, f32).reshape(1, D).copy(),
        boc=_cols(np.asarray(L1_attn_bo, f32)))
    W1 = np.asarray(L1_ffn_W1, f32)
    W2 = np.asarray(L1_ffn_W2, f32)
    w1t = np.ascontiguousarray(W1.reshape(NE, 32, 128, CC, 128).transpose(0, 1, 4, 3, 2)
                               .reshape(NE, 32, 128, D))
    w2t = np.ascontiguousarray(W2.reshape(NE, CC, 128, 32, 128).transpose(0, 1, 4, 3, 2))
    Ws1 = np.asarray(L1_ffn_Ws1, f32)
    Ws2 = np.asarray(L1_ffn_Ws2, f32)
    common4 = dict(
        wrt=np.ascontiguousarray(np.asarray(L1_ffn_w_router, f32).T),
        w1t=w1t, b1c=np.ascontiguousarray(np.asarray(L1_ffn_b1, f32).reshape(NE * 32, 128).T),
        w2t=w2t, b2m=np.asarray(L1_ffn_b2, f32),
        ws1t=np.ascontiguousarray(Ws1.reshape(32, 128, CC, 128).transpose(0, 3, 2, 1)
                                  .reshape(32, 128, D)),
        bs1c=_cols(np.asarray(L1_ffn_bs1, f32)),
        ws2t=np.ascontiguousarray(Ws2.reshape(CC, 128, 32, 128).transpose(0, 3, 2, 1)),
        bs2c=_cols(np.asarray(L1_ffn_bs2, f32)),
        oneh=onehE)

    def core_extras(b, h):
        xT = np.ascontiguousarray(x[b].T)
        own = slice(h * HALF, (h + 1) * HALF)
        d = {}
        if h == 0:
            d["xin"] = np.ascontiguousarray(
                np.concatenate([np.zeros((D, HALF), f32), xT[:, 0:HALF]], 1))
            d["bval"] = zb
            d["pbias"] = np.full((128, 1), NEG, f32)
        else:
            d["xin"] = xT
            d["bval"] = ones_c
            d["pbias"] = zb
        d["sinq"] = np.ascontiguousarray(sin_t[:, own])
        d["cosq"] = np.ascontiguousarray(cos_t[:, own])
        d["sink"] = np.ascontiguousarray(np.concatenate([sin_t[:, 0:HALF], sin_t[:, own]], 1))
        d["cosk"] = np.ascontiguousarray(np.concatenate([cos_t[:, 0:HALF], cos_t[:, own]], 1))
        return d

    if _USE_FUSED:
        if "fused" not in _CACHE:
            _CACHE["fused"] = build_fused()
        ncF = _CACHE["fused"]
        ims = []
        for b in range(B):
            for h in (0, 1):
                d = {}
                d.update(common1)
                d.update(common2)
                d.update(common3)
                d.update(common4)
                d.update(core_extras(b, h))
                ims.append(d)
        r = run_bass_kernel_spmd(ncF, ims, core_ids=list(range(NC)))
        out = np.stack([np.concatenate(
            [r.results[2 * b]["x4"], r.results[2 * b + 1]["x4"]], 1).T for b in range(B)])
        return np.ascontiguousarray(out.astype(np.float32))

    if "ncs" not in _CACHE:
        _CACHE["ncs"] = (build_phase1(), build_phase2(), build_phase3(), build_phase4())
    nc1, nc2, nc3, nc4 = _CACHE["ncs"]
    im1 = []
    for b in range(B):
        for h in (0, 1):
            d = dict(common1)
            d.update({k: v for k, v in core_extras(b, h).items()
                      if k in ("xin", "bval", "pbias")})
            im1.append(d)
    r1 = run_bass_kernel_spmd(nc1, im1, core_ids=list(range(NC)))
    x1T = [np.concatenate([r1.results[2 * b]["x1o"], r1.results[2 * b + 1]["x1o"]], 1)
           for b in range(B)]
    im2 = []
    for b in range(B):
        for h in (0, 1):
            d = dict(common2)
            d["ocf"] = ones_c
            d["orf"] = ones_r
            if h == 0:
                d["x1"] = np.ascontiguousarray(
                    np.concatenate([x1T[b][:, 0:HALF], x1T[b][:, 0:HALF]], 1))
                d["bval"] = zb
            else:
                d["x1"] = np.ascontiguousarray(x1T[b])
                d["bval"] = ones_c
            im2.append(d)
    r2 = run_bass_kernel_spmd(nc2, im2, core_ids=list(range(NC)))
    x2T = [np.concatenate([r2.results[2 * b]["x2"], r2.results[2 * b + 1]["x2"]], 1)
           for b in range(B)]
    im3 = []
    for b in range(B):
        for h in (0, 1):
            d = dict(common3)
            ce = core_extras(b, h)
            d.update({k: ce[k] for k in ("pbias", "sinq", "cosq", "sink", "cosk")})
            d.update(dict(trim=trim, zbias=zb, ocf=ones_c, orf=ones_r))
            if h == 0:
                d["x2"] = np.ascontiguousarray(
                    np.concatenate([x2T[b][:, 0:HALF], x2T[b][:, 0:HALF]], 1))
            else:
                d["x2"] = np.ascontiguousarray(x2T[b])
            im3.append(d)
    r3 = run_bass_kernel_spmd(nc3, im3, core_ids=list(range(NC)))
    im4 = []
    for c in range(NC):
        d = dict(common4)
        d.update(dict(ident=ident, ocf=ones_c, orf=ones_r))
        d["x3"] = np.ascontiguousarray(r3.results[c]["x3"])
        im4.append(d)
    r4 = run_bass_kernel_spmd(nc4, im4, core_ids=list(range(NC)))
    out = np.stack([np.concatenate([r4.results[2 * b]["x4"], r4.results[2 * b + 1]["x4"]], 1).T
                    for b in range(B)])
    return np.ascontiguousarray(out.astype(np.float32))


# version 08 (final)
# speedup vs baseline: 1.7567x; 1.0092x over previous
import contextlib
import numpy as np
import concourse.bass as bass
import concourse.mybir as mybir
from concourse.bass_utils import run_bass_kernel_spmd

F32 = mybir.dt.float32
BF16 = mybir.dt.bfloat16
AF = mybir.ActivationFunctionType
ALU = mybir.AluOpType
AX = mybir.AxisListType

B, T, D, H, DH = 4, 4096, 1024, 16, 64
HALF = T // 2
NC = 8
CC = D // 128
NT = T // 128
NG = T // 512
DFF = 4096
NE = 8
MOE_SCALE = float(1.0 / np.sqrt(1 + 2.0 / 8.0))
NH = 2
HYT = HALF // NH
PAIRS = [(0, 1), (0, 2), (0, 3), (1, 2), (1, 3), (2, 3)]
JSIGN = [1.0, -1.0, 1.0, 1.0, -1.0, 1.0]
NEG = -1e9

ENGINES = ("sync", "gpsimd", "tensor", "vector", "scalar")


class Plan:
    def __init__(self, nc, ctx):
        self.nc, self.ctx = nc, ctx
        self.ops = {e: [] for e in ENGINES}
        self.sems = {}

    def sem(self, *names):
        for name in names:
            h = self.ctx.enter_context(self.nc.semaphore(name))
            self.sems[name] = [h, 0]

    def n(self, name):
        return self.sems[name][1]

    def wait(self, eng, *pairs):
        for (nm, v) in pairs:
            self.ops[eng].append(("w", self.sems[nm][0], v))

    def waitn(self, eng, *names):
        for nm in names:
            self.ops[eng].append(("w", self.sems[nm][0], self.sems[nm][1]))

    def op(self, eng, method, *args, inc=None, **kw):
        if eng == "vector" or eng == "scalar":
            assert inc is None or inc[0] in ("vS", "sS")
            nm = "vS" if eng == "vector" else "sS"
            rec = self.sems[nm]
            if rec[1] > 0:
                self.ops[eng].append(("w", rec[0], rec[1]))
            rec[1] += 1
            self.ops[eng].append(("o", method, args, kw, rec[0], 1))
            return
        if inc is not None:
            nm, k = inc
            rec = self.sems[nm]
            rec[1] += k
            self.ops[eng].append(("o", method, args, kw, rec[0], k))
        else:
            self.ops[eng].append(("o", method, args, kw, None, 0))

    def gop(self, method, *args, **kw):
        rec = self.sems["gS"]
        if rec[1] > 0:
            self.ops["gpsimd"].append(("w", rec[0], rec[1]))
        rec[1] += 1
        self.ops["gpsimd"].append(("o", method, args, kw, rec[0], 1))

    def dma(self, eng, out, in_, inc, **kw):
        self.op(eng, "dma_start", inc=inc, out=out, in_=in_, **kw)

    def build(self):
        blk = self.ctx.enter_context(self.nc.Block())
        for eng in ENGINES:
            lst = self.ops[eng]
            if not lst:
                continue

            def runner(e, lst=lst):
                for it in lst:
                    if it[0] == "w":
                        e.wait_ge(it[1], it[2])
                    else:
                        _, method, args, kw, h, k = it
                        ins = getattr(e, method)(*args, **kw)
                        if h is not None:
                            ins.then_inc(h, k)
            getattr(blk, eng)(runner)


def din(nc, name, shape, dtype=F32):
    return nc.dram_tensor(name, list(shape), dtype, kind="ExternalInput").ap()


def dout(nc, name, shape, dtype=F32):
    return nc.dram_tensor(name, list(shape), dtype, kind="ExternalOutput").ap()


def dscr(nc, name, shape, dtype=BF16):
    return nc.dram_tensor(name, list(shape), dtype, kind="Internal").ap()


_NAMEC = [0]


def sb(ctx, nc, name, shape, dtype):
    _NAMEC[0] += 1
    return ctx.enter_context(nc.sbuf_tensor(f"{name}_{_NAMEC[0]}", list(shape), dtype)).ap()


def psum(ctx, nc, name, shape, dtype=F32):
    _NAMEC[0] += 1
    return ctx.enter_context(nc.psum_tensor(f"{name}_{_NAMEC[0]}", list(shape), dtype)).ap()


def bounce_cast(p, scr, src, buf, ld, st, ncc=CC, tcols=T, pad=False):
    for cc in range(ncc):
        self_ld0 = p.n(ld)
        p.wait("gpsimd", (st, max(0, p.n(st) - 16)))
        p.dma("gpsimd", buf, src[cc * 128:(cc + 1) * 128, :], (ld, 16))
        p.wait("gpsimd", (ld, p.n(ld)))
        if pad:
            p.dma("gpsimd", scr[cc, :, 1:tcols + 1], buf, (st, 16))
        else:
            p.dma("gpsimd", scr[cc, :, :], buf, (st, 16))
        del self_ld0


def emit_ln(p, s, out, bufs, nblk=CC, inv_n=1.0 / D):
    psS0, psS1 = bufs["psS"]
    psB0, psB1 = bufs["psB"]
    sq0, sq1 = bufs["sq"]
    r_m, r_v, r_sd, r_i = bufs["rows"]
    mm0 = p.n("mm")
    for dt in range(nblk):
        p.op("tensor", "matmul", psS0, bufs["ones_col_f"], s[:, dt, :],
             start=(dt == 0), stop=(dt == nblk - 1))
        if dt >= 2:
            p.wait("scalar", ("mm", mm0 + dt - 1))
        p.op("scalar", "square", sq0 if dt % 2 == 0 else sq1, s[:, dt, :])
        p.wait("tensor", ("sS", p.n("sS")))
        p.op("tensor", "matmul", psS1, bufs["ones_col_f"], sq0 if dt % 2 == 0 else sq1,
             start=(dt == 0), stop=(dt == nblk - 1), inc=("mm", 1))
    p.wait("vector", ("mm", p.n("mm")))
    p.op("vector", "tensor_scalar", r_m, psS0, inv_n, None, ALU.mult)
    p.op("vector", "tensor_scalar", r_v, psS1, inv_n, None, ALU.mult)
    p.op("vector", "tensor_tensor", r_sd, r_m, r_m, ALU.mult)
    p.op("vector", "tensor_tensor", r_v, r_v, r_sd, ALU.subtract)
    p.op("vector", "tensor_scalar", r_v, r_v, 0.0, 1e-5, ALU.max, ALU.add)
    p.wait("scalar", ("vS", p.n("vS")))
    p.op("scalar", "activation", r_sd, r_v, AF.Sqrt)
    p.wait("vector", ("sS", p.n("sS")))
    p.op("vector", "reciprocal", r_i, r_sd)
    p.wait("tensor", ("vS", p.n("vS")))
    p.op("tensor", "matmul", psB0, bufs["ones_row_f"], r_m, start=True, stop=True)
    p.op("tensor", "matmul", psB1, bufs["ones_row_f"], r_i, start=True, stop=True, inc=("mm", 1))
    p.wait("vector", ("mm", p.n("mm")))
    for dt in range(nblk):
        p.op("vector", "tensor_tensor", sq0 if dt % 2 == 0 else sq1, s[:, dt, :], psB0, ALU.subtract)
        p.op("vector", "tensor_tensor", out[:, dt, :], sq0 if dt % 2 == 0 else sq1, psB1, ALU.mult)


BARRIER_SEMS = ["ld0", "ga", "gw", "gw2", "sa", "sb", "sr", "sxa", "sxb", "stB", "stC", "stK", "stV", "stV2", "ka", "kb", "kc2", "kd", "va", "vb", "vc", "vd", "so", "mm", "vS", "sS", "gS", "g1a", "g1b", "g2a", "g2b", "sxw"]


def barrier(p):
    vals = [(nm, p.n(nm)) for nm in BARRIER_SEMS]
    for eng in ENGINES:
        for nm, v in vals:
            if v > 0:
                p.ops[eng].append(("w", p.sems[nm][0], v))


def emit_phase2(p, nc, x1, wr, wk, wv, mur, muk, bval, ocf, orf, x2, x1b, xr_s, xk_s, r_s, ka_s, fused=False):
    ctx = contextlib.ExitStack()
    with ctx:
        cbuf = sb(ctx, nc, "cbuf", [128, T], BF16)
        zcol = sb(ctx, nc, "zcol", [128, 1], BF16)
        w_a = sb(ctx, nc, "w_a", [128, CC, 1024], BF16)
        w_b = sb(ctx, nc, "w_b", [128, CC, 1024], BF16)
        w_c = sb(ctx, nc, "w_c", [128, CC, 1024], BF16)
        murb = sb(ctx, nc, "murb", [128, CC], F32)
        mukb = sb(ctx, nc, "mukb", [128, CC], F32)
        bvb = sb(ctx, nc, "bvb", [128, 1], F32)
        ocfb = sb(ctx, nc, "ocfb", [128, 1], F32)
        orfb = sb(ctx, nc, "orfb", [1, 128], F32)
        xw = sb(ctx, nc, "xw", [128, CC, 513], BF16)
        xw2 = sb(ctx, nc, "xw2", [128, CC, 513], BF16)
        xrb = [sb(ctx, nc, f"xrb{i}", [128, CC, 512], BF16) for i in range(2)]
        xkb = [sb(ctx, nc, f"xkb{i}", [128, CC, 512], BF16) for i in range(2)]
        rbf = [sb(ctx, nc, f"rbf{i}", [128, CC, 512], BF16) for i in range(2)]
        kab = [sb(ctx, nc, f"kab{i}", [128, CC, 512], BF16) for i in range(2)]
        dbuf = sb(ctx, nc, "dbuf", [128, 512], F32)
        tcol = sb(ctx, nc, "tcol", [128, CC], F32)
        tcol2 = sb(ctx, nc, "tcol2", [128, CC], F32)
        relu_t = sb(ctx, nc, "relu_t", [128, 512], F32)
        relu_t2 = sb(ctx, nc, "relu_t2", [128, 512], F32)
        s_f = sb(ctx, nc, "s_f", [128, CC, 512], F32)
        xres = sb(ctx, nc, "xres", [128, 512], F32)
        xres2 = sb(ctx, nc, "xres2", [128, 512], F32)
        sq = sb(ctx, nc, "sq", [128, 512], F32)
        sq2 = sb(ctx, nc, "sq2", [128, 512], F32)
        rows = [sb(ctx, nc, f"row{i}", [1, 512], F32) for i in range(4)]
        pB = [psum(ctx, nc, f"pB{i}", [128, 512]) for i in range(4)]
        psS0 = psum(ctx, nc, "psS0", [1, 512])
        psS1 = psum(ctx, nc, "psS1", [1, 512])
        lnb = dict(psS=(psS0, psS1), psB=(pB[0], pB[1]), sq=(sq, sq2),
                   rows=rows, ones_col_f=ocfb, ones_row_f=orfb)

        for dst, src in ((murb, mur), (mukb, muk), (bvb, bval), (ocfb, ocf), (orfb, orf)):
            p.dma("sync", dst, src, ("ld0", 16))
        p.gop("memset", zcol, 0.0)
        for cc in range(CC):
            p.dma("gpsimd", cbuf, x1[cc * 128:(cc + 1) * 128, :], ("ga", 16))
            p.wait("gpsimd", ("ga", p.n("ga")))
            p.dma("gpsimd", x1b[cc, :, 1:T + 1], cbuf, ("ga", 16))
            p.dma("gpsimd", x1b[cc, :, 0:1], zcol, ("ga", 16), allow_slow_non_contiguous=True)
            p.wait("gpsimd", ("ga", p.n("ga")))
        gaA = p.n("ga")
        for cc in range(CC):
            p.dma("gpsimd", w_a[:, cc, :], wr[cc * 128:(cc + 1) * 128, :], ("gw", 16))
            p.dma("gpsimd", w_b[:, cc, :], wk[cc * 128:(cc + 1) * 128, :], ("gw", 16))
            p.dma("gpsimd", w_c[:, cc, :], wv[cc * 128:(cc + 1) * 128, :], ("gw", 16))
        gwT = p.n("gw")

        p.wait("sync", ("ga", gaA), ("ld0", p.n("ld0")))
        p.wait("tensor", ("gw", gwT))
        psfree = [None, None, None, None]
        mm_tc = {}
        xr_done = {}
        for tc in range(NG):
            bufa = xw if tc % 2 == 0 else xw2
            snm = "sa" if tc % 2 == 0 else "sb"
            if tc >= 2:
                p.wait("sync", ("mm", mm_tc[tc - 2]))
            for cc in range(CC):
                p.dma("sync", bufa[:, cc, :], x1b[cc, :, tc * 512:tc * 512 + 513], (snm, 16))
            xr_, xk_ = xrb[tc % 2], xkb[tc % 2]
            rb_, ka_ = rbf[tc % 2], kab[tc % 2]
            p.wait("vector", (snm, p.n(snm)))
            if tc >= 2:
                p.wait("vector", ("mm", mm_tc[tc - 2]))
            for cc in range(CC):
                cur = bufa[:, cc, 1:513]
                prv = bufa[:, cc, 0:512]
                p.op("vector", "tensor_tensor", dbuf, cur, prv, ALU.subtract)
                if tc == 4:
                    p.op("vector", "tensor_scalar", tcol[:, cc:cc + 1], bufa[:, cc, 0:1], bvb, None, ALU.mult)
                    p.op("vector", "tensor_tensor", dbuf[:, 0:1], bufa[:, cc, 1:2], tcol[:, cc:cc + 1], ALU.subtract)
                p.op("vector", "scalar_tensor_tensor", xr_[:, cc, :], dbuf, murb[:, cc:cc + 1], prv, ALU.mult, ALU.add)
                if tc == 4:
                    p.op("vector", "tensor_tensor", tcol2[:, cc:cc + 1], dbuf[:, 0:1], murb[:, cc:cc + 1], ALU.mult)
                    p.op("vector", "tensor_tensor", xr_[:, cc, 0:1], tcol2[:, cc:cc + 1], tcol[:, cc:cc + 1], ALU.add)
                p.op("vector", "scalar_tensor_tensor", xk_[:, cc, :], dbuf, mukb[:, cc:cc + 1], prv, ALU.mult, ALU.add)
                if tc == 4:
                    p.op("vector", "tensor_tensor", tcol2[:, cc:cc + 1], dbuf[:, 0:1], mukb[:, cc:cc + 1], ALU.mult)
                    p.op("vector", "tensor_tensor", xk_[:, cc, 0:1], tcol2[:, cc:cc + 1], tcol[:, cc:cc + 1], ALU.add)
            xr_done[tc] = p.n("vS")
            p.wait("tensor", ("vS", xr_done[tc]))
            for dt in range(CC):
                bank = dt % 4
                if psfree[bank] is not None:
                    p.wait("tensor", psfree[bank])
                for cc in range(CC):
                    p.op("tensor", "matmul", pB[bank], w_a[:, cc, dt * 128:(dt + 1) * 128],
                         xr_[:, cc, :], start=(cc == 0), stop=(cc == CC - 1),
                         inc=("mm", 1) if cc == CC - 1 else None)
                p.wait("scalar", ("mm", p.n("mm")))
                p.op("scalar", "activation", rb_[:, dt, :], pB[bank], AF.Sigmoid)
                psfree[bank] = ("sS", p.n("sS"))
            for dt in range(CC):
                bank = dt % 4
                if psfree[bank] is not None:
                    p.wait("tensor", psfree[bank])
                for cc in range(CC):
                    p.op("tensor", "matmul", pB[bank], w_b[:, cc, dt * 128:(dt + 1) * 128],
                         xk_[:, cc, :], start=(cc == 0), stop=(cc == CC - 1),
                         inc=("mm", 1) if cc == CC - 1 else None)
                rt_ = relu_t if dt % 2 == 0 else relu_t2
                if dt >= 2:
                    p.wait("vector", ("sS", p.n("sS")))
                p.wait("vector", ("mm", p.n("mm")))
                p.op("vector", "tensor_scalar", rt_, pB[bank], 0.0, None, ALU.max)
                psfree[bank] = ("vS", p.n("vS"))
                p.wait("scalar", ("vS", p.n("vS")))
                p.op("scalar", "activation", ka_[:, dt, :], rt_, AF.Square)
            kadone = p.n("sS")
            p.wait("tensor", ("sS", kadone))
            vsD = p.n("vS")
            for dt in range(CC):
                bank = dt % 4
                if psfree[bank] is not None:
                    p.wait("tensor", psfree[bank])
                for cc in range(CC):
                    p.op("tensor", "matmul", pB[bank], w_c[:, cc, dt * 128:(dt + 1) * 128],
                         ka_[:, cc, :], start=(cc == 0), stop=(cc == CC - 1),
                         inc=("mm", 1) if cc == CC - 1 else None)
                if dt >= 2:
                    p.wait("sync", ("vS", vsD + 2 * (dt - 1)))
                elif tc > 0:
                    p.wait("sync", ("vS", vsD))
                p.dma("sync", xres if dt % 2 == 0 else xres2,
                      x1[dt * 128:(dt + 1) * 128, tc * 512:(tc + 1) * 512],
                      ("sxa" if dt % 2 == 0 else "sxb", 16))
                p.wait("vector", ("mm", p.n("mm")),
                       ("sxa" if dt % 2 == 0 else "sxb", p.n("sxa" if dt % 2 == 0 else "sxb")))
                p.op("vector", "tensor_tensor", dbuf, pB[bank], rb_[:, dt, :], ALU.mult)
                p.op("vector", "tensor_tensor", s_f[:, dt, :], dbuf,
                     xres if dt % 2 == 0 else xres2, ALU.add)
                psfree[bank] = ("vS", p.n("vS"))
            p.wait("tensor", ("vS", p.n("vS")))
            p.wait("scalar", ("vS", p.n("vS")))
            emit_ln(p, s_f, s_f, lnb)
            if tc >= 4 or fused:
                oc0 = tc * 512 if fused else (tc - 4) * 512
                p.wait("gpsimd", ("vS", p.n("vS")))
                for dt in range(CC):
                    p.dma("gpsimd", x2[dt * 128:(dt + 1) * 128, oc0:oc0 + 512],
                          s_f[:, dt, :], ("so", 16))
                p.wait("vector", ("so", p.n("so")))
            mm_tc[tc] = p.n("mm")
        p.wait("gpsimd", ("so", p.n("so")))


def build_phase2():
    nc = bass.Bass("TRN2", target_bir_lowering=False)
    ctx = contextlib.ExitStack()
    x1 = din(nc, "x1", [D, T])
    wr = din(nc, "wr", [D, D])
    wk = din(nc, "wk", [D, D])
    wv = din(nc, "wv", [D, D])
    mur = din(nc, "mur", [128, CC])
    muk = din(nc, "muk", [128, CC])
    bval = din(nc, "bval", [128, 1])
    ocf = din(nc, "ocf", [128, 1])
    orf = din(nc, "orf", [1, 128])
    x2 = dout(nc, "x2", [D, HALF])

    x1b = dscr(nc, "x1b", [CC, 128, T + 1])
    xr_s = dscr(nc, "xr_s", [CC, 128, T])
    xk_s = dscr(nc, "xk_s", [CC, 128, T])
    r_s = dscr(nc, "r_s", [CC, 128, T])
    ka_s = dscr(nc, "ka_s", [CC, 128, T])

    with ctx:
        p = Plan(nc, ctx)
        p.sem("ld0", "ga", "gw", "gw2", "sa", "sb", "sr", "sxa", "sxb", "stB", "stC", "stK", "stV", "stV2", "ka", "kb", "kc2", "kd", "va", "vb", "vc", "vd", "so", "mm", "vS", "sS", "gS", "g1a", "g1b", "g2a", "g2b", "sxw")
        emit_phase2(p, nc, x1, wr, wk, wv, mur, muk, bval, ocf, orf, x2, x1b, xr_s, xk_s, r_s, ka_s)
        p.build()
    return nc


def emit_phase3(p, nc, x2, wqm, wkm, wvm, wom, bqc, bkc, bvr, boc, sinq, cosq, sink, cosk, pbias, zbias, trim, ocf, orf, x3, x2b, k_scr, v_scr, fused=False):
    ctx = contextlib.ExitStack()
    with ctx:
        cbuf = sb(ctx, nc, "cbuf", [128, T], BF16)
        wqb = sb(ctx, nc, "wqb", [128, CC, D], BF16)
        wkb = sb(ctx, nc, "wkb", [128, CC, D], BF16)
        wvb = sb(ctx, nc, "wvb", [128, CC, D], BF16)
        wob = sb(ctx, nc, "wob", [128, CC, D], BF16)
        bqb = sb(ctx, nc, "bqb", [128, CC], F32)
        bkb = sb(ctx, nc, "bkb", [128, CC], F32)
        bvb = sb(ctx, nc, "bvb", [1, D], BF16)
        bob = sb(ctx, nc, "bob", [128, CC], F32)
        sqb = sb(ctx, nc, "sqb", [32, HALF], BF16)
        cqb = sb(ctx, nc, "cqb", [32, HALF], BF16)
        skb = sb(ctx, nc, "skb", [32, T], BF16)
        ckb = sb(ctx, nc, "ckb", [32, T], BF16)
        pbb = sb(ctx, nc, "pbb", [128, 1], F32)
        zbb = sb(ctx, nc, "zbb", [128, 1], F32)
        trib = sb(ctx, nc, "trib", [128, 128], BF16)
        orb = sb(ctx, nc, "orb", [1, 128], BF16)
        o64 = sb(ctx, nc, "o64", [1, 64], F32)
        ocfb = sb(ctx, nc, "ocfb", [128, 1], F32)
        orfb = sb(ctx, nc, "orfb", [1, 128], F32)
        xwk = sb(ctx, nc, "xwk", [128, CC, 512], BF16)
        xwk2 = sb(ctx, nc, "xwk2", [128, CC, 512], BF16)
        kraw = sb(ctx, nc, "kraw", [64, 2, 512], BF16)
        krot = sb(ctx, nc, "krot", [64, 2, 512], BF16)
        visb = sb(ctx, nc, "visb", [128, H * 65], BF16)
        visb2 = sb(ctx, nc, "visb2", [128, H * 65], BF16)
        qbuf = sb(ctx, nc, "qbuf", [64, H, 512], BF16)
        kchs = [sb(ctx, nc, f"kch{i}", [64, 4, 128], BF16) for i in range(4)]
        vchs = [sb(ctx, nc, f"vch{i}", [128, 4 * 65], BF16) for i in range(4)]
        prbufs = [sb(ctx, nc, f"prbx{i}", [128, 512], BF16) for i in range(8)]
        rcrow = sb(ctx, nc, "rcrow", [1, 512], F32)
        rcs = sb(ctx, nc, "rcs", [64, 512], BF16)
        comb = sb(ctx, nc, "comb", [128, CC, 512], BF16)
        s_f = sb(ctx, nc, "s_f", [128, CC, 512], F32)
        xres = sb(ctx, nc, "xres", [128, 512], F32)
        xres2 = sb(ctx, nc, "xres2", [128, 512], F32)
        sq = sb(ctx, nc, "sq", [128, 512], F32)
        sq2 = sb(ctx, nc, "sq2", [128, 512], F32)
        rows = [sb(ctx, nc, f"row{i}", [1, 512], F32) for i in range(4)]
        pA = [psum(ctx, nc, f"pA{i}", [128, 512]) for i in range(2)]
        pS = [psum(ctx, nc, f"pS{i}", [128, 512]) for i in range(2)]
        soP = [psum(ctx, nc, f"soP{i}", [65, 512]) for i in range(4)]
        lnb = dict(psS=(soP[0][0:1, :], soP[1][0:1, :]), psB=(pA[0], pA[1]),
                   sq=(sq, sq2), rows=rows, ones_col_f=ocfb, ones_row_f=orfb)

        for dst, src in ((bqb, bqc), (bkb, bkc), (bob, boc), (pbb, pbias), (zbb, zbias),
                         (ocfb, ocf), (orfb, orf)):
            p.dma("sync", dst, src, ("ld0", 16))
        for dst, src in ((bvb, bvr), (sqb, sinq), (cqb, cosq), (skb, sink), (ckb, cosk),
                         (trib, trim), (orb, orf)):
            p.dma("gpsimd", dst, src, ("ga", 16))
        p.gop("memset", o64, 1.0)
        p.op("vector", "memset", visb.rearrange("q (h m) -> q h m", m=65)[:, :, 64:65], 1.0)
        p.op("vector", "memset", visb2.rearrange("q (h m) -> q h m", m=65)[:, :, 64:65], 1.0)
        for cc in range(CC):
            p.dma("gpsimd", cbuf, x2[cc * 128:(cc + 1) * 128, :], ("ga", 16))
            p.wait("gpsimd", ("ga", p.n("ga")))
            p.dma("gpsimd", x2b[cc, :, :], cbuf, ("ga", 16))
            p.wait("gpsimd", ("ga", p.n("ga")))
        gaA = p.n("ga")
        for cc in range(CC):
            p.dma("gpsimd", wqb[:, cc, :], wqm[cc * 128:(cc + 1) * 128, :], ("gw", 16))
            p.dma("gpsimd", wkb[:, cc, :], wkm[cc * 128:(cc + 1) * 128, :], ("gw", 16))
            p.dma("gpsimd", wvb[:, cc, :], wvm[cc * 128:(cc + 1) * 128, :], ("gw", 16))
            p.dma("gpsimd", wob[:, cc, :], wom[cc * 128:(cc + 1) * 128, :], ("gw", 16))
        gwT = p.n("gw")

        def rope(dst, srcb, sint, cost, col0, w):
            for hh in range(2):
                d_ = krot if dst is None else dst
                k1 = srcb[0:32, hh, :]
                ss = sint[:, col0:col0 + w]
                cs = cost[:, col0:col0 + w]
                ta = sq[0:32, 0:w]
                tb = sq2[0:32, 0:w]
                t0 = rcs[0:32, 0:w]
                p.op("vector", "tensor_copy", t0, srcb[32:64, hh, :])
                p.op("vector", "tensor_tensor", ta, cs, k1, ALU.mult)
                p.op("vector", "tensor_tensor", tb, ss, t0, ALU.mult)
                p.op("vector", "tensor_tensor", d_[0:32, hh, :], ta, tb, ALU.subtract)
                p.op("vector", "tensor_tensor", ta, ss, k1, ALU.mult)
                p.op("vector", "tensor_tensor", tb, cs, t0, ALU.mult)
                p.op("vector", "tensor_tensor", ta, ta, tb, ALU.add)
                p.op("vector", "tensor_copy", d_[32:64, hh, :], ta)

        p.wait("sync", ("ga", gaA), ("ld0", p.n("ld0")))
        p.wait("tensor", ("gw", gwT), ("ga", gaA))
        vs_s1 = {}
        for tcx in range(CC):
            bufa = xwk if tcx % 2 == 0 else xwk2
            snm = "sa" if tcx % 2 == 0 else "sb"
            if tcx >= 2:
                p.wait("sync", ("mm", vs_s1[tcx - 2]))
            for cc in range(CC):
                p.dma("sync", bufa[:, cc, :], x2b[cc, :, tcx * 512:(tcx + 1) * 512], (snm, 16))
            p.wait("tensor", (snm, p.n(snm)))
            for dt in range(CC):
                pa = pA[dt % 2]
                p.wait("tensor", ("sS", p.n("sS")))
                for cc in range(CC):
                    p.op("tensor", "matmul", pa, wkb[:, cc, dt * 128:(dt + 1) * 128],
                         bufa[:, cc, :], start=(cc == 0), stop=(cc == CC - 1),
                         inc=("mm", 1) if cc == CC - 1 else None)
                p.wait("scalar", ("mm", p.n("mm")), ("vS", p.n("vS")))
                p.op("scalar", "activation", kraw[0:64, 0, :], pa[0:64, :], AF.Identity,
                     bias=bkb[0:64, dt:dt + 1])
                p.op("scalar", "activation", kraw[0:64, 1, :], pa[64:128, :], AF.Identity,
                     bias=bkb[64:128, dt:dt + 1])
                p.wait("vector", ("sS", p.n("sS")), ("stK", p.n("stK")))
                rope(krot, kraw, skb, ckb, tcx * 512, 512)
                p.wait("gpsimd", ("vS", p.n("vS")))
                for kj in range(4):
                    p.dma("gpsimd",
                          k_scr.rearrange("a q (h n) -> a q h n", n=128)[tcx * 4 + kj, 0:64, 2 * dt:2 * dt + 2, :],
                          krot[:, :, kj * 128:(kj + 1) * 128], ("stK", 16))
            for tt4 in range(4):
                vb_ = visb if tt4 % 2 == 0 else visb2
                vsem = "stV" if tt4 % 2 == 0 else "stV2"
                p.wait("scalar", (vsem, p.n(vsem)))
                for dh2 in range(2):
                    pa = pA[dh2]
                    p.wait("tensor", ("sS", p.n("sS")))
                    for cc in range(CC):
                        p.op("tensor", "matmul", pa, bufa[:, cc, tt4 * 128:(tt4 + 1) * 128],
                             wvb[:, cc, dh2 * 512:(dh2 + 1) * 512],
                             start=(cc == 0), stop=False)
                    p.op("tensor", "matmul", pa, orb, bvb[:, dh2 * 512:(dh2 + 1) * 512],
                         start=False, stop=True, inc=("mm", 1))
                    p.wait("scalar", ("mm", p.n("mm")))
                    vtgt = vb_.rearrange("q (h m) -> q h m", m=65)
                    p.op("scalar", "copy",
                         vtgt[:, dh2 * 8:(dh2 + 1) * 8, 0:64],
                         pa.rearrange("q (h m) -> q h m", m=64))
                p.wait("gpsimd", ("sS", p.n("sS")))
                p.dma("gpsimd", v_scr[tcx * 4 + tt4], vb_, (vsem, 16))
            vs_s1[tcx] = p.n("mm")
        stK_T, stV_T, stV2_T = p.n("stK"), p.n("stV"), p.n("stV2")

        p.wait("sync", ("stK", stK_T), ("stV", stV_T), ("stV2", stV2_T))
        for g in range(4, 8):
            Kg = 4 * g + 4
            snm = "sa" if g % 2 == 0 else "sb"
            bufa = xwk if g % 2 == 0 else xwk2
            if g == 4:
                p.wait("sync", ("mm", p.n("mm")))
            else:
                p.wait("sync", ("mm", mm_gend))
            for cc in range(CC):
                p.dma("sync", bufa[:, cc, :], x2b[cc, :, g * 512:(g + 1) * 512], (snm, 16))
            p.wait("tensor", (snm, p.n(snm)), ("vS", p.n("vS")), ("sS", p.n("sS")))
            for dt in range(CC):
                pa = pA[dt % 2]
                if dt >= 2:
                    p.wait("tensor", ("vS", p.n("vS")))
                for cc in range(CC):
                    p.op("tensor", "matmul", pa, wqb[:, cc, dt * 128:(dt + 1) * 128],
                         bufa[:, cc, :], start=(cc == 0), stop=(cc == CC - 1),
                         inc=("mm", 1) if cc == CC - 1 else None)
                p.wait("scalar", ("mm", p.n("mm")), ("vS", p.n("vS")))
                p.op("scalar", "activation", kraw[0:64, 0, :], pa[0:64, :], AF.Identity,
                     bias=bqb[0:64, dt:dt + 1])
                p.op("scalar", "activation", kraw[0:64, 1, :], pa[64:128, :], AF.Identity,
                     bias=bqb[64:128, dt:dt + 1])
                p.wait("vector", ("sS", p.n("sS")))
                rope(qbuf.rearrange("q (a h) w -> q a h w", a=1)[:, 0, 2 * dt:2 * dt + 2, :],
                     kraw, sqb, cqb, (g - 4) * 512, 512)
            qdone = p.n("vS")
            scfree = [None, None, None, None]
            prfree = {(i, j): None for i in range(4) for j in range(2)}
            pready = {}
            av_mm = {}
            for hb in range(4):
                for kc in range(Kg):
                    knm = ("ka", "kb", "kc2", "kd")[kc % 4]
                    vnm = ("va", "vb", "vc", "vd")[kc % 4]
                    kb_ = kchs[kc % 4]
                    vb_ = vchs[kc % 4]
                    if kc >= 4:
                        p.wait("sync", ("mm", av_mm[kc - 4]))
                    elif hb > 0 or g > 4:
                        p.wait("sync", ("mm", mm_kprev))
                    p.dma("sync", kb_.rearrange("q a n -> q (a n)"),
                          k_scr[kc, 0:64, hb * 512:(hb + 1) * 512], (knm, 16))
                    p.dma("sync", vb_, v_scr[kc, :, hb * 260:(hb + 1) * 260], (vnm, 16))
                    p.wait("tensor", (knm, p.n(knm)), (vnm, p.n(vnm)), ("vS", qdone))
                    dk0 = Kg - 4
                    scb = [pS[0], pS[1], pA[0], pA[1]]
                    for hin in range(4):
                        h = hb * 4 + hin
                        if scfree[hin] is not None:
                            p.wait("tensor", scfree[hin])
                        p.op("tensor", "matmul", scb[hin], kb_[0:64, hin, :], qbuf[0:64, h, :],
                             start=True, stop=True, inc=("mm", 1))
                        mmsc = p.n("mm")
                        if prfree[(hin, kc % 2)] is not None:
                            p.wait("scalar", prfree[(hin, kc % 2)])
                        p.wait("scalar", ("mm", mmsc))
                        prbh = prbufs[hin * 2 + kc % 2]
                        p.op("scalar", "activation", prbh, scb[hin], AF.Exp, scale=0.125,
                             bias=pbb if kc < 16 else zbb)
                        scfree[hin] = ("sS", p.n("sS"))
                        pready[(hin, kc % 2)] = ("sS", p.n("sS"))
                        if kc >= dk0:
                            jd = kc - dk0
                            p.wait("vector", ("sS", p.n("sS")))
                            if jd > 0:
                                p.op("vector", "memset", prbh[:, 0:jd * 128], 0.0)
                            p.op("vector", "tensor_tensor", prbh[:, jd * 128:(jd + 1) * 128],
                                 prbh[:, jd * 128:(jd + 1) * 128], trib, ALU.mult)
                            pready[(hin, kc % 2)] = ("vS", p.n("vS"))
                    if kc > 0:
                        vprev = vchs[(kc - 1) % 4]
                        for hin in range(4):
                            p.wait("tensor", pready[(hin, (kc - 1) % 2)])
                            p.op("tensor", "matmul", soP[hin],
                                 vprev[:, hin * 65:(hin + 1) * 65],
                                 prbufs[hin * 2 + (kc - 1) % 2],
                                 start=(kc - 1 == 0), stop=False, inc=("mm", 1))
                            prfree[(hin, (kc - 1) % 2)] = ("mm", p.n("mm"))
                        av_mm[kc - 1] = p.n("mm")
                    if kc == Kg - 1:
                        vlast = vchs[kc % 4]
                        for hin in range(4):
                            p.wait("tensor", pready[(hin, kc % 2)])
                            p.op("tensor", "matmul", soP[hin],
                                 vlast[:, hin * 65:(hin + 1) * 65],
                                 prbufs[hin * 2 + kc % 2],
                                 start=(kc == 0), stop=True, inc=("mm", 1))
                            prfree[(hin, kc % 2)] = ("mm", p.n("mm"))
                        av_mm[kc] = p.n("mm")
                mm_kprev = p.n("mm")
                p.wait("vector", ("mm", p.n("mm")))
                for hin in range(4):
                    h = hb * 4 + hin
                    p.op("vector", "reciprocal", rcrow, soP[hin][64:65, :])
                    p.wait("tensor", ("vS", p.n("vS")))
                    if scfree[0] is not None:
                        p.wait("tensor", scfree[0])
                    p.op("tensor", "matmul", pS[0][0:64, :], o64, rcrow, start=True, stop=True,
                         inc=("mm", 1))
                    p.wait("scalar", ("mm", p.n("mm")))
                    p.op("scalar", "copy", rcs, pS[0][0:64, :])
                    scfree[0] = ("sS", p.n("sS"))
                    p.wait("vector", ("sS", p.n("sS")))
                    p.op("vector", "tensor_tensor",
                         comb[(h % 2) * 64:(h % 2) * 64 + 64, h // 2, :],
                         soP[hin][0:64, :], rcs, ALU.mult)
                vs_hbend = p.n("vS")
                p.wait("tensor", ("vS", vs_hbend))
            vs_pre = p.n("vS")
            for dt2 in range(CC):
                pa = pA[dt2 % 2]
                if dt2 >= 2:
                    p.wait("tensor", ("vS", vs_pre + dt2 - 1))
                for cc in range(CC):
                    p.op("tensor", "matmul", pa, wob[:, cc, dt2 * 128:(dt2 + 1) * 128],
                         comb[:, cc, :], start=(cc == 0), stop=(cc == CC - 1),
                         inc=("mm", 1) if cc == CC - 1 else None)
                if dt2 >= 2:
                    p.wait("sync", ("vS", vs_pre + dt2 - 1))
                elif g > 4:
                    p.wait("sync", ("vS", vs_pre))
                p.dma("sync", xres if dt2 % 2 == 0 else xres2,
                      x2[dt2 * 128:(dt2 + 1) * 128, g * 512:(g + 1) * 512],
                      ("sxa" if dt2 % 2 == 0 else "sxb", 16))
                p.wait("vector", ("mm", p.n("mm")),
                       ("sxa" if dt2 % 2 == 0 else "sxb", p.n("sxa" if dt2 % 2 == 0 else "sxb")))
                p.op("vector", "scalar_tensor_tensor", s_f[:, dt2, :], pa,
                     bob[:, dt2:dt2 + 1], xres if dt2 % 2 == 0 else xres2, ALU.add, ALU.add)
            p.wait("tensor", ("vS", p.n("vS")))
            p.wait("scalar", ("vS", p.n("vS")))
            emit_ln(p, s_f, s_f, lnb)
            p.wait("gpsimd", ("vS", p.n("vS")))
            for dt in range(CC):
                p.dma("gpsimd", x3[dt * 128:(dt + 1) * 128, (g - 4) * 512:(g - 3) * 512],
                      s_f[:, dt, :], ("so", 16))
            p.wait("vector", ("so", p.n("so")))
            mm_gend = p.n("mm")
        p.wait("gpsimd", ("so", p.n("so")))


def build_phase3():
    nc = bass.Bass("TRN2", target_bir_lowering=False)
    ctx = contextlib.ExitStack()
    x2 = din(nc, "x2", [D, T])
    wqm = din(nc, "wqm", [D, D])
    wkm = din(nc, "wkm", [D, D])
    wvm = din(nc, "wvm", [D, D])
    wom = din(nc, "wom", [D, D])
    bqc = din(nc, "bqc", [128, CC])
    bkc = din(nc, "bkc", [128, CC])
    bvr = din(nc, "bvr", [1, D])
    boc = din(nc, "boc", [128, CC])
    sinq = din(nc, "sinq", [32, HALF])
    cosq = din(nc, "cosq", [32, HALF])
    sink = din(nc, "sink", [32, T])
    cosk = din(nc, "cosk", [32, T])
    pbias = din(nc, "pbias", [128, 1])
    zbias = din(nc, "zbias", [128, 1])
    trim = din(nc, "trim", [128, 128])
    ocf = din(nc, "ocf", [128, 1])
    orf = din(nc, "orf", [1, 128])
    x3 = dout(nc, "x3", [D, HALF])
    x2b = dscr(nc, "x2b", [CC, 128, T])
    k_scr = dscr(nc, "k_scr", [NT, 64, H * 128])
    v_scr = dscr(nc, "v_scr", [NT, 128, H * 65])

    with ctx:
        p = Plan(nc, ctx)
        p.sem("ld0", "ga", "gw", "gw2", "sa", "sb", "sr", "sxa", "sxb", "stB", "stC", "stK", "stV", "stV2", "ka", "kb", "kc2", "kd", "va", "vb", "vc", "vd", "so", "mm", "vS", "sS", "gS", "g1a", "g1b", "g2a", "g2b", "sxw")
        emit_phase3(p, nc, x2, wqm, wkm, wvm, wom, bqc, bkc, bvr, boc, sinq, cosq, sink, cosk, pbias, zbias, trim, ocf, orf, x3, x2b, k_scr, v_scr)
        p.build()
    return nc


def emit_phase4(p, nc, x3, wrt, w1t, b1c, w2t, b2m, ws1t, bs1c, ws2t, bs2c, ident, oneh, ocf, orf, x4, x3b, fused=False):
    ctx = contextlib.ExitStack()
    with ctx:
        cbuf = sb(ctx, nc, "cbuf", [128, HALF], BF16)
        wrtb = sb(ctx, nc, "wrtb", [128, CC, NE], BF16)
        b1cb = sb(ctx, nc, "b1cb", [128, NE * 32], F32)
        bs1cb = sb(ctx, nc, "bs1cb", [128, 32], F32)
        bs2cb = sb(ctx, nc, "bs2cb", [128, CC], F32)
        b2b = sb(ctx, nc, "b2b", [NE, D], BF16)
        idb = sb(ctx, nc, "idb", [128, 128], BF16)
        onehb = sb(ctx, nc, "onehb", [NE, NE * 128], BF16)
        orb = sb(ctx, nc, "orb", [1, 128], BF16)
        ocfb = sb(ctx, nc, "ocfb", [128, 1], F32)
        orfb = sb(ctx, nc, "orfb", [1, 128], F32)
        gatesT = sb(ctx, nc, "gatesT", [NE, HALF], BF16)
        gb = sb(ctx, nc, "gb", [128, 1024], BF16)
        probs = sb(ctx, nc, "probs", [128, NE], F32)
        pm = sb(ctx, nc, "pm", [128, NE], F32)
        keep = sb(ctx, nc, "keep", [128, NE], F32)
        gcol = sb(ctx, nc, "gcol", [128, 1], F32)
        gcol2 = sb(ctx, nc, "gcol2", [128, 1], F32)
        gbf = sb(ctx, nc, "gbf", [128, NE], BF16)
        xw3 = sb(ctx, nc, "xw3", [128, CC, HYT], BF16)
        w1blk = sb(ctx, nc, "w1blk", [128, CC, 128], BF16)
        w1blk2 = sb(ctx, nc, "w1blk2", [128, CC, 128], BF16)
        w2blk = sb(ctx, nc, "w2blk", [128, 32, 128], BF16)
        w2blk2 = sb(ctx, nc, "w2blk2", [128, 32, 128], BF16)
        hE = sb(ctx, nc, "hE", [128, 32, HYT], BF16)
        htmp = sb(ctx, nc, "htmp", [128, 512], BF16)
        htmp2 = sb(ctx, nc, "htmp2", [128, 512], BF16)
        acc = sb(ctx, nc, "acc", [128, CC, HYT], F32)
        s_f = sb(ctx, nc, "s_f", [128, CC, 512], F32)
        xres = sb(ctx, nc, "xres", [128, 512], F32)
        xres2 = sb(ctx, nc, "xres2", [128, 512], F32)
        sq = sb(ctx, nc, "sq", [128, 512], F32)
        sq2 = sb(ctx, nc, "sq2", [128, 512], F32)
        rows = [sb(ctx, nc, f"row{i}", [1, 512], F32) for i in range(4)]
        pA = [psum(ctx, nc, f"pA{i}", [128, 512]) for i in range(4)]
        psS0 = psum(ctx, nc, "psS0", [1, 512])
        psS1 = psum(ctx, nc, "psS1", [1, 512])
        psT = psum(ctx, nc, "psT", [8, 128], BF16)
        lnb = dict(psS=(psS0, psS1), psB=(pA[0], pA[1]), sq=(sq, sq2),
                   rows=rows, ones_col_f=ocfb, ones_row_f=orfb)

        for dst, src in ((b1cb, b1c), (bs1cb, bs1c), (bs2cb, bs2c), (ocfb, ocf), (orfb, orf)):
            p.dma("sync", dst, src, ("ld0", 16))
        p.dma("gpsimd", idb, ident, ("ga", 16))
        p.dma("gpsimd", onehb, oneh, ("ga", 16))
        p.dma("gpsimd", orb, orf, ("ga", 16))
        p.dma("gpsimd", b2b, b2m, ("ga", 16))
        for cc in range(CC):
            p.dma("gpsimd", wrtb[:, cc, :], wrt[cc * 128:(cc + 1) * 128, :], ("ga", 16))
        for cc in range(CC):
            p.dma("gpsimd", cbuf, x3[cc * 128:(cc + 1) * 128, :], ("ga", 16))
            p.wait("gpsimd", ("ga", p.n("ga")))
            p.dma("gpsimd", x3b[cc, :, :], cbuf, ("ga", 16))
            p.wait("gpsimd", ("ga", p.n("ga")))
        gaA = p.n("ga")

        p.wait("sync", ("ga", gaA), ("ld0", p.n("ld0")))
        mm_hyend = 0
        psfree = [None, None, None, None]
        hfree = [None, None]
        for hy in range(NH):
            if hy > 0:
                p.wait("sync", ("mm", mm_hyend))
            for cc in range(CC):
                p.dma("sync", xw3[:, cc, :], x3b[cc, :, hy * HYT:(hy + 1) * HYT], ("sxw", 16))
            ldw = p.n("sxw")
            p.wait("tensor", ("sxw", ldw))
            for tt in range(HYT // 128):
                pr = pA[0][:, 0:NE]
                p.wait("tensor", ("vS", p.n("vS")), ("sS", p.n("sS")))
                if psfree[0] is not None:
                    p.wait("tensor", psfree[0])
                    psfree[0] = None
                for cc in range(CC):
                    p.op("tensor", "matmul", pr, xw3[:, cc, tt * 128:(tt + 1) * 128],
                         wrtb[:, cc, :], start=(cc == 0), stop=(cc == CC - 1),
                         inc=("mm", 1) if cc == CC - 1 else None)
                p.wait("vector", ("mm", p.n("mm")))
                p.op("vector", "tensor_reduce", gcol, pr, AX.X, ALU.max, negate=True)
                p.wait("scalar", ("vS", p.n("vS")))
                p.op("scalar", "activation", probs, pr, AF.Exp, bias=gcol)
                p.wait("vector", ("sS", p.n("sS")))
                p.op("vector", "tensor_reduce", gcol, probs, AX.X, ALU.add)
                p.op("vector", "reciprocal", gcol2, gcol)
                p.op("vector", "tensor_scalar", probs, probs, gcol2, None, ALU.mult)
                p.op("vector", "tensor_reduce", gcol, probs, AX.X, ALU.max)
                p.op("vector", "tensor_scalar", keep, probs, gcol, None, ALU.is_ge)
                p.op("vector", "scalar_tensor_tensor", pm, keep, -2.0, probs, ALU.mult, ALU.add)
                p.op("vector", "tensor_reduce", gcol, pm, AX.X, ALU.max)
                p.op("vector", "tensor_scalar", keep, probs, gcol, None, ALU.is_ge)
                p.op("vector", "tensor_tensor", gbf, probs, keep, ALU.mult)
                p.wait("tensor", ("vS", p.n("vS")))
                p.op("tensor", "transpose", psT, gbf, idb, inc=("mm", 1))
                p.wait("scalar", ("mm", p.n("mm")))
                p.op("scalar", "copy", gatesT[:, hy * HYT + tt * 128: hy * HYT + (tt + 1) * 128], psT)
            gts = p.n("sS")
            for e in range(NE + 1):
                if e < NE:
                    p.wait("tensor", ("sS", gts), ("vS", p.n("vS")), ("sS", p.n("sS")))
                    if psfree[1] is not None:
                        p.wait("tensor", psfree[1])
                    p.op("tensor", "matmul", pA[1], onehb[:, e * 128:(e + 1) * 128],
                         gatesT[0:NE, hy * HYT: hy * HYT + 512],
                         start=True, stop=True, inc=("mm", 1))
                    p.wait("scalar", ("mm", p.n("mm")))
                    p.op("scalar", "copy", gb[:, 0:512], pA[1])
                    p.wait("tensor", ("sS", p.n("sS")))
                    p.op("tensor", "matmul", pA[1], onehb[:, e * 128:(e + 1) * 128],
                         gatesT[0:NE, hy * HYT + 512: hy * HYT + 1024],
                         start=True, stop=True, inc=("mm", 1))
                    p.wait("scalar", ("mm", p.n("mm")))
                    p.op("scalar", "copy", gb[:, 512:1024], pA[1])
                    psfree[1] = ("sS", p.n("sS"))
                mmA0 = p.n("mm")
                if e > 0:
                    p.wait("scalar", ("mm", mm_Bend))
                    p.wait("vector", ("mm", mm_Bend))
                gidx = 0
                for ft in range(32):
                    wb_ = w1blk if ft % 2 == 0 else w1blk2
                    gnm = "g1a" if ft % 2 == 0 else "g1b"
                    if ft >= 2:
                        p.wait("gpsimd", ("mm", mmA0 + (ft - 1) * 2))
                    elif e > 0 or hy > 0:
                        p.wait("gpsimd", ("mm", mmA0))
                    src = (w1t[e, ft] if e < NE else ws1t[ft]).rearrange("q (cc n) -> q cc n", n=128)
                    p.dma("gpsimd", wb_, src, (gnm, 16))
                    p.wait("tensor", (gnm, p.n(gnm)))
                    for th in range(2):
                        bank = gidx % 4
                        ph = pA[bank]
                        if psfree[bank] is not None:
                            p.wait("tensor", psfree[bank])
                        for cc in range(CC):
                            p.op("tensor", "matmul", ph, wb_[:, cc, :],
                                 xw3[:, cc, th * 512:(th + 1) * 512],
                                 start=(cc == 0), stop=(cc == CC - 1),
                                 inc=("mm", 1) if cc == CC - 1 else None)
                        bias = b1cb[:, e * 32 + ft: e * 32 + ft + 1] if e < NE else bs1cb[:, ft:ft + 1]
                        p.wait("scalar", ("mm", p.n("mm")))
                        if e < NE:
                            hb_ = htmp if gidx % 2 == 0 else htmp2
                            if hfree[gidx % 2] is not None:
                                p.wait("scalar", hfree[gidx % 2])
                            p.op("scalar", "activation", hb_, ph, AF.Gelu_apprx_tanh, bias=bias)
                            psfree[bank] = ("sS", p.n("sS"))
                            p.wait("vector", ("sS", p.n("sS")))
                            p.op("vector", "tensor_tensor", hE[:, ft, th * 512:(th + 1) * 512],
                                 hb_, gb[:, th * 512:(th + 1) * 512], ALU.mult)
                            hfree[gidx % 2] = ("vS", p.n("vS"))
                        else:
                            p.op("scalar", "activation", hE[:, ft, th * 512:(th + 1) * 512],
                                 ph, AF.Gelu_apprx_tanh, bias=bias)
                            psfree[bank] = ("sS", p.n("sS"))
                        gidx += 1
                hdone_v, hdone_s = p.n("vS"), p.n("sS")
                mmB0 = p.n("mm")
                gidx = 0
                for dt in range(CC):
                    wb2 = w2blk if dt % 2 == 0 else w2blk2
                    gnm = "g2a" if dt % 2 == 0 else "g2b"
                    if dt >= 2:
                        p.wait("gpsimd", ("mm", mmB0 + (dt - 1) * 2))
                    elif e > 0 or hy > 0:
                        p.wait("gpsimd", ("mm", mmB0))
                    src = w2t[e, dt] if e < NE else ws2t[dt]
                    p.dma("gpsimd", wb2, src, (gnm, 16))
                    p.wait("tensor", (gnm, p.n(gnm)), ("vS", hdone_v), ("sS", hdone_s))
                    for th in range(2):
                        bank = gidx % 4
                        ph = pA[bank]
                        gidx += 1
                        if psfree[bank] is not None:
                            p.wait("tensor", psfree[bank])
                        for ft in range(32):
                            p.op("tensor", "matmul", ph, wb2[:, ft, :],
                                 hE[:, ft, th * 512:(th + 1) * 512],
                                 start=(ft == 0), stop=(e != 0 and ft == 31),
                                 inc=("mm", 1) if (e != 0 and ft == 31) else None)
                        if e == 0:
                            p.op("tensor", "matmul", ph, b2b[0:NE, dt * 128:(dt + 1) * 128],
                                 gatesT[0:NE, hy * HYT + th * 512: hy * HYT + (th + 1) * 512],
                                 start=False, stop=True, inc=("mm", 1))
                        p.wait("vector", ("mm", p.n("mm")))
                        asl = acc[:, dt, th * 512:(th + 1) * 512]
                        if e == 0:
                            p.op("vector", "tensor_copy", asl, ph)
                        elif e < NE:
                            p.op("vector", "tensor_tensor", asl, asl, ph, ALU.add)
                        else:
                            p.op("vector", "scalar_tensor_tensor", sq, ph, bs2cb[:, dt:dt + 1], asl,
                                 ALU.add, ALU.add)
                            p.op("vector", "tensor_copy", asl, sq)
                        psfree[bank] = ("vS", p.n("vS"))
                mm_Bend = p.n("mm")
            mm_hyend = p.n("mm")
            for j in range(2):
                col0 = hy * HYT + j * 512
                vs_pre = p.n("vS")
                for dt in range(CC):
                    if dt >= 2:
                        p.wait("sync", ("vS", vs_pre + dt - 1))
                    elif hy > 0 or j > 0:
                        p.wait("sync", ("vS", vs_pre))
                    p.dma("sync", xres if dt % 2 == 0 else xres2,
                          x3[dt * 128:(dt + 1) * 128, col0:col0 + 512],
                          ("sxa" if dt % 2 == 0 else "sxb", 16))
                    p.wait("vector", ("sxa" if dt % 2 == 0 else "sxb",
                                      p.n("sxa" if dt % 2 == 0 else "sxb")))
                    p.op("vector", "scalar_tensor_tensor", s_f[:, dt, :],
                         acc[:, dt, j * 512:(j + 1) * 512], MOE_SCALE,
                         xres if dt % 2 == 0 else xres2, ALU.mult, ALU.add)
                p.wait("tensor", ("vS", p.n("vS")))
                p.wait("scalar", ("vS", p.n("vS")))
                emit_ln(p, s_f, s_f, lnb)
                p.wait("gpsimd", ("vS", p.n("vS")))
                for dt in range(CC):
                    p.dma("gpsimd", x4[dt * 128:(dt + 1) * 128, col0:col0 + 512],
                          s_f[:, dt, :], ("so", 16))
                p.wait("vector", ("so", p.n("so")))
                del vs_pre
        p.wait("gpsimd", ("so", p.n("so")))


def build_phase4():
    nc = bass.Bass("TRN2", target_bir_lowering=False)
    ctx = contextlib.ExitStack()
    x3 = din(nc, "x3", [D, HALF])
    wrt = din(nc, "wrt", [D, NE])
    w1t = din(nc, "w1t", [NE, 32, 128, D])
    b1c = din(nc, "b1c", [128, NE * 32])
    w2t = din(nc, "w2t", [NE, CC, 128, 32, 128])
    b2m = din(nc, "b2m", [NE, D])
    ws1t = din(nc, "ws1t", [32, 128, D])
    bs1c = din(nc, "bs1c", [128, 32])
    ws2t = din(nc, "ws2t", [CC, 128, 32, 128])
    bs2c = din(nc, "bs2c", [128, CC])
    ident = din(nc, "ident", [128, 128])
    oneh = din(nc, "oneh", [NE, NE * 128])
    ocf = din(nc, "ocf", [128, 1])
    orf = din(nc, "orf", [1, 128])
    x4 = dout(nc, "x4", [D, HALF])
    x3b = dscr(nc, "x3b", [CC, 128, HALF])

    with ctx:
        p = Plan(nc, ctx)
        p.sem("ld0", "ga", "gw", "gw2", "sa", "sb", "sr", "sxa", "sxb", "stB", "stC", "stK", "stV", "stV2", "ka", "kb", "kc2", "kd", "va", "vb", "vc", "vd", "so", "mm", "vS", "sS", "gS", "g1a", "g1b", "g2a", "g2b", "sxw")
        emit_phase4(p, nc, x3, wrt, w1t, b1c, w2t, b2m, ws1t, bs1c, ws2t, bs2c, ident, oneh, ocf, orf, x4, x3b)
        p.build()
    return nc


def emit_phase1(p, nc, xin, wqkv, qbc, kbc, vbr, wquad, memv, memvbc, memg, memgbr, mscr, imix, declog, wout, obc, delta, negm, trow, dcol, signs, ident, oneh32, bval, pbias, zbias, trim, ocf, orf, x1o, xb, k_scr, v_scr, fused=False):
    ctx = contextlib.ExitStack()
    with ctx:
        cbuf = sb(ctx, nc, "cbuf", [128, T], BF16)
        zcol = sb(ctx, nc, "zcol", [128, 1], BF16)
        wkv = sb(ctx, nc, "wkv", [128, CC, 2 * D], BF16)
        wquadb = sb(ctx, nc, "wquadb", [128, CC, 256], BF16)
        memvb = sb(ctx, nc, "memvb", [128, CC, D], BF16)
        memgb = sb(ctx, nc, "memgb", [128, CC, H], BF16)
        woutb = sb(ctx, nc, "woutb", [128, CC, D], BF16)
        qbb = sb(ctx, nc, "qbb", [128, CC], F32)
        kbb = sb(ctx, nc, "kbb", [128, CC], F32)
        vbb = sb(ctx, nc, "vbb", [1, D], BF16)
        mvbb = sb(ctx, nc, "mvbb", [128, CC], F32)
        mgbb = sb(ctx, nc, "mgbb", [1, H], BF16)
        obb = sb(ctx, nc, "obb", [128, CC], F32)
        mscb = sb(ctx, nc, "mscb", [1, H], F32)
        imixb = sb(ctx, nc, "imixb", [1, 1], F32)
        declb = sb(ctx, nc, "declb", [1, H], F32)
        deltab = sb(ctx, nc, "deltab", [128, 128], F32)
        negmb = sb(ctx, nc, "negmb", [128, 128], F32)
        trowb = sb(ctx, nc, "trowb", [1, 128], F32)
        dcolb = sb(ctx, nc, "dcolb", [128, 1], F32)
        signsb = sb(ctx, nc, "signsb", [128, 96], F32)
        idb = sb(ctx, nc, "idb", [128, 128], BF16)
        oh32b = sb(ctx, nc, "oh32b", [32, 32 * 128], BF16)
        bvalb = sb(ctx, nc, "bvalb", [128, 1], F32)
        pbb = sb(ctx, nc, "pbb", [128, 1], F32)
        zbb = sb(ctx, nc, "zbb", [128, 1], F32)
        trib = sb(ctx, nc, "trib", [128, 128], BF16)
        orb = sb(ctx, nc, "orb", [1, 128], BF16)
        o64 = sb(ctx, nc, "o64", [1, 64], F32)
        ocfb = sb(ctx, nc, "ocfb", [128, 1], F32)
        orfb = sb(ctx, nc, "orfb", [1, 128], F32)
        Ah = sb(ctx, nc, "Ah", [128, H * 128], BF16)
        dvh = sb(ctx, nc, "dvh", [1, H * 128], BF16)
        drh = sb(ctx, nc, "drh", [128, H], BF16)
        d128 = sb(ctx, nc, "d128", [1, H], F32)
        lndr = sb(ctx, nc, "lndr", [1, H], F32)
        lndc = sb(ctx, nc, "lndc", [128, H], F32)
        alphc = sb(ctx, nc, "alphc", [128, 1], F32)
        mscbc = sb(ctx, nc, "mscbc", [128, H], F32)
        carry = sb(ctx, nc, "carry", [1, 576], BF16)
        q4 = sb(ctx, nc, "q4", [128, 256], F32)
        wl = sb(ctx, nc, "wl", [128, 96], F32)
        rd = sb(ctx, nc, "rdb", [128, 96], F32)
        jw = sb(ctx, nc, "jw", [128, 96], F32)
        t96a = sb(ctx, nc, "t96a", [128, 96], F32)
        t96b = sb(ctx, nc, "t96b", [128, 96], F32)
        t16a = sb(ctx, nc, "t16a", [128, 16], F32)
        t16b = sb(ctx, nc, "t16b", [128, 16], F32)
        t16c = sb(ctx, nc, "t16c", [128, 16], F32)
        outer = sb(ctx, nc, "outer", [128, 576], BF16)
        msb = sb(ctx, nc, "msb", [128, 576], F32)
        gated = sb(ctx, nc, "gated", [128, NT], F32)
        gT = sb(ctx, nc, "gT", [32, 128], BF16)
        gb = sb(ctx, nc, "gb", [128, 512], BF16)
        mvs = sb(ctx, nc, "mvs", [128, 512], BF16)
        xmw = sb(ctx, nc, "xmw", [128, CC, 129], BF16)
        xmw2 = sb(ctx, nc, "xmw2", [128, CC, 129], BF16)
        xwk = sb(ctx, nc, "xwk", [128, CC, 513], BF16)
        xwk2 = sb(ctx, nc, "xwk2", [128, CC, 513], BF16)
        kraw = sb(ctx, nc, "kraw", [64, 2, 512], BF16)
        visb = sb(ctx, nc, "visb", [128, H * 65], BF16)
        visb2 = sb(ctx, nc, "visb2", [128, H * 65], BF16)
        qbuf = sb(ctx, nc, "qbuf", [64, H, 512], BF16)
        kchs = [sb(ctx, nc, f"kch{i}", [64, 4, 128], BF16) for i in range(4)]
        vchs = [sb(ctx, nc, f"vch{i}", [128, 4 * 65], BF16) for i in range(4)]
        prbufs = [sb(ctx, nc, f"prbx{i}", [128, 512], BF16) for i in range(8)]
        rcrow = sb(ctx, nc, "rcrow", [1, 512], F32)
        rcs = sb(ctx, nc, "rcs", [64, 512], BF16)
        comb = sb(ctx, nc, "comb", [128, CC, 512], BF16)
        s_f = sb(ctx, nc, "s_f", [128, CC, 512], F32)
        xres = sb(ctx, nc, "xres", [128, 512], F32)
        xres2 = sb(ctx, nc, "xres2", [128, 512], F32)
        sq = sb(ctx, nc, "sq", [128, 512], F32)
        sq2 = sb(ctx, nc, "sq2", [128, 512], F32)
        rows = [sb(ctx, nc, f"row{i}", [1, 512], F32) for i in range(4)]
        pA = [psum(ctx, nc, f"pA{i}", [128, 512]) for i in range(2)]
        pS = [psum(ctx, nc, f"pS{i}", [128, 512]) for i in range(2)]
        soP = [psum(ctx, nc, f"soP{i}", [65, 512]) for i in range(4)]
        lnb = dict(psS=(soP[0][0:1, :], soP[1][0:1, :]), psB=(pA[0], pA[1]),
                   sq=(sq, sq2), rows=rows, ones_col_f=ocfb, ones_row_f=orfb)

        for dst, src in ((qbb, qbc), (kbb, kbc), (obb, obc), (mvbb, memvbc), (mscb, mscr), (imixb, imix),
                         (declb, declog), (deltab, delta), (negmb, negm), (trowb, trow),
                         (dcolb, dcol), (signsb, signs), (bvalb, bval), (pbb, pbias),
                         (zbb, zbias), (ocfb, ocf), (orfb, orf)):
            p.dma("sync", dst, src, ("ld0", 16))
        for dst, src in ((vbb, vbr), (mgbb, memgbr), (idb, ident),
                         (oh32b, oneh32), (trib, trim), (orb, orf)):
            p.dma("gpsimd", dst, src, ("ga", 16))
        p.gop("memset", zcol, 0.0)
        p.gop("memset", o64, 1.0)
        p.op("vector", "memset", visb.rearrange("q (h m) -> q h m", m=65)[:, :, 64:65], 1.0)
        p.op("vector", "memset", visb2.rearrange("q (h m) -> q h m", m=65)[:, :, 64:65], 1.0)
        p.op("vector", "memset", carry, 0.0)
        for cc in range(CC):
            p.dma("gpsimd", cbuf, xin[cc * 128:(cc + 1) * 128, :], ("ga", 16))
            p.wait("gpsimd", ("ga", p.n("ga")))
            p.dma("gpsimd", xb[cc, :, 1:T + 1], cbuf, ("ga", 16))
            p.dma("gpsimd", xb[cc, :, 0:1], zcol, ("ga", 16), allow_slow_non_contiguous=True)
            p.wait("gpsimd", ("ga", p.n("ga")))
        gaA = p.n("ga")
        for cc in range(CC):
            p.dma("gpsimd", wkv[:, cc, :], wqkv[cc * 128:(cc + 1) * 128, D:3 * D], ("gw", 16))
            p.dma("gpsimd", wquadb[:, cc, :], wquad[cc * 128:(cc + 1) * 128, :], ("gw", 16))
            p.dma("gpsimd", memvb[:, cc, :], memv[cc * 128:(cc + 1) * 128, :], ("gw", 16))
            p.dma("gpsimd", memgb[:, cc, :], memg[cc * 128:(cc + 1) * 128, :], ("gw", 16))
            p.dma("gpsimd", woutb[:, cc, :], wout[cc * 128:(cc + 1) * 128, :], ("gw", 16))
        gwT = p.n("gw")

        p.wait("scalar", ("ld0", p.n("ld0")))
        p.op("scalar", "activation", lndr, declb, AF.Sigmoid)
        p.op("scalar", "activation", d128, imixb.broadcast_to([1, H]), AF.Sigmoid)
        p.wait("vector", ("sS", p.n("sS")))
        p.op("vector", "tensor_scalar", lndr, lndr, 1e-30, None, ALU.max)
        p.wait("scalar", ("vS", p.n("vS")))
        p.op("scalar", "activation", lndr, lndr, AF.Ln)
        p.wait("tensor", ("sS", p.n("sS")), ("ga", gaA))
        p.op("tensor", "matmul", pS[0][:, 0:H], orfb, lndr, start=True, stop=True)
        p.op("tensor", "matmul", pS[1][:, 0:1], orfb, d128[:, 0:1], start=True, stop=True)
        p.op("tensor", "matmul", pS[1][:, 1:1 + H], orfb, mscb, start=True, stop=True, inc=("mm", 1))
        p.wait("scalar", ("mm", p.n("mm")))
        p.op("scalar", "copy", lndc, pS[0][:, 0:H])
        p.op("scalar", "copy", alphc, pS[1][:, 0:1])
        p.op("scalar", "copy", mscbc, pS[1][:, 1:1 + H])
        p.wait("vector", ("sS", p.n("sS")))
        p.op("vector", "tensor_scalar", d128, lndr, 128.0, None, ALU.mult)
        p.wait("scalar", ("vS", p.n("vS")))
        p.op("scalar", "activation", d128, d128, AF.Exp)
        for h in range(H):
            p.op("vector", "tensor_scalar", Ah[:, h * 128:(h + 1) * 128], deltab,
                 lndc[:, h:h + 1], None, ALU.mult)
            p.op("vector", "tensor_tensor", Ah[:, h * 128:(h + 1) * 128],
                 Ah[:, h * 128:(h + 1) * 128], negmb, ALU.add)
            p.wait("scalar", ("vS", p.n("vS")))
            p.op("scalar", "activation", Ah[:, h * 128:(h + 1) * 128],
                 Ah[:, h * 128:(h + 1) * 128], AF.Exp)
            p.op("scalar", "activation", dvh[:, h * 128:(h + 1) * 128], trowb, AF.Exp,
                 scale=lndr[:, h:h + 1])
            p.op("scalar", "activation", drh[:, h:h + 1], dcolb, AF.Exp,
                 scale=lndc[:, h:h + 1])
        dec_done_s, dec_done_v = p.n("sS"), p.n("vS")

        p.wait("sync", ("ga", gaA))
        I6 = [0, 0, 0, 1, 1, 2]
        J6 = [1, 2, 3, 2, 3, 3]
        mm_mend = {}
        for ct in range(NT):
            bufm = xmw if ct % 2 == 0 else xmw2
            snm = "sa" if ct % 2 == 0 else "sb"
            if ct >= 2:
                p.wait("sync", ("mm", mm_mend[ct - 2]))
            for cc in range(CC):
                p.dma("sync", bufm[:, cc, :], xb[cc, :, ct * 128:ct * 128 + 129], (snm, 16))
            p.wait("tensor", (snm, p.n(snm)), ("gw", gwT))
            pa = pA[ct % 2]
            if ct >= 2:
                p.wait("tensor", ("sS", q4_done[ct - 2] + 1))
            for cc in range(CC):
                p.op("tensor", "matmul", pa[:, 0:64], bufm[:, cc, 0:128],
                     wquadb[:, cc, 0:64], start=(cc == 0), stop=(cc == CC - 1))
            for cc in range(CC):
                p.op("tensor", "matmul", pa[:, 64:256], bufm[:, cc, 1:129],
                     wquadb[:, cc, 64:256], start=(cc == 0), stop=(cc == CC - 1),
                     inc=("mm", 1) if cc == CC - 1 else None)
            if ct == 16:
                p.wait("vector", ("mm", p.n("mm")))
                p.op("vector", "tensor_scalar", pa[0:1, 0:64], pa[0:1, 0:64],
                     bvalb[0:1, :], None, ALU.mult)
                p.wait("scalar", ("vS", p.n("vS")))
            p.wait("scalar", ("mm", p.n("mm")))
            p.op("scalar", "copy", q4, pa[:, 0:256])
            q4_done = q4_done if ct > 0 else {}
            q4_done[ct] = p.n("sS")
            p.wait("tensor", ("sS", p.n("sS")))
            for cc in range(CC):
                p.op("tensor", "matmul", pa[:, 256:256 + H], bufm[:, cc, 1:129],
                     memgb[:, cc, :], start=(cc == 0), stop=False)
            p.op("tensor", "matmul", pa[:, 256:256 + H], orb, mgbb,
                 start=False, stop=True, inc=("mm", 1))
            p.wait("scalar", ("mm", p.n("mm")), ("vS", p.n("vS")))
            p.op("scalar", "activation", t16c, pa[:, 256:256 + H], AF.Sigmoid)
            p.wait("vector", ("sS", p.n("sS")))
            for (dst, o1, o2) in ((wl, 0, 64), (rd, 128, 192)):
                q4v = q4.rearrange("q (g h i) -> q g h i", g=4, i=4)
                g1, g2 = o1 // 64, o2 // 64
                dstv = dst.rearrange("q (h k) -> q h k", k=6)
                for kk in range(6):
                    i, j = PAIRS[kk]
                    p.op("vector", "tensor_tensor", t16a, q4v[:, g1, :, i], q4v[:, g2, :, j], ALU.mult)
                    p.op("vector", "tensor_tensor", t16b, q4v[:, g1, :, j], q4v[:, g2, :, i], ALU.mult)
                    p.op("vector", "tensor_tensor", dstv[:, :, kk], t16a, t16b, ALU.subtract)
                p.op("vector", "tensor_tensor", t96a, dst, dst, ALU.mult)
                t96v = t96a.rearrange("q (h k) -> q h k", k=6)
                p.op("vector", "tensor_tensor", t16a, t96v[:, :, 0], t96v[:, :, 1], ALU.add)
                for kk in range(2, 6):
                    p.op("vector", "tensor_tensor", t16a, t16a, t96v[:, :, kk], ALU.add)
                p.wait("scalar", ("vS", p.n("vS")))
                p.op("scalar", "activation", t16b, t16a, AF.Sqrt)
                p.wait("vector", ("sS", p.n("sS")))
                p.op("vector", "tensor_scalar", t16b, t16b, 1e-12, None, ALU.max)
                p.op("vector", "reciprocal", t16a, t16b)
                for kk in range(6):
                    p.op("vector", "tensor_tensor", dstv[:, :, kk], dstv[:, :, kk], t16a, ALU.mult)
            wlv = wl.rearrange("q (h k) -> q h k", k=6)
            p.op("vector", "tensor_tensor", jw.rearrange("q (h k) -> q h k", k=6),
                 wlv[:, :, ::-1], signsb.rearrange("q (h k) -> q h k", k=6), ALU.mult)
            jv = jw.rearrange("q (h i) -> q h i", i=6)
            p.op("vector", "tensor_tensor", outer.rearrange("q (h i j) -> q h i j", i=6, j=6),
                 jv.unsqueeze(3).broadcast_to([128, 16, 6, 6]),
                 jv.unsqueeze(2).broadcast_to([128, 16, 6, 6]), ALU.mult)
            p.wait("tensor", ("vS", p.n("vS")), ("sS", dec_done_s))
            if ct > 0:
                p.wait("tensor", ("vS", carry_v[ct - 1]))
            for h in range(H):
                pM = pS[h // 8]
                p.op("tensor", "matmul", pM[:, (h % 8) * 36:(h % 8) * 36 + 36],
                     Ah[:, h * 128:(h + 1) * 128], outer[:, h * 36:(h + 1) * 36],
                     start=True, stop=False)
                p.op("tensor", "matmul", pM[:, (h % 8) * 36:(h % 8) * 36 + 36],
                     dvh[0:1, h * 128:(h + 1) * 128], carry[:, h * 36:(h + 1) * 36],
                     start=False, stop=True)
            for h in range(H):
                cP = soP[2 + h // 8]
                p.op("tensor", "matmul", cP[0:1, (h % 8) * 36:(h % 8) * 36 + 36],
                     drh[:, h:h + 1], outer[:, h * 36:(h + 1) * 36],
                     start=True, stop=True, inc=("mm", 1) if h == H - 1 else None)
            p.wait("vector", ("mm", p.n("mm")))
            for h in range(H):
                cP = soP[2 + h // 8]
                p.op("vector", "scalar_tensor_tensor", carry[:, h * 36:(h + 1) * 36],
                     carry[:, h * 36:(h + 1) * 36], d128[:, h:h + 1],
                     cP[0:1, (h % 8) * 36:(h % 8) * 36 + 36], ALU.mult, ALU.add)
            carry_v = carry_v if ct > 0 else {}
            carry_v[ct] = p.n("vS")
            p.wait("scalar", ("vS", p.n("vS")))
            p.op("scalar", "copy", msb[:, 0:288], pS[0][:, 0:288])
            p.op("scalar", "copy", msb[:, 288:576], pS[1][:, 0:288])
            p.wait("vector", ("sS", p.n("sS")))
            rv = rd.rearrange("q (h i) -> q h i", i=6)
            mv_ = msb.rearrange("q (g h i j) -> q g h i j", g=2, h=8, i=6)
            rv2 = rd.rearrange("q (g h i) -> q g h i", g=2, h=8)
            rmv = t96a.rearrange("q (g h j) -> q g h j", g=2, h=8)
            for i in range(6):
                p.op("vector", "tensor_tensor", t96b.rearrange("q (g h j) -> q g h j", g=2, h=8),
                     rv2[:, :, :, i].unsqueeze(3).broadcast_to([128, 2, 8, 6]),
                     mv_[:, :, :, i, :], ALU.mult)
                if i == 0:
                    p.op("vector", "tensor_copy", t96a, t96b)
                else:
                    p.op("vector", "tensor_tensor", t96a, t96a, t96b, ALU.add)
            p.op("vector", "tensor_tensor", t96b, t96a, rd, ALU.mult)
            tv = t96b.rearrange("q (h k) -> q h k", k=6)
            p.op("vector", "tensor_tensor", t16a, tv[:, :, 0], tv[:, :, 1], ALU.add)
            for kk in range(2, 6):
                p.op("vector", "tensor_tensor", t16a, t16a, tv[:, :, kk], ALU.add)
            p.op("vector", "tensor_tensor", t96b, t96a, t96a, ALU.mult)
            p.op("vector", "tensor_tensor", t16b, tv[:, :, 0], tv[:, :, 1], ALU.add)
            for kk in range(2, 6):
                p.op("vector", "tensor_tensor", t16b, t16b, tv[:, :, kk], ALU.add)
            p.op("vector", "tensor_tensor", t16b, t16b, t16a, ALU.subtract)
            p.op("vector", "scalar_tensor_tensor", t16a, t16b, alphc, t16a, ALU.mult, ALU.add)
            p.op("vector", "tensor_tensor", t16a, t16a, mscbc, ALU.mult)
            p.wait("scalar", ("vS", p.n("vS")))
            p.op("scalar", "activation", t16b, t16a, AF.Sigmoid)
            p.wait("vector", ("sS", p.n("sS")))
            p.op("vector", "tensor_tensor", t16a, t16b, t16c, ALU.mult)
            p.op("vector", "tensor_reduce", gated[:, ct:ct + 1], t16a, AX.X, ALU.add)
            p.op("vector", "tensor_scalar", gated[:, ct:ct + 1], gated[:, ct:ct + 1],
                 1.0 / H, None, ALU.mult)
            mm_mend[ct] = p.n("mm")
        p.wait("sync", ("vS", dec_done_v))
        p.dma("sync", deltab, ident, ("ld0", 16))
        p.wait("tensor", ("vS", p.n("vS")), ("sS", p.n("sS")), ("ld0", p.n("ld0")))
        p.op("tensor", "transpose", pS[0][0:32, 0:128], gated, deltab, inc=("mm", 1))
        p.wait("scalar", ("mm", p.n("mm")))
        p.op("scalar", "copy", gT[:, 0:128], pS[0][0:32, 0:128])
        gT_done = p.n("sS")

        vs_s1 = {}
        for tcx in range(CC):
            bufa = xwk if tcx % 2 == 0 else xwk2
            snm = "sa" if tcx % 2 == 0 else "sb"
            if tcx >= 2:
                p.wait("sync", ("mm", vs_s1[tcx - 2]))
            else:
                p.wait("sync", ("mm", p.n("mm")))
            for cc in range(CC):
                p.dma("sync", bufa[:, cc, 0:512], xb[cc, :, tcx * 512 + 1:(tcx + 1) * 512 + 1], (snm, 16))
            p.wait("tensor", (snm, p.n(snm)))
            if tcx == 0:
                p.wait("tensor", ("vS", p.n("vS")), ("sS", p.n("sS")))
            for dt in range(CC):
                pa = pA[dt % 2]
                p.wait("tensor", ("sS", p.n("sS")))
                for cc in range(CC):
                    p.op("tensor", "matmul", pa, wkv[:, cc, dt * 128:(dt + 1) * 128],
                         bufa[:, cc, 0:512], start=(cc == 0), stop=(cc == CC - 1),
                         inc=("mm", 1) if cc == CC - 1 else None)
                p.wait("scalar", ("mm", p.n("mm")), ("stK", p.n("stK")))
                p.op("scalar", "activation", kraw[0:64, 0, :], pa[0:64, :], AF.Identity,
                     bias=kbb[0:64, dt:dt + 1])
                p.op("scalar", "activation", kraw[0:64, 1, :], pa[64:128, :], AF.Identity,
                     bias=kbb[64:128, dt:dt + 1])
                p.wait("gpsimd", ("sS", p.n("sS")))
                for kj in range(4):
                    p.dma("gpsimd",
                          k_scr.rearrange("a q (h n) -> a q h n", n=128)[tcx * 4 + kj, 0:64, 2 * dt:2 * dt + 2, :],
                          kraw[:, :, kj * 128:(kj + 1) * 128], ("stK", 16))
            for tt4 in range(4):
                vb_ = visb if tt4 % 2 == 0 else visb2
                vsem = "stV" if tt4 % 2 == 0 else "stV2"
                p.wait("scalar", (vsem, p.n(vsem)))
                for dh2 in range(2):
                    pa = pA[dh2]
                    p.wait("tensor", ("sS", p.n("sS")))
                    for cc in range(CC):
                        p.op("tensor", "matmul", pa, bufa[:, cc, tt4 * 128:(tt4 + 1) * 128],
                             wkv[:, cc, D + dh2 * 512:D + (dh2 + 1) * 512],
                             start=(cc == 0), stop=False)
                    p.op("tensor", "matmul", pa, orb, vbb[:, dh2 * 512:(dh2 + 1) * 512],
                         start=False, stop=True, inc=("mm", 1))
                    p.wait("scalar", ("mm", p.n("mm")))
                    vtgt = vb_.rearrange("q (h m) -> q h m", m=65)
                    p.op("scalar", "copy",
                         vtgt[:, dh2 * 8:(dh2 + 1) * 8, 0:64],
                         pa.rearrange("q (h m) -> q h m", m=64))
                p.wait("gpsimd", ("sS", p.n("sS")))
                p.dma("gpsimd", v_scr[tcx * 4 + tt4], vb_, (vsem, 16))
            vs_s1[tcx] = p.n("mm")
        stK_T, stV_T, stV2_T = p.n("stK"), p.n("stV"), p.n("stV2")
        p.waitn("gpsimd", "mm")
        for cc in range(CC):
            p.dma("gpsimd", wkv[:, cc, 0:D], wqkv[cc * 128:(cc + 1) * 128, 0:D], ("gw2", 16))
        gw2T = p.n("gw2")

        p.wait("sync", ("stK", stK_T), ("stV", stV_T), ("stV2", stV2_T))
        p.wait("tensor", ("gw2", gw2T))
        for g in range(8):
            Kg = 4 * g + 4
            snm = "sa" if g % 2 == 0 else "sb"
            bufa = xwk if g % 2 == 0 else xwk2
            if g == 0:
                p.wait("sync", ("mm", p.n("mm")))
            else:
                p.wait("sync", ("mm", mm_gend))
            for cc in range(CC):
                p.dma("sync", bufa[:, cc, 0:512], xb[cc, :, g * 512 + 1:(g + 1) * 512 + 1], (snm, 16))
            p.wait("tensor", (snm, p.n(snm)), ("vS", p.n("vS")), ("sS", p.n("sS")))
            for dt in range(CC):
                pa = pA[dt % 2]
                if dt >= 2:
                    p.wait("tensor", ("sS", p.n("sS")))
                for cc in range(CC):
                    p.op("tensor", "matmul", pa, wkv[:, cc, dt * 128:(dt + 1) * 128],
                         bufa[:, cc, 0:512], start=(cc == 0), stop=(cc == CC - 1),
                         inc=("mm", 1) if cc == CC - 1 else None)
                p.wait("scalar", ("mm", p.n("mm")))
                p.op("scalar", "activation", qbuf[0:64, 2 * dt, :], pa[0:64, :], AF.Identity,
                     bias=qbb[0:64, dt:dt + 1])
                p.op("scalar", "activation", qbuf[0:64, 2 * dt + 1, :], pa[64:128, :], AF.Identity,
                     bias=qbb[64:128, dt:dt + 1])
            qdone = p.n("sS")
            p.wait("tensor", ("sS", qdone))
            scfree = [None, None, None, None]
            prfree = {(i, j): None for i in range(4) for j in range(2)}
            pready = {}
            av_mm = {}
            for hb in range(4):
                for kc in range(Kg):
                    knm = ("ka", "kb", "kc2", "kd")[kc % 4]
                    vnm = ("va", "vb", "vc", "vd")[kc % 4]
                    kb_ = kchs[kc % 4]
                    vb_ = vchs[kc % 4]
                    if kc >= 4:
                        p.wait("sync", ("mm", av_mm[kc - 4]))
                    elif hb > 0 or g > 0:
                        p.wait("sync", ("mm", mm_kprev))
                    p.dma("sync", kb_.rearrange("q a n -> q (a n)"),
                          k_scr[kc, 0:64, hb * 512:(hb + 1) * 512], (knm, 16))
                    p.dma("sync", vb_, v_scr[kc, :, hb * 260:(hb + 1) * 260], (vnm, 16))
                    p.wait("tensor", (knm, p.n(knm)), (vnm, p.n(vnm)))
                    dk0 = Kg - 4
                    scb = [pS[0], pS[1], pA[0], pA[1]]
                    for hin in range(4):
                        h = hb * 4 + hin
                        if scfree[hin] is not None:
                            p.wait("tensor", scfree[hin])
                        p.op("tensor", "matmul", scb[hin], kb_[0:64, hin, :], qbuf[0:64, h, :],
                             start=True, stop=True, inc=("mm", 1))
                        mmsc = p.n("mm")
                        if prfree[(hin, kc % 2)] is not None:
                            p.wait("scalar", prfree[(hin, kc % 2)])
                        p.wait("scalar", ("mm", mmsc))
                        prbh = prbufs[hin * 2 + kc % 2]
                        p.op("scalar", "activation", prbh, scb[hin], AF.Exp, scale=0.125,
                             bias=pbb if (kc < 16 and g >= 4) else zbb)
                        scfree[hin] = ("sS", p.n("sS"))
                        pready[(hin, kc % 2)] = ("sS", p.n("sS"))
                        if kc >= dk0:
                            jd = kc - dk0
                            p.wait("vector", ("sS", p.n("sS")))
                            if jd > 0:
                                p.op("vector", "memset", prbh[:, 0:jd * 128], 0.0)
                            p.op("vector", "tensor_tensor", prbh[:, jd * 128:(jd + 1) * 128],
                                 prbh[:, jd * 128:(jd + 1) * 128], trib, ALU.mult)
                            pready[(hin, kc % 2)] = ("vS", p.n("vS"))
                    if kc > 0:
                        vprev = vchs[(kc - 1) % 4]
                        for hin in range(4):
                            p.wait("tensor", pready[(hin, (kc - 1) % 2)])
                            p.op("tensor", "matmul", soP[hin],
                                 vprev[:, hin * 65:(hin + 1) * 65],
                                 prbufs[hin * 2 + (kc - 1) % 2],
                                 start=(kc - 1 == 0), stop=False, inc=("mm", 1))
                            prfree[(hin, (kc - 1) % 2)] = ("mm", p.n("mm"))
                        av_mm[kc - 1] = p.n("mm")
                    if kc == Kg - 1:
                        vlast = vchs[kc % 4]
                        for hin in range(4):
                            p.wait("tensor", pready[(hin, kc % 2)])
                            p.op("tensor", "matmul", soP[hin],
                                 vlast[:, hin * 65:(hin + 1) * 65],
                                 prbufs[hin * 2 + kc % 2],
                                 start=(kc == 0), stop=True, inc=("mm", 1))
                            prfree[(hin, kc % 2)] = ("mm", p.n("mm"))
                        av_mm[kc] = p.n("mm")
                mm_kprev = p.n("mm")
                p.wait("vector", ("mm", p.n("mm")))
                for hin in range(4):
                    h = hb * 4 + hin
                    p.op("vector", "reciprocal", rcrow, soP[hin][64:65, :])
                    p.wait("tensor", ("vS", p.n("vS")))
                    if scfree[0] is not None:
                        p.wait("tensor", scfree[0])
                    p.op("tensor", "matmul", pS[0][0:64, :], o64, rcrow, start=True, stop=True,
                         inc=("mm", 1))
                    p.wait("scalar", ("mm", p.n("mm")))
                    p.op("scalar", "copy", rcs, pS[0][0:64, :])
                    scfree[0] = ("sS", p.n("sS"))
                    p.wait("vector", ("sS", p.n("sS")))
                    p.op("vector", "tensor_tensor",
                         comb[(h % 2) * 64:(h % 2) * 64 + 64, h // 2, :],
                         soP[hin][0:64, :], rcs, ALU.mult)
                p.wait("tensor", ("vS", p.n("vS")))
            p.wait("tensor", ("sS", gT_done))
            for j in range(4):
                p.op("tensor", "matmul", pS[1][:, j * 128:(j + 1) * 128],
                     oh32b[:, (4 * g + j) * 128:(4 * g + j + 1) * 128], gT,
                     start=(j == 0), stop=(j == 3), inc=("mm", 1) if j == 3 else None)
            p.wait("scalar", ("mm", p.n("mm")))
            p.op("scalar", "copy", gb, pS[1])
            gbd = p.n("sS")
            for dt in range(CC):
                pa = pA[dt % 2]
                if dt >= 2:
                    p.wait("tensor", ("vS", p.n("vS")))
                for cc in range(CC):
                    p.op("tensor", "matmul", pa, memvb[:, cc, dt * 128:(dt + 1) * 128],
                         bufa[:, cc, 0:512], start=(cc == 0), stop=(cc == CC - 1),
                         inc=("mm", 1) if cc == CC - 1 else None)
                p.wait("scalar", ("mm", p.n("mm")), ("sS", gbd), ("vS", p.n("vS")))
                p.op("scalar", "activation", mvs, pa, AF.Identity, bias=mvbb[:, dt:dt + 1])
                p.wait("vector", ("sS", p.n("sS")))
                p.op("vector", "tensor_tensor", prbufs[0], mvs, gb, ALU.mult)
                p.op("vector", "tensor_tensor", comb[:, dt, :], comb[:, dt, :], prbufs[0], ALU.add)
            p.wait("tensor", ("vS", p.n("vS")))
            vs_pre = p.n("vS")
            for dt2 in range(CC):
                pa = pA[dt2 % 2]
                if dt2 >= 2:
                    p.wait("tensor", ("vS", vs_pre + dt2 - 1))
                for cc in range(CC):
                    p.op("tensor", "matmul", pa, woutb[:, cc, dt2 * 128:(dt2 + 1) * 128],
                         comb[:, cc, :], start=(cc == 0), stop=(cc == CC - 1),
                         inc=("mm", 1) if cc == CC - 1 else None)
                if dt2 >= 2:
                    p.wait("sync", ("vS", vs_pre + dt2 - 1))
                elif g > 0:
                    p.wait("sync", ("vS", vs_pre))
                p.dma("sync", xres if dt2 % 2 == 0 else xres2,
                      xin[dt2 * 128:(dt2 + 1) * 128, g * 512:(g + 1) * 512],
                      ("sxa" if dt2 % 2 == 0 else "sxb", 16))
                p.wait("vector", ("mm", p.n("mm")),
                       ("sxa" if dt2 % 2 == 0 else "sxb", p.n("sxa" if dt2 % 2 == 0 else "sxb")))
                p.op("vector", "scalar_tensor_tensor", s_f[:, dt2, :], pa,
                     obb[:, dt2:dt2 + 1], xres if dt2 % 2 == 0 else xres2, ALU.add, ALU.add)
            p.wait("tensor", ("vS", p.n("vS")))
            p.wait("scalar", ("vS", p.n("vS")))
            emit_ln(p, s_f, s_f, lnb)
            if g >= 4 or fused:
                oc0 = g * 512 if fused else (g - 4) * 512
                p.wait("gpsimd", ("vS", p.n("vS")))
                for dt in range(CC):
                    p.dma("gpsimd", x1o[dt * 128:(dt + 1) * 128, oc0:oc0 + 512],
                          s_f[:, dt, :], ("so", 16))
                p.wait("vector", ("so", p.n("so")))
            mm_gend = p.n("mm")
        p.wait("gpsimd", ("so", p.n("so")))


def build_phase1():
    nc = bass.Bass("TRN2", target_bir_lowering=False)
    ctx = contextlib.ExitStack()
    xin = din(nc, "xin", [D, T])
    wqkv = din(nc, "wqkv", [D, 3 * D])
    qbc = din(nc, "qbc", [128, CC])
    kbc = din(nc, "kbc", [128, CC])
    vbr = din(nc, "vbr", [1, D])
    wquad = din(nc, "wquad", [D, 256])
    memv = din(nc, "memv", [D, D])
    memvbc = din(nc, "memvbc", [128, CC])
    memg = din(nc, "memg", [D, H])
    memgbr = din(nc, "memgbr", [1, H])
    mscr = din(nc, "mscr", [1, H])
    imix = din(nc, "imix", [1, 1])
    declog = din(nc, "declog", [1, H])
    wout = din(nc, "wout", [D, D])
    obc = din(nc, "obc", [128, CC])
    delta = din(nc, "delta", [128, 128])
    negm = din(nc, "negm", [128, 128])
    trow = din(nc, "trow", [1, 128])
    dcol = din(nc, "dcol", [128, 1])
    signs = din(nc, "signs", [128, 96])
    ident = din(nc, "ident", [128, 128])
    oneh32 = din(nc, "oneh32", [32, 32 * 128])
    bval = din(nc, "bval", [128, 1])
    pbias = din(nc, "pbias", [128, 1])
    zbias = din(nc, "zbias", [128, 1])
    trim = din(nc, "trim", [128, 128])
    ocf = din(nc, "ocf", [128, 1])
    orf = din(nc, "orf", [1, 128])
    x1o = dout(nc, "x1o", [D, HALF])
    xb = dscr(nc, "xb", [CC, 128, T + 1])
    k_scr = dscr(nc, "k_scr", [NT, 64, H * 128])
    v_scr = dscr(nc, "v_scr", [NT, 128, H * 65])

    with ctx:
        p = Plan(nc, ctx)
        p.sem("ld0", "ga", "gw", "gw2", "sa", "sb", "sr", "sxa", "sxb", "stB", "stC", "stK", "stV", "stV2", "ka", "kb", "kc2", "kd", "va", "vb", "vc", "vd", "so", "mm", "vS", "sS", "gS", "g1a", "g1b", "g2a", "g2b", "sxw")
        emit_phase1(p, nc, xin, wqkv, qbc, kbc, vbr, wquad, memv, memvbc, memg, memgbr, mscr, imix, declog, wout, obc, delta, negm, trow, dcol, signs, ident, oneh32, bval, pbias, zbias, trim, ocf, orf, x1o, xb, k_scr, v_scr)
        p.build()
    return nc


def build_fused():
    nc = bass.Bass("TRN2", target_bir_lowering=False)
    ctx = contextlib.ExitStack()
    ident = din(nc, "ident", [128, 128])
    trim = din(nc, "trim", [128, 128])
    ocf = din(nc, "ocf", [128, 1])
    orf = din(nc, "orf", [1, 128])
    bval = din(nc, "bval", [128, 1])
    pbias = din(nc, "pbias", [128, 1])
    zbias = din(nc, "zbias", [128, 1])
    xin = din(nc, "xin", [D, T])
    wqkv = din(nc, "wqkv", [D, 3 * D])
    qbc = din(nc, "qbc", [128, CC])
    kbc = din(nc, "kbc", [128, CC])
    vbr = din(nc, "vbr", [1, D])
    wquad = din(nc, "wquad", [D, 256])
    memv = din(nc, "memv", [D, D])
    memvbc = din(nc, "memvbc", [128, CC])
    memg = din(nc, "memg", [D, H])
    memgbr = din(nc, "memgbr", [1, H])
    mscr = din(nc, "mscr", [1, H])
    imix = din(nc, "imix", [1, 1])
    declog = din(nc, "declog", [1, H])
    wout = din(nc, "wout", [D, D])
    obc = din(nc, "obc", [128, CC])
    delta = din(nc, "delta", [128, 128])
    negm = din(nc, "negm", [128, 128])
    trow = din(nc, "trow", [1, 128])
    dcol = din(nc, "dcol", [128, 1])
    signs = din(nc, "signs", [128, 96])
    oneh32 = din(nc, "oneh32", [32, 32 * 128])
    wr = din(nc, "wr", [D, D])
    wk = din(nc, "wk", [D, D])
    wv = din(nc, "wv", [D, D])
    mur = din(nc, "mur", [128, CC])
    muk = din(nc, "muk", [128, CC])
    wqm = din(nc, "wqm", [D, D])
    wkm = din(nc, "wkm", [D, D])
    wvm = din(nc, "wvm", [D, D])
    wom = din(nc, "wom", [D, D])
    bqc = din(nc, "bqc", [128, CC])
    bkc = din(nc, "bkc", [128, CC])
    bvr = din(nc, "bvr", [1, D])
    boc = din(nc, "boc", [128, CC])
    sinq = din(nc, "sinq", [32, HALF])
    cosq = din(nc, "cosq", [32, HALF])
    sink = din(nc, "sink", [32, T])
    cosk = din(nc, "cosk", [32, T])
    wrt = din(nc, "wrt", [D, NE])
    w1t = din(nc, "w1t", [NE, 32, 128, D])
    b1c = din(nc, "b1c", [128, NE * 32])
    w2t = din(nc, "w2t", [NE, CC, 128, 32, 128])
    b2m = din(nc, "b2m", [NE, D])
    ws1t = din(nc, "ws1t", [32, 128, D])
    bs1c = din(nc, "bs1c", [128, 32])
    ws2t = din(nc, "ws2t", [CC, 128, 32, 128])
    bs2c = din(nc, "bs2c", [128, CC])
    oneh = din(nc, "oneh", [NE, NE * 128])
    x1i = dscr(nc, "x1i", [D, T], F32)
    x2i = dscr(nc, "x2i", [D, T], F32)
    x3i = dscr(nc, "x3i", [D, HALF], F32)
    xb1 = dscr(nc, "xb1", [CC, 128, T + 1])
    ks1 = dscr(nc, "ks1", [NT, 64, H * 128])
    vs1 = dscr(nc, "vs1", [NT, 128, H * 65])
    x1b = dscr(nc, "x1b", [CC, 128, T + 1])
    xr_s = dscr(nc, "xr_s", [CC, 128, T])
    xk_s = dscr(nc, "xk_s", [CC, 128, T])
    r_s = dscr(nc, "r_s", [CC, 128, T])
    ka_s = dscr(nc, "ka_s", [CC, 128, T])
    x2b = dscr(nc, "x2b", [CC, 128, T])
    ks3 = dscr(nc, "ks3", [NT, 64, H * 128])
    vs3 = dscr(nc, "vs3", [NT, 128, H * 65])
    x3b = dscr(nc, "x3b", [CC, 128, HALF])
    x4 = dout(nc, "x4", [D, HALF])
    with ctx:
        p = Plan(nc, ctx)
        p.sem(*BARRIER_SEMS)
        emit_phase1(p, nc, xin, wqkv, qbc, kbc, vbr, wquad, memv, memvbc, memg, memgbr,
                    mscr, imix, declog, wout, obc, delta, negm, trow, dcol, signs, ident,
                    oneh32, bval, pbias, zbias, trim, ocf, orf, x1i, xb1, ks1, vs1, fused=True)
        barrier(p)
        emit_phase2(p, nc, x1i, wr, wk, wv, mur, muk, bval, ocf, orf, x2i,
                    x1b, xr_s, xk_s, r_s, ka_s, fused=True)
        barrier(p)
        emit_phase3(p, nc, x2i, wqm, wkm, wvm, wom, bqc, bkc, bvr, boc, sinq, cosq,
                    sink, cosk, pbias, zbias, trim, ocf, orf, x3i, x2b, ks3, vs3, fused=True)
        barrier(p)
        emit_phase4(p, nc, x3i, wrt, w1t, b1c, w2t, b2m, ws1t, bs1c, ws2t, bs2c,
                    ident, oneh, ocf, orf, x4, x3b, fused=True)
        p.build()
    return nc


_CACHE = {}
_USE_FUSED = True


def _cols(b):
    return np.ascontiguousarray(b.reshape(-1, 128).T.astype(np.float32))


def kernel(L0_attn_x, L0_attn_qkv_w, L0_attn_qkv_b, L0_attn_w1w, L0_attn_w2w,
           L0_attn_r1w, L0_attn_r2w, L0_attn_memv_w, L0_attn_memv_b, L0_attn_memg_w,
           L0_attn_memg_b, L0_attn_mem_scale, L0_attn_iter_mix, L0_attn_out_w,
           L0_attn_out_b, L0_attn_decay_logits, L0_ffn_W_r, L0_ffn_W_k, L0_ffn_W_v,
           L0_ffn_mu_r, L0_ffn_mu_k, L1_attn_Wq, L1_attn_bq, L1_attn_Wk, L1_attn_bk,
           L1_attn_Wv, L1_attn_bv, L1_attn_Wo, L1_attn_bo, L1_ffn_w_router, L1_ffn_W1,
           L1_ffn_b1, L1_ffn_W2, L1_ffn_b2, L1_ffn_Ws1, L1_ffn_bs1, L1_ffn_Ws2,
           L1_ffn_bs2):
    f32 = np.float32
    x = np.asarray(L0_attn_x, f32)
    ones_c = np.ones((128, 1), f32)
    ones_r = np.ones((1, 128), f32)
    delta = (np.arange(128)[None, :] - np.arange(128)[:, None]).astype(f32)
    negm = np.where(delta >= 1, 0.0, NEG).astype(f32)
    signs = np.tile(np.array(JSIGN, f32), H)[None, :].repeat(128, 0)
    oneh32 = np.zeros((32, 32 * 128), f32)
    for t in range(32):
        oneh32[t, t * 128:(t + 1) * 128] = 1
    onehE = np.zeros((NE, NE * 128), f32)
    for e in range(NE):
        onehE[e, e * 128:(e + 1) * 128] = 1
    trim = np.triu(np.ones((128, 128), f32))
    ident = np.eye(128, dtype=f32)
    zb = np.zeros((128, 1), f32)
    pos = np.arange(T, dtype=np.float64)
    inv_freq = np.exp(np.arange(0, DH, 2, dtype=np.float64) * (-np.log(10000.0) / DH))
    ang = pos[:, None] * inv_freq[None, :]
    sin_t = np.sin(ang).astype(f32).T.copy()
    cos_t = np.cos(ang).astype(f32).T.copy()

    qb = np.asarray(L0_attn_qkv_b, f32)
    common1 = dict(
        wqkv=np.ascontiguousarray(np.asarray(L0_attn_qkv_w, f32).T),
        qbc=_cols(qb[0:D]), kbc=_cols(qb[D:2 * D]), vbr=qb[2 * D:].reshape(1, D).copy(),
        wquad=np.ascontiguousarray(np.concatenate(
            [np.asarray(L0_attn_w1w, f32).T, np.asarray(L0_attn_w2w, f32).T,
             np.asarray(L0_attn_r1w, f32).T, np.asarray(L0_attn_r2w, f32).T], 1)),
        memv=np.ascontiguousarray(np.asarray(L0_attn_memv_w, f32).T),
        memvbc=_cols(np.asarray(L0_attn_memv_b, f32)),
        memg=np.ascontiguousarray(np.asarray(L0_attn_memg_w, f32).T),
        memgbr=np.asarray(L0_attn_memg_b, f32).reshape(1, H).copy(),
        mscr=np.asarray(L0_attn_mem_scale, f32).reshape(1, H).copy(),
        imix=np.asarray(L0_attn_iter_mix, f32).reshape(1, 1).copy(),
        declog=np.asarray(L0_attn_decay_logits, f32).reshape(1, H).copy(),
        wout=np.ascontiguousarray(np.asarray(L0_attn_out_w, f32).T),
        obc=_cols(np.asarray(L0_attn_out_b, f32)),
        delta=delta, negm=negm,
        trow=np.arange(128, dtype=f32).reshape(1, -1),
        dcol=(128 - np.arange(128, dtype=f32)).reshape(-1, 1),
        signs=signs, ident=ident, oneh32=oneh32, trim=trim, zbias=zb,
        ocf=ones_c, orf=ones_r)
    common2 = dict(
        wr=np.ascontiguousarray(np.asarray(L0_ffn_W_r, f32).T),
        wk=np.ascontiguousarray(np.asarray(L0_ffn_W_k, f32).T),
        wv=np.ascontiguousarray(np.asarray(L0_ffn_W_v, f32).T),
        mur=_cols(np.asarray(L0_ffn_mu_r, f32)), muk=_cols(np.asarray(L0_ffn_mu_k, f32)))
    common3 = dict(
        wqm=np.ascontiguousarray(np.asarray(L1_attn_Wq, f32)),
        wkm=np.ascontiguousarray(np.asarray(L1_attn_Wk, f32)),
        wvm=np.ascontiguousarray(np.asarray(L1_attn_Wv, f32)),
        wom=np.ascontiguousarray(np.asarray(L1_attn_Wo, f32)),
        bqc=_cols(np.asarray(L1_attn_bq, f32)), bkc=_cols(np.asarray(L1_attn_bk, f32)),
        bvr=np.asarray(L1_attn_bv, f32).reshape(1, D).copy(),
        boc=_cols(np.asarray(L1_attn_bo, f32)))
    W1 = np.asarray(L1_ffn_W1, f32)
    W2 = np.asarray(L1_ffn_W2, f32)
    w1t = np.ascontiguousarray(W1.reshape(NE, 32, 128, CC, 128).transpose(0, 1, 4, 3, 2)
                               .reshape(NE, 32, 128, D))
    w2t = np.ascontiguousarray(W2.reshape(NE, CC, 128, 32, 128).transpose(0, 1, 4, 3, 2))
    Ws1 = np.asarray(L1_ffn_Ws1, f32)
    Ws2 = np.asarray(L1_ffn_Ws2, f32)
    common4 = dict(
        wrt=np.ascontiguousarray(np.asarray(L1_ffn_w_router, f32).T),
        w1t=w1t, b1c=np.ascontiguousarray(np.asarray(L1_ffn_b1, f32).reshape(NE * 32, 128).T),
        w2t=w2t, b2m=np.asarray(L1_ffn_b2, f32),
        ws1t=np.ascontiguousarray(Ws1.reshape(32, 128, CC, 128).transpose(0, 3, 2, 1)
                                  .reshape(32, 128, D)),
        bs1c=_cols(np.asarray(L1_ffn_bs1, f32)),
        ws2t=np.ascontiguousarray(Ws2.reshape(CC, 128, 32, 128).transpose(0, 3, 2, 1)),
        bs2c=_cols(np.asarray(L1_ffn_bs2, f32)),
        oneh=onehE)

    def core_extras(b, h):
        xT = np.ascontiguousarray(x[b].T)
        own = slice(h * HALF, (h + 1) * HALF)
        d = {}
        if h == 0:
            d["xin"] = np.ascontiguousarray(
                np.concatenate([np.zeros((D, HALF), f32), xT[:, 0:HALF]], 1))
            d["bval"] = zb
            d["pbias"] = np.full((128, 1), NEG, f32)
        else:
            d["xin"] = xT
            d["bval"] = ones_c
            d["pbias"] = zb
        d["sinq"] = np.ascontiguousarray(sin_t[:, own])
        d["cosq"] = np.ascontiguousarray(cos_t[:, own])
        d["sink"] = np.ascontiguousarray(np.concatenate([sin_t[:, 0:HALF], sin_t[:, own]], 1))
        d["cosk"] = np.ascontiguousarray(np.concatenate([cos_t[:, 0:HALF], cos_t[:, own]], 1))
        return d

    if _USE_FUSED:
        if "fused" not in _CACHE:
            _CACHE["fused"] = build_fused()
        ncF = _CACHE["fused"]
        ims = []
        for b in range(B):
            for h in (0, 1):
                d = {}
                d.update(common1)
                d.update(common2)
                d.update(common3)
                d.update(common4)
                d.update(core_extras(b, h))
                ims.append(d)
        r = run_bass_kernel_spmd(ncF, ims, core_ids=list(range(NC)))
        out = np.stack([np.concatenate(
            [r.results[2 * b]["x4"], r.results[2 * b + 1]["x4"]], 1).T for b in range(B)])
        return np.ascontiguousarray(out.astype(np.float32))

    if "ncs" not in _CACHE:
        _CACHE["ncs"] = (build_phase1(), build_phase2(), build_phase3(), build_phase4())
    nc1, nc2, nc3, nc4 = _CACHE["ncs"]
    im1 = []
    for b in range(B):
        for h in (0, 1):
            d = dict(common1)
            d.update({k: v for k, v in core_extras(b, h).items()
                      if k in ("xin", "bval", "pbias")})
            im1.append(d)
    r1 = run_bass_kernel_spmd(nc1, im1, core_ids=list(range(NC)))
    x1T = [np.concatenate([r1.results[2 * b]["x1o"], r1.results[2 * b + 1]["x1o"]], 1)
           for b in range(B)]
    im2 = []
    for b in range(B):
        for h in (0, 1):
            d = dict(common2)
            d["ocf"] = ones_c
            d["orf"] = ones_r
            if h == 0:
                d["x1"] = np.ascontiguousarray(
                    np.concatenate([x1T[b][:, 0:HALF], x1T[b][:, 0:HALF]], 1))
                d["bval"] = zb
            else:
                d["x1"] = np.ascontiguousarray(x1T[b])
                d["bval"] = ones_c
            im2.append(d)
    r2 = run_bass_kernel_spmd(nc2, im2, core_ids=list(range(NC)))
    x2T = [np.concatenate([r2.results[2 * b]["x2"], r2.results[2 * b + 1]["x2"]], 1)
           for b in range(B)]
    im3 = []
    for b in range(B):
        for h in (0, 1):
            d = dict(common3)
            ce = core_extras(b, h)
            d.update({k: ce[k] for k in ("pbias", "sinq", "cosq", "sink", "cosk")})
            d.update(dict(trim=trim, zbias=zb, ocf=ones_c, orf=ones_r))
            if h == 0:
                d["x2"] = np.ascontiguousarray(
                    np.concatenate([x2T[b][:, 0:HALF], x2T[b][:, 0:HALF]], 1))
            else:
                d["x2"] = np.ascontiguousarray(x2T[b])
            im3.append(d)
    r3 = run_bass_kernel_spmd(nc3, im3, core_ids=list(range(NC)))
    im4 = []
    for c in range(NC):
        d = dict(common4)
        d.update(dict(ident=ident, ocf=ones_c, orf=ones_r))
        d["x3"] = np.ascontiguousarray(r3.results[c]["x3"])
        im4.append(d)
    r4 = run_bass_kernel_spmd(nc4, im4, core_ids=list(range(NC)))
    out = np.stack([np.concatenate([r4.results[2 * b]["x4"], r4.results[2 * b + 1]["x4"]], 1).T
                    for b in range(B)])
    return np.ascontiguousarray(out.astype(np.float32))
